# version 02
# baseline (speedup 1.0000x reference)
import numpy as np
import concourse.bass as bass
import concourse.mybir as mybir
from concourse.bass_utils import run_bass_kernel_spmd

F32 = mybir.dt.float32
BF16 = mybir.dt.bfloat16
ALU = mybir.AluOpType
ACT = mybir.ActivationFunctionType
AX = mybir.AxisListType.X

B, T, DM, DH = 4, 8192, 1024, 4096
H, D = 8, 128
NCORE = 8
TOWN = T // 2
HALO = 8
FT = 256
NFT = 17
SLICE = NFT * FT
OWN0 = HALO
CH = 32
NCHUNK = TOWN // CH
KD = DM // 128
HD = DH // 128
NSUB = FT // 128
RSQRT_D = float(np.float32(D ** (-0.5)))
C32V = 1.5 * 2.0 ** 23
C64V = 1.5 * 2.0 ** 52
EPS = 1e-5


def bfree(col_ap, n):
    return bass.AP(col_ap.tensor, col_ap.offset, list(col_ap.ap[:1]) + [[0, n]])


class Prog:
    def __init__(self):
        from contextlib import ExitStack
        self.ctx = ExitStack()
        self.nc = bass.Bass("TRN2", target_bir_lowering=False, debug=False)
        self.blocks = []
        self.sem_h = {}
        self.sem_v = {}

    def sem(self, name):
        if name not in self.sem_h:
            self.sem_h[name] = self.ctx.enter_context(self.nc.semaphore(name))
            self.sem_v[name] = 0
        return self.sem_h[name]

    def inc(self, name, instr=None, n=1):
        self.sem(name)
        self.sem_v[name] += n
        if instr is not None:
            instr.then_inc(self.sem_h[name], n)
        return self.sem_v[name]

    def val(self, name):
        self.sem(name)
        return self.sem_v[name]

    def barrier(self, tag):
        P = self.phase
        snap = [(n, v) for n, v in sorted(self.sem_v.items())
                if v > 0 and n != "pb" and not n.startswith("c2_")]

        @P("vector")
        def _(v, snap=snap):
            for n, val in snap:
                v.wait_ge(self.sem_h[n], val)
            self.inc("pb", v.engine_nop())
        return self.val("pb")

    def vb(self, eng, name="vch"):
        mknop = getattr(eng, "engine_nop", eng.nop)
        n = self.inc(name, mknop())
        eng.wait_ge(self.sem(name), n)

    def phase(self, engine):
        def reg(fn):
            getattr(self.block, engine)(fn)
            return fn
        return reg

    def sb(self, name, shape, dt=F32, scope=None):
        t = (scope or self.ctx).enter_context(self.nc.sbuf_tensor(name, shape, dt))
        return t[:]

    def psum(self, name, shape, scope=None):
        t = (scope or self.ctx).enter_context(self.nc.psum_tensor(name, shape, F32))
        return t[:]


def build(run_phases=("pre", "f1", "a1a", "a1b", "scan", "p3", "f2"), debug_out=None,
          single=False, nch=128, scan_dbg=False):
    from contextlib import ExitStack
    S = Prog()
    nc = S.nc
    P = S.phase

    def din(name, shape, dt=F32):
        return nc.dram_tensor(name, shape, dt, kind="ExternalInput").ap()

    def dscr(name, shape, dt=F32):
        return nc.dram_tensor(name, shape, dt).ap()

    x0f = din("x0f", [KD, 128, SLICE])
    x0t = din("x0t", [SLICE, DM])
    vrowi = din("vrow", [1, SLICE])
    vtok = din("vtok", [128, SLICE // 128])
    cfgi = din("cfg", [128, 8])
    identi = din("ident", [CH, CH])
    trii = din("tri", [CH, CH])
    id128i = din("id128", [128, 128])
    w1i = din("w1", [DH, DM])
    w2i = din("w2", [DM, DH])
    cwi = din("cw", [128, HD, 3])
    cbi = din("cb", [128, HD])
    wpi = {n: din(n, [DM, DM]) for n in ("wq", "wk", "wv", "wa", "wg", "wo")}
    wbi = din("wb", [DM, H])
    bai = din("ba", [128, KD])
    bbi = din("bb", [H, 1])
    convi = {"q": (din("cqw", [128, KD, 4]), din("cqb", [128, KD])),
             "k": (din("ckw", [128, KD, 4]), din("ckb", [128, KD])),
             "v": (din("cvw", [128, KD, 4]), din("cvb", [128, KD]))}
    lngi = din("lng", [128, 1])
    lnbi = din("lnb", [128, 1])
    out = nc.dram_tensor("out", [KD, 128, TOWN], F32, kind="ExternalOutput").ap()

    d_w1q = dscr("d_w1q", [128, KD * DH], BF16)
    d_w2q = dscr("d_w2q", [128, HD * DM], BF16)
    d_xmid = dscr("d_xmid", [KD, 128, SLICE])
    d_xn = dscr("d_xn", [KD, 128, SLICE])
    d_q = dscr("d_q", [H, 128, SLICE])
    d_k = dscr("d_k", [H, 128, SLICE])
    d_v = dscr("d_v", [H, 128, SLICE])
    d_a = dscr("d_a", [H, 128, SLICE])
    d_g = dscr("d_g", [H, 128, SLICE])
    d_beta = dscr("d_beta", [H, SLICE])
    d_o = dscr("d_o", [H, 128, TOWN])
    d_xattn = dscr("d_xattn", [KD, 128, SLICE])
    d_xat = dscr("d_xat", [SLICE, DM])
    d_cc1in = dscr("d_cc1in", [128, H * 128])
    d_cc1out = dscr("d_cc1out", [128, 2 * H * 128])
    d_cc2in = dscr("d_cc2in", [128, 2 * KD])
    d_cc2out = dscr("d_cc2out", [128, 4 * KD])
    d_xfin = dscr("d_xfin", [KD, 128, SLICE])
    d_dbg = dscr("d_dbg", [16, 128, 1024])

    S.block = S.ctx.enter_context(nc.Block())

    k_ones = S.sb("k_ones", [1, 128])
    k_onec = S.sb("k_onec", [128, 2])
    k_id32 = S.sb("k_id32", [CH, CH])
    k_tri = S.sb("k_tri", [CH, CH])
    k_id128 = S.sb("k_id128", [128, 128])
    k_cfg = S.sb("k_cfg", [128, 8])
    k_cr = S.sb("k_cr", [128, 2])
    k_vtok = S.sb("k_vtok", [128, SLICE // 128])
    k_cw = S.sb("k_cw", [128, 3 * HD])
    k_cb = S.sb("k_cb", [128, HD])
    k_aconv = {nm: (S.sb("k_c%sw" % nm, [128, 4 * KD]), S.sb("k_c%sb" % nm, [128, KD]))
               for nm in ("q", "k", "v")}
    k_ba = S.sb("k_ba", [128, KD])
    k_bb = S.sb("k_bb", [H, 2])
    k_lng = S.sb("k_lng", [128, 2])
    k_lnb = S.sb("k_lnb", [128, 2])
    k_sw = S.sb("k_sw", [128, 4])

    ps = [S.psum(f"ps{i}", [128, 512]) for i in range(8)]

    def emit_pre(wscope):
        w1q = S.sb("w1q", [128, KD * DH], BF16, scope=wscope)
        w2q = S.sb("w2q", [128, HD * DM], BF16, scope=wscope)

        @P("sync")
        def _(e):
            def ld(dst, src):
                S.inc("c_ld", e.dma_start(out=dst, in_=src), 16)
            ld(k_id32, identi)
            ld(k_tri, trii)
            ld(k_id128, id128i)
            ld(k_cfg, cfgi)
            ld(k_vtok, vtok)
            ld(k_cw, cwi.rearrange("p k c -> p (k c)"))
            ld(k_cb, cbi)
            for nm in ("q", "k", "v"):
                ld(k_aconv[nm][0], convi[nm][0].rearrange("p k c -> p (k c)"))
                ld(k_aconv[nm][1], convi[nm][1])
            ld(k_ba, bai)
            ld(k_bb[:, 0:1], bbi)
            ld(k_lng[:, 0:1], lngi)
            ld(k_lnb[:, 0:1], lnbi)
        nld = S.val("c_ld")

        @P("vector")
        def _(v):
            v.wait_ge(S.sem("c_ld"), nld)
            v.memset(k_ones, 1.0)
            v.memset(k_onec[:, 0:1], 1.0)
            v.memset(k_cr[:, 1:2], 1.0)
            S.vb(v)
            v.tensor_scalar_add(k_cr[:, 1:2], k_cr[:, 1:2], 2.0 ** -30)
            S.vb(v)
            v.tensor_scalar_add(k_cr[:, 1:2], k_cr[:, 1:2], -1.0)
            S.vb(v)
            v.tensor_scalar(k_cr[:, 0:1], k_cr[:, 1:2], 2.0 ** 30, C64V - C32V,
                            ALU.mult, ALU.mult)
            S.vb(v)
            v.tensor_scalar_add(k_cr[:, 0:1], k_cr[:, 0:1], C32V)
            S.vb(v)
            S.inc("c_pre", v.engine_nop())
        npre = S.val("c_pre")

        with ExitStack() as pscope:
            wbuf = S.sb("wbuf", [128, DH], scope=pscope)
            wacc = S.sb("wacc", [128, 36], scope=pscope)
            for wi, (wsrc, wres, dsp, rows, cols) in enumerate(
                    [(w1i, w1q, d_w1q, DH, DM), (w2i, w2q, d_w2q, DM, DH)]):
                nblk = rows // 128
                for bk in range(nblk):
                    need = S.val("p_dv")
                    needmm = S.val("p_mm")

                    @P("sync")
                    def _(e, bk=bk, need=need, needmm=needmm, wsrc=wsrc, cols=cols):
                        if need:
                            e.wait_ge(S.sem("p_dv"), need)
                        if needmm:
                            e.wait_ge(S.sem("p_mm"), needmm)
                        S.inc("p_ld", e.dma_start(
                            out=wbuf[:, :cols],
                            in_=wsrc[bk * 128:(bk + 1) * 128, :]), 16)

                    @P("vector")
                    def _(v, bk=bk, n=S.val("p_ld"), cols=cols):
                        v.wait_ge(S.sem("p_ld"), n)
                        i = v.reduce_sum(wacc[:, bk:bk + 1], wbuf[:, :cols], AX,
                                         apply_absolute_value=True)
                        S.inc("p_dv", i)

                @P("vector")
                def _(v, nblk=nblk, n=S.val("p_dv")):
                    v.wait_ge(S.sem("p_dv"), n)
                    v.reduce_sum(wacc[:, 33:34], wacc[:, 0:nblk], AX)
                    S.vb(v)
                    S.inc("p_dv", v.engine_nop())

                @P("tensor")
                def _(t, n=S.val("p_dv")):
                    t.wait_ge(S.sem("p_dv"), n)
                    t.wait_ge(S.sem("c_pre"), npre)
                    i = t.matmul(ps[0][:1, 0:1], k_onec[:, 0:1], wacc[:, 33:34],
                                 start=True, stop=True)
                    S.inc("p_mm", i)

                @P("vector")
                def _(v, n=S.val("p_mm"), denom=1.0 / (rows * cols)):
                    v.wait_ge(S.sem("p_mm"), n)
                    v.tensor_scalar_mul(wacc[:1, 34:35], ps[0][:1, 0:1], denom)
                    S.vb(v)
                    v.tensor_scalar_max(wacc[:1, 34:35], wacc[:1, 34:35], 1e-5)
                    S.vb(v)
                    v.reciprocal(wacc[:1, 35:36], wacc[:1, 34:35])
                    S.vb(v)
                    S.inc("p_dv", v.engine_nop())

                @P("tensor")
                def _(t, n=S.val("p_dv")):
                    t.wait_ge(S.sem("p_dv"), n)
                    i = t.matmul(ps[1][:, 0:2], k_ones, wacc[:1, 34:36],
                                 start=True, stop=True)
                    S.inc("p_mm", i)

                @P("vector")
                def _(v, n=S.val("p_mm"), wi=wi):
                    v.wait_ge(S.sem("p_mm"), n)
                    v.tensor_copy(k_sw[:, 2 * wi:2 * wi + 1], ps[1][:, 1:2])
                    v.tensor_copy(k_sw[:, 2 * wi + 1:2 * wi + 2], ps[1][:, 0:1])
                    S.vb(v)
                    S.inc("p_dv", v.engine_nop())

                for bk in range(nblk):
                    need = S.val("p_dv")
                    needsc = S.val("p_sc")
                    needmm = S.val("p_mm")

                    @P("sync")
                    def _(e, bk=bk, need=need, needsc=needsc, needmm=needmm,
                          wsrc=wsrc, cols=cols):
                        e.wait_ge(S.sem("p_dv"), need)
                        if needsc:
                            e.wait_ge(S.sem("p_sc"), needsc)
                        if needmm:
                            e.wait_ge(S.sem("p_mm"), needmm)
                        S.inc("p_ld", e.dma_start(
                            out=wbuf[:, :cols],
                            in_=wsrc[bk * 128:(bk + 1) * 128, :]), 16)

                    @P("vector")
                    def _(v, n=S.val("p_ld"), wi=wi, cols=cols):
                        v.wait_ge(S.sem("p_ld"), n)
                        v.tensor_scalar_mul(wbuf[:, :cols], wbuf[:, :cols],
                                            k_sw[:, 2 * wi:2 * wi + 1])
                        S.vb(v)
                        v.tensor_scalar(wbuf[:, :cols], wbuf[:, :cols],
                                        k_cr[:, 0:1], k_cr[:, 0:1],
                                        ALU.add, ALU.subtract)
                        S.vb(v)
                        v.tensor_scalar(wbuf[:, :cols], wbuf[:, :cols], -1.0, 1.0,
                                        ALU.max, ALU.min)
                        S.vb(v)
                        S.inc("p_dv", v.engine_nop())

                    nkc = cols // 128
                    base = S.val("p_mm")

                    @P("tensor")
                    def _(t, n=S.val("p_dv"), nkc=nkc, basc=S.val("p_sc")):
                        t.wait_ge(S.sem("p_dv"), n)
                        for kc in range(nkc):
                            if basc + kc - 1 > 0:
                                t.wait_ge(S.sem("p_sc"), basc + kc - 1)
                            i = t.transpose(ps[2 + (kc % 2)][:, 0:128],
                                            wbuf[:, kc * 128:(kc + 1) * 128], k_id128)
                            S.inc("p_mm", i)

                    @P("scalar")
                    def _(sc, base=base, nkc=nkc, bk=bk, wres=wres, rows=rows):
                        wv = wres.rearrange("p (k r) -> p k r", r=rows)
                        for kc in range(nkc):
                            sc.wait_ge(S.sem("p_mm"), base + kc + 1)
                            i = sc.copy(wv[:, kc, bk * 128:(bk + 1) * 128],
                                        ps[2 + (kc % 2)][:, 0:128])
                            S.inc("p_sc", i)

                @P("gpsimd")
                def _(g, n=S.val("p_sc"), wres=wres, dsp=dsp):
                    g.wait_ge(S.sem("p_sc"), n)
                    S.inc("p_out", g.dma_start(out=dsp, in_=wres), 16)
        return w1q, w2q

    def emit_fphase(tag, src_tok, src_fm, dst_fm, w1q, w2q, fscope,
                    wl=None, patch=None):
        _FB = {}
        """dst_fm = src_fm + 0.5*FFN(LN(src)). 1-tile stage skew."""
        sm = lambda n: ("f_" + n) if n not in () else (sn(n))
        sn = lambda n: tag + "_" + n
        _FB["s1r"] = S.val(sm("s1r"))
        _FB["mm1"] = S.val(sm("mm1"))
        xtok = S.sb(sn("xtok"), [128, NSUB * DM], scope=fscope)
        stats = S.sb(sn("stats"), [128, 16], scope=fscope)
        srow = S.sb(sn("srow"), [1, 2 * 272], scope=fscope)
        orow = S.sb(sn("orow"), [1, 2 * FT], scope=fscope)
        carr = S.sb(sn("carr"), [128, HD * FT], scope=fscope)
        hbuf = S.sb(sn("hbuf"), [128, 258], scope=fscope)
        s2s = S.sb(sn("s2s"), [128, 2 * FT], scope=fscope)
        i1s = S.sb(sn("i1s"), [128, 258], scope=fscope)
        rbuf = S.sb(sn("rbuf"), [128, FT], scope=fscope)
        obuf = S.sb(sn("obuf"), [128, FT], scope=fscope)
        maxc = S.sb(sn("maxc"), [128, 2 * FT], scope=fscope)
        xq = [S.sb(sn("xq%d" % i), [128, KD * 266], BF16, scope=fscope)
              for i in range(2)]
        cqa = S.sb(sn("cqa"), [128, HD * FT], BF16, scope=fscope)
        nbar = S.barrier(tag)
        for _eng in ("sync", "tensor", "scalar", "gpsimd", "vector"):
            @P(_eng)
            def _(e, nbar=nbar):
                e.wait_ge(S.sem("pb"), nbar)

        def stage1(j):
            c0 = j * FT
            pp = j % 2

            @P("sync")
            def _(e, j=j, c0=c0, n2=S.val(sm("st")), ntq=S.val(sm("tq"))):
                if j >= 1:
                    e.wait_ge(S.sem(sm("st")), n2)
                    e.wait_ge(S.sem(sm("tq")), ntq)
                xs = xtok.rearrange("p (s f) -> p s f", s=NSUB)

                def ld_rows(r0, r1):
                    t0 = r0 - c0
                    s_ = t0 // 128
                    p0 = t0 % 128
                    i = e.dma_start(out=xs[p0:p0 + (r1 - r0), s_:s_ + 1, :],
                                    in_=src_tok[r0:r1, :].rearrange(
                                        "(s p) f -> p s f", p=r1 - r0))
                    S.inc(sm("ld"), i, 16)
                    return i

                if patch is not None and j == 0:
                    ld_rows(8, 128)
                    ld_rows(128, 256)
                    e.wait_ge(patch["bd0"], 1)
                    ld_rows(0, 8)
                elif patch is not None and j == NFT - 1:
                    ld_rows(c0, OWN0 + TOWN)
                    ld_rows(OWN0 + TOWN + 1, c0 + 128)
                    ld_rows(c0 + 128, c0 + 256)
                    e.wait_ge(patch["bd1"], 1)
                    ld_rows(OWN0 + TOWN, OWN0 + TOWN + 1)
                else:
                    i = e.dma_start(out=xs,
                                    in_=src_tok[c0:c0 + FT, :].rearrange(
                                        "(s p) f -> p s f", p=128))
                    S.inc(sm("ld"), i, 16)
            nld = S.val(sm("ld"))

            @P("scalar")
            def _(sc, n=nld, ntq=S.val(sm("tq")), nxqe=S.val(sm("xqe"))):
                sc.wait_ge(S.sem(sm("ld")), n)
                if ntq:
                    sc.wait_ge(S.sem(sm("tq")), ntq)
                if nxqe:
                    sc.wait_ge(S.sem(sm("xqe")), nxqe)
                xs = xtok.rearrange("p (s f) -> p s f", s=NSUB)
                for s_ in range(NSUB):
                    if s_:
                        sc.wait_ge(S.sem(sm("sq")), S.val(sm("sq")))
                    for hf in range(2):
                        i = sc.activation(ps[4 + hf][:, 0:512],
                                          xs[:, s_, hf * 512:(hf + 1) * 512],
                                          ACT.Square,
                                          accum_out=stats[:, 2 * s_ + hf:2 * s_ + hf + 1])
                        S.inc(sm("sq"), i)
            nsq = S.val(sm("sq"))

            @P("vector")
            def _(v, n=nld, nsq=nsq, j=j):
                v.wait_ge(S.sem(sm("ld")), n)
                v.wait_ge(S.sem("c_pre"), S.val("c_pre"))
                xs = xtok.rearrange("p (s f) -> p s f", s=NSUB)
                for s_ in range(NSUB):
                    v.reduce_sum(stats[:, 4 + s_:5 + s_], xs[:, s_, :], AX)
                v.wait_ge(S.sem(sm("sq")), nsq)
                S.vb(v)
                v.tensor_add(stats[:, 0:1], stats[:, 0:1], stats[:, 1:2])
                S.vb(v)
                v.tensor_add(stats[:, 1:2], stats[:, 2:3], stats[:, 3:4])
                S.vb(v)
                v.tensor_copy(stats[:, 2:4], stats[:, 4:6])
                S.vb(v)
                v.tensor_scalar_mul(stats[:, 4:6], stats[:, 2:4], 1.0 / DM)
                v.tensor_scalar_mul(stats[:, 6:8], stats[:, 0:2], 1.0 / DM)
                S.vb(v)
                v.tensor_mul(stats[:, 8:10], stats[:, 4:6], stats[:, 4:6])
                S.vb(v)
                v.tensor_sub(stats[:, 6:8], stats[:, 6:8], stats[:, 8:10])
                S.vb(v)
                v.tensor_scalar_add(stats[:, 6:8], stats[:, 6:8], EPS)
                S.vb(v)
                v.reciprocal(stats[:, 8:10], stats[:, 6:8])
                S.vb(v)
                S.inc(sm("st"), v.engine_nop())

            @P("scalar")
            def _(sc, n=S.val(sm("st"))):
                sc.wait_ge(S.sem(sm("st")), n)
                i = sc.activation(stats[:, 10:12], stats[:, 8:10], ACT.Sqrt)
                S.inc(sm("sq"), i)

            @P("vector")
            def _(v, n=S.val(sm("sq")), j=j):
                v.wait_ge(S.sem(sm("sq")), n)
                v.tensor_mul(stats[:, 12:14], stats[:, 10:12], stats[:, 10:12])
                S.vb(v)
                v.tensor_mul(stats[:, 12:14], stats[:, 12:14], stats[:, 6:8])
                S.vb(v)
                v.tensor_scalar(stats[:, 12:14], stats[:, 12:14], -1.0, 3.0,
                                ALU.mult, ALU.add)
                S.vb(v)
                v.tensor_mul(stats[:, 12:14], stats[:, 12:14], stats[:, 10:12])
                S.vb(v)
                v.tensor_scalar_mul(stats[:, 10:12], stats[:, 12:14], 0.5)
                S.vb(v)
                xs = xtok.rearrange("p (s f) -> p s f", s=NSUB)
                for s_ in range(NSUB):
                    for hf in range(2):
                        v.tensor_scalar(ps[4 + hf][:, 0:512],
                                        xs[:, s_, hf * 512:(hf + 1) * 512],
                                        stats[:, 4 + s_:5 + s_], -1.0,
                                        ALU.subtract, ALU.mult)
                    S.vb(v)
                    v.reduce_max(stats[:, 12 + s_:13 + s_], ps[4][:, 0:512], AX,
                                 apply_absolute_value=True)
                    v.reduce_max(stats[:, 2:3], ps[5][:, 0:512], AX,
                                 apply_absolute_value=True)
                    S.vb(v)
                    v.tensor_max(stats[:, 12 + s_:13 + s_], stats[:, 12 + s_:13 + s_],
                                 stats[:, 2:3])
                    S.vb(v)
                v.tensor_mul(stats[:, 12:14], stats[:, 12:14], stats[:, 10:12])
                S.vb(v)
                v.tensor_scalar_max(stats[:, 12:14], stats[:, 12:14], 1e-5)
                S.vb(v)
                v.reciprocal(stats[:, 14:16], stats[:, 12:14])
                S.vb(v)
                v.tensor_scalar_mul(stats[:, 14:16], stats[:, 14:16], 127.0)
                S.vb(v)
                v.tensor_scalar_mul(stats[:, 12:14], stats[:, 12:14], 1.0 / 127.0)
                S.vb(v)
                v.tensor_scalar_mul(stats[:, 12:14], stats[:, 12:14], k_sw[:, 1:2])
                S.vb(v)
                v.tensor_mul(stats[:, 12:14], stats[:, 12:14],
                             k_vtok[:, j * NSUB:(j + 1) * NSUB])
                S.vb(v)
                v.tensor_mul(stats[:, 14:16], stats[:, 14:16], stats[:, 10:12])
                S.vb(v)
                v.tensor_mul(stats[:, 8:10], stats[:, 4:6], stats[:, 14:16])
                S.vb(v)
                v.tensor_scalar_mul(stats[:, 8:10], stats[:, 8:10], -1.0)
                S.vb(v)
                for s_ in range(NSUB):
                    for hf in range(2):
                        v.tensor_scalar(ps[4 + hf][:, 0:512],
                                        xs[:, s_, hf * 512:(hf + 1) * 512],
                                        stats[:, 14 + s_:15 + s_],
                                        stats[:, 8 + s_:9 + s_], ALU.mult, ALU.add)
                    S.vb(v)
                    for hf in range(2):
                        v.tensor_scalar(xs[:, s_, hf * 512:(hf + 1) * 512],
                                        ps[4 + hf][:, 0:512], k_cr[:, 0:1],
                                        k_cr[:, 0:1], ALU.add, ALU.subtract)
                    S.vb(v)
                S.inc(sm("st"), v.engine_nop())
            nst = S.val(sm("st"))

            baseT = S.val(sm("tq"))
            baseE = S.val(sm("xqe"))

            @P("tensor")
            def _(t, n=nst, baseT=baseT, baseE=baseE, j=j):
                t.wait_ge(S.sem(sm("st")), n)
                if j >= 2:
                    t.wait_ge(S.sem(sm("mm1")), _FB["mm1"] + (j - 1) * HD)
                xs = xtok.rearrange("p (s f) -> p s f", s=NSUB)
                k = 0
                for s_ in range(NSUB):
                    for kc in range(KD):
                        if baseE + k - 1 > 0:
                            t.wait_ge(S.sem(sm("xqe")), baseE + k - 1)
                        i = t.transpose(ps[4 + (k % 2)][:, 0:128],
                                        xs[:, s_, kc * 128:(kc + 1) * 128], k_id128)
                        S.inc(sm("tq"), i)
                        k += 1
                if j >= 1:
                    t.wait_ge(S.sem(sm("rwe")), S.val(sm("rwe")))
                for s_ in range(NSUB):
                    i = t.transpose(ps[7][:1, s_ * 128:(s_ + 1) * 128],
                                    stats[:, 12 + s_:13 + s_], k_id128)
                    S.inc(sm("tq"), i)

            @P("scalar")
            def _(sc, baseT=baseT, pp=pp, j=j):
                xv = xq[pp].rearrange("p (k t) -> p k t", k=KD)
                k = 0
                for s_ in range(NSUB):
                    for kc in range(KD):
                        sc.wait_ge(S.sem(sm("tq")), baseT + k + 1)
                        i = sc.copy(xv[:, kc, HALO + s_ * 128:HALO + (s_ + 1) * 128],
                                    ps[4 + (k % 2)][:, 0:128])
                        S.inc(sm("xqe"), i)
                        k += 1
                sc.wait_ge(S.sem(sm("tq")), baseT + k + NSUB)
                rr = srow.rearrange("o (b t) -> o b t", b=2)
                i = sc.copy(rr[:, j % 2, 8:8 + FT], ps[7][:1, 0:FT])
                S.inc(sm("rwe"), i)

            @P("vector")
            def _(v, n=S.val(sm("xqe")), nr=S.val(sm("rwe")), pp=pp, j=j):
                v.wait_ge(S.sem(sm("xqe")), n)
                v.wait_ge(S.sem(sm("rwe")), nr)
                xv = xq[pp].rearrange("p (k t) -> p k t", k=KD)
                rr = srow.rearrange("o (b t) -> o b t", b=2)
                if j == 0:
                    v.memset(xv[:, :, 0:HALO], 0.0)
                    v.memset(xq[0].rearrange("p (k t) -> p k t", k=KD)[:, :, 264:266], 0.0)
                    v.memset(xq[1].rearrange("p (k t) -> p k t", k=KD)[:, :, 264:266], 0.0)
                    v.memset(rr[:, 0, 0:8], 0.0)
                    v.memset(rr[:, 0, 264:272], 0.0)
                    v.memset(rr[:, 1, 264:272], 0.0)
                else:
                    xvp = xq[1 - pp].rearrange("p (k t) -> p k t", k=KD)
                    v.tensor_copy(xv[:, :, 0:HALO], xvp[:, :, FT:FT + HALO])
                    v.tensor_copy(rr[:, pp, 0:8], rr[:, 1 - pp, FT:FT + 8])
                    S.vb(v)
                    v.tensor_copy(xvp[:, :, 264:265], xv[:, :, HALO:HALO + 1])
                    v.tensor_copy(rr[:, 1 - pp, 264:265], rr[:, pp, 8:9])
                S.vb(v)
                S.inc(sm("s1r"), v.engine_nop())

        def stage2(j):
            c0 = j * FT
            pp = j % 2
            need_s1 = _FB["s1r"] + min(j + 2, NFT)

            nresb = S.val(sm("res"))

            @P("sync")
            def _(e, j=j, c0=c0, nresb=nresb):
                for oc in range(KD):
                    if nresb + oc > 0:
                        e.wait_ge(S.sem(sm("res")), nresb + oc)
                    i = e.dma_start(out=rbuf,
                                    in_=src_fm[oc, :, c0:c0 + FT])
                    S.inc(sm("ldf"), i, 16)
            nldf0 = S.val(sm("ldf")) - 16 * KD

            @P("tensor")
            def _(t, n1=need_s1, j=j, pp=pp, ns2e=S.val(sm("s2e"))):
                t.wait_ge(S.sem(sm("s1r")), n1)
                if ns2e:
                    t.wait_ge(S.sem(sm("s2e")), ns2e)
                rr = srow.rearrange("o (b t) -> o b t", b=2)
                i = t.matmul(ps[6][:, 0:258], k_ones, rr[:1, pp, 7:7 + 258],
                             start=True, stop=True)
                S.inc(sm("i1"), i)

            @P("scalar")
            def _(sc, n=S.val(sm("i1"))):
                sc.wait_ge(S.sem(sm("i1")), n)
                i = sc.copy(i1s, ps[6][:, 0:258])
                S.inc(sm("i1e"), i)

            basem = S.val(sm("mm1"))

            @P("tensor")
            def _(t, j=j, pp=pp, basem=basem, n1=need_s1):
                t.wait_ge(S.sem(sm("s1r")), n1)
                if j == 0:
                    if wl is None:
                        t.wait_ge(S.sem("p_out"), S.val("p_out"))
                    else:
                        t.wait_ge(S.sem(wl[0]), wl[1])
                xv = xq[pp].rearrange("p (k t) -> p k t", k=KD)
                w1v = w1q.rearrange("p (k r) -> p k r", r=DH)
                for hc in range(HD):
                    if basem + hc - 1 > 0:
                        t.wait_ge(S.sem(sm("cv")), basem + hc - 1)
                    i = None
                    for kc in range(KD):
                        i = t.matmul(ps[hc % 2][:, 0:258],
                                     w1v[:, kc, hc * 128:(hc + 1) * 128],
                                     xv[:, kc, HALO - 1:HALO + 257],
                                     start=(kc == 0), stop=(kc == KD - 1))
                    S.inc(sm("mm1"), i)

            @P("vector")
            def _(v, j=j, basem=basem, ni=S.val(sm("i1e")), nsi=S.val(sm("si"))):
                v.wait_ge(S.sem(sm("i1e")), ni)
                cwv = k_cw.rearrange("p (k c) -> p k c", k=HD)
                cbv = k_cb.rearrange("p (k c) -> p k c", k=HD)
                ca = carr.rearrange("p (h t) -> p h t", h=HD)
                basecv = S.val(sm("cv"))
                for hc in range(HD):
                    v.wait_ge(S.sem(sm("mm1")), basem + hc + 1)
                    if hc == 0 and j > 0:
                        v.wait_ge(S.sem(sm("si")), nsi)
                    if hc:
                        v.wait_ge(S.sem(sm("cv")), basecv + hc)
                    hb = hbuf
                    v.tensor_mul(hb[:, 0:258], ps[hc % 2][:, 0:258], i1s)
                    S.vb(v)
                    cc = ca[:, hc, :]
                    v.scalar_tensor_tensor(cc, hb[:, 0:256], cwv[:, hc, 0:1],
                                           bfree(cbv[:, hc, 0:1], FT), ALU.mult, ALU.add)
                    S.vb(v)
                    v.scalar_tensor_tensor(cc, hb[:, 1:257], cwv[:, hc, 1:2], cc,
                                           ALU.mult, ALU.add)
                    S.vb(v)
                    i = v.scalar_tensor_tensor(cc, hb[:, 2:258], cwv[:, hc, 2:3], cc,
                                               ALU.mult, ALU.add)
                    S.inc(sm("cv"), i)
            ncv = S.val(sm("cv"))

            @P("scalar")
            def _(sc, ncv=ncv):
                ca = carr.rearrange("p (h t) -> p h t", h=HD)
                for hc in range(HD):
                    sc.wait_ge(S.sem(sm("cv")), ncv - HD + hc + 1)
                    i = sc.activation(ca[:, hc, :], ca[:, hc, :], ACT.Silu)
                    S.inc(sm("si"), i)
            nsi = S.val(sm("si"))

            @P("vector")
            def _(v, nsi=nsi, j=j):
                ca = carr.rearrange("p (h t) -> p h t", h=HD)
                mxv = maxc.rearrange("p (b t) -> p b t", b=2)
                for hc in range(HD):
                    v.wait_ge(S.sem(sm("si")), nsi - HD + hc + 1)
                    if hc == 0:
                        v.tensor_copy(mxv[:, 0, :], ca[:, 0, :])
                        v.tensor_copy(mxv[:, 1, :], ca[:, 0, :])
                        S.vb(v)
                    else:
                        v.tensor_max(mxv[:, 0, :], mxv[:, 0, :], ca[:, hc, :])
                        v.tensor_tensor(mxv[:, 1, :], mxv[:, 1, :], ca[:, hc, :],
                                        ALU.min)
                        S.vb(v)
                v.tensor_scalar_mul(mxv[:, 1, :], mxv[:, 1, :], -1.0)
                S.vb(v)
                v.tensor_max(mxv[:, 0, :], mxv[:, 0, :], mxv[:, 1, :])
                S.vb(v)
                S.inc(sm("mx"), v.engine_nop())

            @P("tensor")
            def _(t, n=S.val(sm("mx")), nprev=S.val(sm("s2c"))):
                t.wait_ge(S.sem(sm("mx")), n)
                if nprev:
                    t.wait_ge(S.sem(sm("s2c")), nprev)
                for s_ in range(NSUB):
                    i = t.transpose(ps[5][:, s_ * 128:(s_ + 1) * 128],
                                    maxc.rearrange("p (b t) -> p b t", b=2)[:, 0, s_ * 128:(s_ + 1) * 128],
                                    k_id128)
                    S.inc(sm("mxt"), i)

            @P("vector")
            def _(v, n=S.val(sm("mxt"))):
                v.wait_ge(S.sem(sm("mxt")), n)
                for s_ in range(NSUB):
                    v.reduce_max(stats[:, 2 + s_:3 + s_],
                                 ps[5][:, s_ * 128:(s_ + 1) * 128], AX)
                S.vb(v)
                v.tensor_scalar_max(stats[:, 2:4], stats[:, 2:4], 1e-5)
                S.vb(v)
                v.reciprocal(stats[:, 4:6], stats[:, 2:4])
                S.vb(v)
                v.tensor_scalar_mul(stats[:, 4:6], stats[:, 4:6], 127.0)
                S.vb(v)
                v.tensor_scalar(stats[:, 6:8], stats[:, 2:4], 1.0 / 127.0, 0.5,
                                ALU.mult, ALU.mult)
                S.vb(v)
                v.tensor_scalar_mul(stats[:, 6:8], stats[:, 6:8], k_sw[:, 3:4])
                S.vb(v)
                S.inc(sm("s2c"), v.engine_nop())

            @P("tensor")
            def _(t, n=S.val(sm("s2c")), ni1e=S.val(sm("i1e"))):
                t.wait_ge(S.sem(sm("s2c")), n)
                t.wait_ge(S.sem(sm("i1e")), ni1e)
                k = 0
                for col in (4, 5, 6, 7):
                    i = t.transpose(ps[6][:1, k * 128:(k + 1) * 128],
                                    stats[:, col:col + 1], k_id128)
                    S.inc(sm("mxt"), i)
                    k += 1

            @P("scalar")
            def _(sc, n=S.val(sm("mxt"))):
                sc.wait_ge(S.sem(sm("mxt")), n)
                i = sc.copy(orow, ps[6][:1, 0:512])
                S.inc(sm("rw2"), i)
            nrw2 = S.val(sm("rw2"))

            @P("tensor")
            def _(t, n=nrw2):
                t.wait_ge(S.sem(sm("rw2")), n)
                i = t.matmul(ps[6][:, 0:512], k_ones, orow, start=True, stop=True)
                S.inc(sm("i1"), i)

            @P("scalar")
            def _(sc, n=S.val(sm("i1"))):
                sc.wait_ge(S.sem(sm("i1")), n)
                i = sc.copy(s2s, ps[6][:, 0:512])
                S.inc(sm("s2e"), i)

            @P("vector")
            def _(v, n=S.val(sm("s2e"))):
                v.wait_ge(S.sem(sm("s2e")), n)
                cav = carr.rearrange("p (h t) -> p h t", h=HD)
                s2m = bass.AP(s2s.tensor, s2s.offset,
                              [list(s2s.ap[0]), [0, HD], [1, FT]])
                v.tensor_tensor(cav, cav, s2m, ALU.mult)
                S.vb(v)
                v.tensor_scalar(cqa.rearrange("p (h t) -> p h t", h=HD), cav,
                                k_cr[:, 0:1], k_cr[:, 0:1], ALU.add, ALU.subtract)
                S.vb(v)
                S.inc(sm("q2"), v.engine_nop())
            nq2 = S.val(sm("q2"))

            basem2 = S.val(sm("mm2"))
            resbase = S.val(sm("res"))

            @P("tensor")
            def _(t, nq2=nq2, basem2=basem2, resbase=resbase):
                t.wait_ge(S.sem(sm("q2")), nq2)
                cv = cqa.rearrange("p (h t) -> p h t", h=HD)
                w2v = w2q.rearrange("p (k r) -> p k r", r=DM)
                for oc in range(KD):
                    if resbase + oc - 1 > 0:
                        t.wait_ge(S.sem(sm("res")), resbase + oc - 1)
                    i = None
                    for hc in range(HD):
                        i = t.matmul(ps[2 + (oc % 2)][:, 0:FT],
                                     w2v[:, hc, oc * 128:(oc + 1) * 128],
                                     cv[:, hc, :],
                                     start=(hc == 0), stop=(hc == HD - 1))
                    S.inc(sm("mm2"), i)

            nspb = S.val(sm("sp"))

            @P("vector")
            def _(v, basem2=basem2, nldf0=nldf0, j=j, c0=c0, nspb=nspb):
                s2i = bass.AP(s2s.tensor, s2s.offset + FT,
                              [list(s2s.ap[0]), [1, FT]])
                for oc in range(KD):
                    v.wait_ge(S.sem(sm("mm2")), basem2 + oc + 1)
                    v.wait_ge(S.sem(sm("ldf")), nldf0 + 16 * (oc + 1))
                    if nspb + 16 * oc > 0:
                        v.wait_ge(S.sem(sm("sp")), nspb + 16 * oc)
                    v.tensor_mul(obuf, ps[2 + (oc % 2)][:, 0:FT], s2i)
                    S.vb(v)
                    i = v.tensor_add(obuf, obuf, rbuf)
                    S.vb(v)
                    S.inc(sm("res"), i)
            nres = S.val(sm("res"))

            @P("gpsimd")
            def _(g, nres0=nres - KD, j=j, c0=c0):
                for oc in range(KD):
                    g.wait_ge(S.sem(sm("res")), nres0 + oc + 1)
                    i = g.dma_start(out=dst_fm[oc, :, c0:c0 + FT],
                                    in_=obuf)
                    S.inc(sm("sp"), i, 16)

        for j in range(NFT):
            stage1(j)
            if j >= 1:
                stage2(j - 1)
        stage2(NFT - 1)

    def emit_a1a(ascope):
        sn = lambda n: "aa_" + n
        wb_ = {nm: S.sb(sn(nm), [128, KD * DM], scope=ascope)
               for nm in ("wq", "wk", "wv")}
        xfm = [S.sb(sn("xfm%d" % i), [128, KD * FT], scope=ascope) for i in range(2)]
        xnf = [S.sb(sn("xnf%d" % i), [128, KD * FT], scope=ascope) for i in range(2)]
        xtok = S.sb(sn("xtok"), [128, NSUB * DM], scope=ascope)
        stats = S.sb(sn("stats"), [128, 16], scope=ascope)
        srow = S.sb(sn("srow"), [1, 512], scope=ascope)
        cbuf = {nm: S.sb(sn("cb" + nm), [128, H * 259], scope=ascope)
                for nm in "qkv"}
        obuf = {nm: S.sb(sn("ob" + nm), [128, H * FT], scope=ascope)
                for nm in "qkv"}
        nbar = S.barrier("aa")
        for _eng in ("sync", "tensor", "scalar", "gpsimd", "vector"):
            @P(_eng)
            def _(e, nbar=nbar):
                e.wait_ge(S.sem("pb"), nbar)

        @P("sync")
        def _(e):
            for nm, src in (("wq", wpi["wq"]), ("wk", wpi["wk"]), ("wv", wpi["wv"])):
                i = e.dma_start(
                    out=wb_[nm].rearrange("p (k o) -> p k o", k=KD),
                    in_=src.rearrange("(k p) o -> p k o", p=128))
                S.inc(sn("wl"), i, 16)
        nwl = S.val(sn("wl"))

        @P("vector")
        def _(v):
            for nm in "qkv":
                v.memset(cbuf[nm].rearrange("p (h c) -> p h c", h=H)[:, :, 0:3], 0.0)
            S.vb(v)
            S.inc(sn("hz"), v.engine_nop())

        def stage1(j):
            c0 = j * FT
            pp = j % 2

            @P("sync")
            def _(e, j=j, c0=c0, pp=pp, ntq=S.val(sn("tq")), nxn=S.val(sn("xnr")),
                  nld0=S.val(sn("ld"))):
                if j >= 2:
                    e.wait_ge(S.sem(sn("tq")), ntq)
                    e.wait_ge(S.sem(sn("xnr")), nxn)
                if j >= 1:
                    e.wait_ge(S.sem(sn("ld")), nld0)
                i = e.dma_start(
                    out=xfm[pp].rearrange("p (k t) -> p k t", k=KD),
                    in_=d_xmid[:, :, c0:c0 + FT].rearrange("k p t -> p k t"))
                S.inc(sn("ld"), i, 16)
            nld = S.val(sn("ld"))

            baseT = S.val(sn("tq"))
            baseE = S.val(sn("xte"))

            @P("tensor")
            def _(t, j=j, pp=pp, nld=nld, baseT=baseT, baseE=baseE,
                  nsq=S.val(sn("sq")), nrw=S.val(sn("rw"))):
                t.wait_ge(S.sem(sn("ld")), nld)
                if nsq:
                    t.wait_ge(S.sem(sn("sq")), nsq)
                xv = xfm[pp].rearrange("p (k t) -> p k t", k=KD)
                k = 0
                for s_ in range(NSUB):
                    for kc in range(KD):
                        if baseE + k - 1 > 0:
                            t.wait_ge(S.sem(sn("xte")), baseE + k - 1)
                        i = t.transpose(ps[4 + (k % 2)][:, 0:128],
                                        xv[:, kc, s_ * 128:(s_ + 1) * 128], k_id128)
                        S.inc(sn("tq"), i)
                        k += 1

            @P("scalar")
            def _(sc, baseT=baseT, j=j):
                xs = xtok.rearrange("p (s f) -> p s f", s=NSUB)
                k = 0
                for s_ in range(NSUB):
                    for kc in range(KD):
                        sc.wait_ge(S.sem(sn("tq")), baseT + k + 1)
                        if k == 0 and j >= 1:
                            sc.wait_ge(S.sem(sn("st")), S.val(sn("st")))
                        i = sc.copy(xs[:, s_, kc * 128:(kc + 1) * 128],
                                    ps[4 + (k % 2)][:, 0:128])
                        S.inc(sn("xte"), i)
                        k += 1
            nxte = S.val(sn("xte"))

            @P("scalar")
            def _(sc, n=nxte):
                sc.wait_ge(S.sem(sn("xte")), n)
                xs = xtok.rearrange("p (s f) -> p s f", s=NSUB)
                for s_ in range(NSUB):
                    if s_:
                        sc.wait_ge(S.sem(sn("sq")), S.val(sn("sq")))
                    for hf in range(2):
                        i = sc.activation(ps[4 + hf][:, 0:512],
                                          xs[:, s_, hf * 512:(hf + 1) * 512],
                                          ACT.Square,
                                          accum_out=stats[:, 2 * s_ + hf:2 * s_ + hf + 1])
                        S.inc(sn("sq"), i)
            nsq = S.val(sn("sq"))

            @P("vector")
            def _(v, n=nxte, nsq=nsq, j=j):
                v.wait_ge(S.sem(sn("xte")), n)
                v.wait_ge(S.sem("c_pre"), S.val("c_pre"))
                xs = xtok.rearrange("p (s f) -> p s f", s=NSUB)
                for s_ in range(NSUB):
                    v.reduce_sum(stats[:, 4 + s_:5 + s_], xs[:, s_, :], AX)
                v.wait_ge(S.sem(sn("sq")), nsq)
                S.vb(v)
                v.tensor_add(stats[:, 0:1], stats[:, 0:1], stats[:, 1:2])
                S.vb(v)
                v.tensor_add(stats[:, 1:2], stats[:, 2:3], stats[:, 3:4])
                S.vb(v)
                v.tensor_scalar_mul(stats[:, 4:6], stats[:, 4:6], 1.0 / DM)
                v.tensor_scalar_mul(stats[:, 6:8], stats[:, 0:2], 1.0 / DM)
                S.vb(v)
                v.tensor_mul(stats[:, 8:10], stats[:, 4:6], stats[:, 4:6])
                S.vb(v)
                v.tensor_sub(stats[:, 6:8], stats[:, 6:8], stats[:, 8:10])
                S.vb(v)
                v.tensor_scalar_add(stats[:, 6:8], stats[:, 6:8], EPS)
                S.vb(v)
                v.reciprocal(stats[:, 8:10], stats[:, 6:8])
                S.vb(v)
                S.inc(sn("st"), v.engine_nop())

            @P("scalar")
            def _(sc, n=S.val(sn("st"))):
                sc.wait_ge(S.sem(sn("st")), n)
                i = sc.activation(stats[:, 10:12], stats[:, 8:10], ACT.Sqrt)
                S.inc(sn("sq"), i)

            @P("vector")
            def _(v, n=S.val(sn("sq")), j=j):
                v.wait_ge(S.sem(sn("sq")), n)
                v.tensor_mul(stats[:, 12:14], stats[:, 10:12], stats[:, 10:12])
                S.vb(v)
                v.tensor_mul(stats[:, 12:14], stats[:, 12:14], stats[:, 6:8])
                S.vb(v)
                v.tensor_scalar(stats[:, 12:14], stats[:, 12:14], -1.0, 3.0,
                                ALU.mult, ALU.add)
                S.vb(v)
                v.tensor_mul(stats[:, 12:14], stats[:, 12:14], stats[:, 10:12])
                S.vb(v)
                v.tensor_scalar_mul(stats[:, 10:12], stats[:, 12:14], 0.5)
                S.vb(v)
                v.tensor_mul(stats[:, 12:14], stats[:, 10:12],
                             k_vtok[:, j * NSUB:(j + 1) * NSUB])
                S.vb(v)
                S.inc(sn("ct"), v.engine_nop())

            @P("tensor")
            def _(t, n=S.val(sn("ct")), nrw=S.val(sn("rw"))):
                t.wait_ge(S.sem(sn("ct")), n)
                if nrw:
                    t.wait_ge(S.sem(sn("rw")), nrw)
                k = 0
                for col in (4, 5, 12, 13):
                    i = t.transpose(ps[7][:1, k * 128:(k + 1) * 128],
                                    stats[:, col:col + 1], k_id128)
                    S.inc(sn("tq"), i)
                    k += 1

            @P("scalar")
            def _(sc, n=S.val(sn("tq")), ni1=S.val(sn("i1"))):
                sc.wait_ge(S.sem(sn("tq")), n)
                if ni1:
                    sc.wait_ge(S.sem(sn("i1")), ni1)
                i = sc.copy(srow, ps[7][:1, 0:512])
                S.inc(sn("rw"), i)

            @P("tensor")
            def _(t, n=S.val(sn("rw")), nxn=S.val(sn("xnr"))):
                t.wait_ge(S.sem(sn("rw")), n)
                if nxn:
                    t.wait_ge(S.sem(sn("xnr")), nxn)
                i = t.matmul(ps[6][:, 0:512], k_ones, srow, start=True, stop=True)
                S.inc(sn("i1"), i)

            @P("vector")
            def _(v, n=S.val(sn("i1")), pp=pp, nld=nld,
                  nxsp=max(0, S.val(sn("xsp")) - 16)):
                v.wait_ge(S.sem(sn("i1")), n)
                v.wait_ge(S.sem(sn("ld")), nld)
                if nxsp:
                    v.wait_ge(S.sem(sn("xsp")), nxsp)
                murep = bass.AP(ps[6].tensor, ps[6].offset,
                                [list(ps[6].ap[0]), [0, KD], [1, FT]])
                rsrep = bass.AP(ps[6].tensor, ps[6].offset + FT,
                                [list(ps[6].ap[0]), [0, KD], [1, FT]])
                xv = xfm[pp].rearrange("p (k t) -> p k t", k=KD)
                nv = xnf[pp].rearrange("p (k t) -> p k t", k=KD)
                v.tensor_tensor(nv, xv, murep, ALU.subtract)
                S.vb(v)
                v.tensor_tensor(nv, nv, rsrep, ALU.mult)
                S.vb(v)
                i = v.engine_nop()
                S.inc(sn("xnr"), i)
            nxnr = S.val(sn("xnr"))

            @P("gpsimd")
            def _(g, n=nxnr, pp=pp, c0=c0, nxsp0=S.val(sn("xsp"))):
                g.wait_ge(S.sem(sn("xnr")), n)
                if nxsp0:
                    g.wait_ge(S.sem(sn("xsp")), nxsp0)
                i = g.dma_start(
                    out=d_xn[:, :, c0:c0 + FT].rearrange("k p t -> p k t"),
                    in_=xnf[pp].rearrange("p (k t) -> p k t", k=KD))
                S.inc(sn("xsp"), i, 16)

        def stage2(j):
            c0 = j * FT
            pp = j % 2
            basem = S.val(sn("mm"))
            basee = S.val(sn("ev"))

            @P("tensor")
            def _(t, j=j, pp=pp, basem=basem, basee=basee,
                  nxn=S.val(sn("xnr")), nwl=nwl):
                t.wait_ge(S.sem(sn("xnr")), nxn)
                if j == 0:
                    t.wait_ge(S.sem(sn("wl")), nwl)
                nv = xnf[pp].rearrange("p (k t) -> p k t", k=KD)
                r = 0
                for nm in ("wq", "wk", "wv"):
                    wv = wb_[nm].rearrange("p (k o) -> p k o", k=KD)
                    for oc in range(KD):
                        if basee + r - 1 > 0:
                            t.wait_ge(S.sem(sn("ev")), basee + r - 1)
                        i = None
                        for fc in range(KD):
                            i = t.matmul(ps[r % 2][:, 0:FT],
                                         wv[:, fc, oc * 128:(oc + 1) * 128],
                                         nv[:, fc, :],
                                         start=(fc == 0), stop=(fc == KD - 1))
                        S.inc(sn("mm"), i)
                        r += 1

            @P("scalar")
            def _(sc, j=j, basem=basem, ncr=S.val(sn("cr")), nhz=S.val(sn("hz"))):
                r = 0
                sc.wait_ge(S.sem(sn("hz")), nhz)
                for nm in "qkv":
                    cv = cbuf[nm].rearrange("p (h c) -> p h c", h=H)
                    for oc in range(KD):
                        sc.wait_ge(S.sem(sn("mm")), basem + r + 1)
                        if j >= 1:
                            sc.wait_ge(S.sem(sn("cr")), ncr - 24 + r + 1)
                        i = sc.copy(cv[:, oc, 3:259], ps[r % 2][:, 0:FT])
                        S.inc(sn("ev"), i)
                        r += 1

            @P("vector")
            def _(v, j=j, basee=basee, nsp=S.val(sn("sp"))):
                r = 0
                for nm in "qkv":
                    cv = cbuf[nm].rearrange("p (h c) -> p h c", h=H)
                    ov = obuf[nm].rearrange("p (h t) -> p h t", h=H)
                    cw = k_aconv[nm][0].rearrange("p (h c) -> p h c", h=H)
                    cb2 = k_aconv[nm][1].rearrange("p (h c) -> p h c", h=H)
                    for oc in range(KD):
                        v.wait_ge(S.sem(sn("ev")), basee + r + 1)
                        if j >= 1 and r == 0:
                            v.wait_ge(S.sem(sn("sp")), nsp)
                        cc = ov[:, oc, :]
                        v.scalar_tensor_tensor(cc, cv[:, oc, 0:FT], cw[:, oc, 0:1],
                                               bfree(cb2[:, oc, 0:1], FT),
                                               ALU.mult, ALU.add)
                        S.vb(v)
                        for jj in range(1, 4):
                            v.scalar_tensor_tensor(cc, cv[:, oc, jj:jj + FT],
                                                   cw[:, oc, jj:jj + 1], cc,
                                                   ALU.mult, ALU.add)
                            S.vb(v)
                        v.tensor_copy(cv[:, oc, 0:3], cv[:, oc, 256:259])
                        S.vb(v)
                        i = v.engine_nop()
                        S.inc(sn("cr"), i)
                        r += 1

            @P("scalar")
            def _(sc, basec=S.val(sn("cr")) - 24):
                r = 0
                for nm in "qkv":
                    ov = obuf[nm].rearrange("p (h t) -> p h t", h=H)
                    for oc in range(KD):
                        sc.wait_ge(S.sem(sn("cr")), basec + r + 1)
                        i = sc.activation(ov[:, oc, :], ov[:, oc, :], ACT.Silu)
                        S.inc(sn("si"), i)
                        r += 1
            nsi = S.val(sn("si"))

            @P("vector")
            def _(v, nsi=nsi):
                v.wait_ge(S.sem(sn("si")), nsi - 8)
                i = v.tensor_scalar_mul(obuf["k"], obuf["k"], 128 ** -0.5)
                S.vb(v)
                S.inc(sn("ks"), v.engine_nop())
            nks = S.val(sn("ks"))

            @P("gpsimd")
            def _(g, j=j, c0=c0, nsi=nsi, nks=nks):
                for nm, dst in (("q", d_q), ("k", d_k), ("v", d_v)):
                    if nm == "k":
                        g.wait_ge(S.sem(sn("ks")), nks)
                    elif nm == "q":
                        g.wait_ge(S.sem(sn("si")), nsi - 16)
                    else:
                        g.wait_ge(S.sem(sn("si")), nsi)
                    i = g.dma_start(
                        out=dst[:, :, c0:c0 + FT].rearrange("h p t -> p h t"),
                        in_=obuf[nm].rearrange("p (h t) -> p h t", h=H))
                    S.inc(sn("sp"), i, 16)

        for j in range(NFT):
            stage1(j)
            if j >= 1:
                stage2(j - 1)
        stage2(NFT - 1)

    def emit_a1b(bscope):
        sm = lambda n: ("aa_" + n) if n not in ('bm',) else (sn(n))
        sn = lambda n: "ab_" + n
        wb_ = {nm: S.sb(sn(nm), [128, KD * DM], scope=bscope)
               for nm in ("wa", "wg")}
        wbb = S.sb(sn("wbb"), [128, KD * H], scope=bscope)
        xnf = [S.sb(sn("xnf%d" % i), [128, KD * FT], scope=bscope) for i in range(2)]
        obuf = {nm: S.sb(sn("ob" + nm), [128, H * FT], scope=bscope)
                for nm in ("a", "g")}
        bbuf = S.sb(sn("bbuf"), [H, FT], scope=bscope)
        svr = S.sb(sn("svr"), [1, FT], scope=bscope)
        nbar = S.barrier("ab")
        for _eng in ("sync", "tensor", "scalar", "gpsimd", "vector"):
            @P(_eng)
            def _(e, nbar=nbar):
                e.wait_ge(S.sem("pb"), nbar)

        @P("sync")
        def _(e):
            for nm, src in (("wa", wpi["wa"]), ("wg", wpi["wg"])):
                i = e.dma_start(
                    out=wb_[nm].rearrange("p (k o) -> p k o", k=KD),
                    in_=src.rearrange("(k p) o -> p k o", p=128))
                S.inc(sm("wl"), i, 16)
            i = e.dma_start(out=wbb.rearrange("p (k o) -> p k o", k=KD),
                            in_=wbi.rearrange("(k p) o -> p k o", p=128))
            S.inc(sm("wl"), i, 16)
        nwl = S.val(sm("wl"))

        def tile(j):
            c0 = j * FT
            pp = j % 2

            @P("sync")
            def _(e, j=j, c0=c0, pp=pp, nmm=S.val(sm("mm")), nld0=S.val(sm("ld"))):
                if j >= 2:
                    e.wait_ge(S.sem(sm("mm")), nmm)
                if j >= 1:
                    e.wait_ge(S.sem(sm("ld")), nld0)
                i = e.dma_start(
                    out=xnf[pp].rearrange("p (k t) -> p k t", k=KD),
                    in_=d_xn[:, :, c0:c0 + FT].rearrange("k p t -> p k t"))
                S.inc(sm("ld"), i, 16)
                if j >= 1:
                    e.wait_ge(S.sem(sm("mm")), S.val(sm("mm")))
                i = e.dma_start(out=svr, in_=vrowi[:, c0:c0 + FT])
                S.inc(sm("ld"), i, 16)
            nld = S.val(sm("ld"))

            basem = S.val(sm("mm"))
            basee = S.val(sm("ev"))

            @P("tensor")
            def _(t, j=j, pp=pp, nld=nld, basem=basem, basee=basee, nwl=nwl,
                  nbm0=S.val(sm("bm"))):
                t.wait_ge(S.sem(sm("ld")), nld)
                if j == 0:
                    t.wait_ge(S.sem(sm("wl")), nwl)
                nv = xnf[pp].rearrange("p (k t) -> p k t", k=KD)
                r = 0
                for nm in ("wa", "wg"):
                    wv = wb_[nm].rearrange("p (k o) -> p k o", k=KD)
                    for oc in range(KD):
                        if basee + r - 1 > 0:
                            t.wait_ge(S.sem(sm("ev")), basee + r - 1)
                        i = None
                        for fc in range(KD):
                            i = t.matmul(ps[r % 2][:, 0:FT],
                                         wv[:, fc, oc * 128:(oc + 1) * 128],
                                         nv[:, fc, :],
                                         start=(fc == 0), stop=(fc == KD - 1))
                        S.inc(sm("mm"), i)
                        r += 1
                if basee:
                    t.wait_ge(S.sem(sm("ev")), basee)
                wv = wbb.rearrange("p (k o) -> p k o", k=KD)
                i = None
                for fc in range(KD):
                    i = t.matmul(ps[2][0:H, 0:FT], wv[:, fc, :], nv[:, fc, :],
                                 start=(fc == 0), stop=(fc == KD - 1))
                S.inc(sm("mm"), i)
                if nbm0:
                    t.wait_ge(S.sem(sm("bm")), nbm0)
                i = t.matmul(ps[3][0:H, 0:FT], k_ones[:, 0:H], svr,
                             start=True, stop=True)
                S.inc(sm("mm"), i)

            @P("scalar")
            def _(sc, j=j, basem=basem, nsp=S.val(sm("sp"))):
                r = 0
                for nm, bias in (("a", k_ba), ("g", None)):
                    ov = obuf[nm].rearrange("p (h t) -> p h t", h=H)
                    for oc in range(KD):
                        sc.wait_ge(S.sem(sm("mm")), basem + r + 1)
                        if j >= 1 and r == 0:
                            sc.wait_ge(S.sem(sm("sp")), nsp)
                        if bias is not None:
                            i = sc.activation(ov[:, oc, :], ps[r % 2][:, 0:FT],
                                              ACT.Sigmoid,
                                              bias=bias[:, oc:oc + 1])
                        else:
                            i = sc.activation(ov[:, oc, :], ps[r % 2][:, 0:FT],
                                              ACT.Sigmoid)
                        S.inc(sm("ev"), i)
                        r += 1
                sc.wait_ge(S.sem(sm("mm")), basem + 18)
                i = sc.activation(bbuf, ps[2][0:H, 0:FT], ACT.Sigmoid,
                                  bias=k_bb[:, 0:1])
                S.inc(sm("ev"), i)

            @P("vector")
            def _(v, basee=basee):
                v.wait_ge(S.sem(sm("ev")), basee + 17)
                i = v.tensor_mul(bbuf, bbuf, ps[3][0:H, 0:FT])
                S.vb(v)
                S.inc(sm("bm"), v.engine_nop())

            @P("gpsimd")
            def _(g, j=j, c0=c0, basee=basee, nbm=S.val(sm("bm"))):
                for nm, dst, nev in (("a", d_a, basee + 8), ("g", d_g, basee + 16)):
                    g.wait_ge(S.sem(sm("ev")), nev)
                    i = g.dma_start(
                        out=dst[:, :, c0:c0 + FT].rearrange("h p t -> p h t"),
                        in_=obuf[nm].rearrange("p (h t) -> p h t", h=H))
                    S.inc(sm("sp"), i, 16)
                g.wait_ge(S.sem(sm("bm")), nbm)
                i = g.dma_start(out=d_beta[:, c0:c0 + FT], in_=bbuf)
                S.inc(sm("sp"), i, 16)

        for j in range(NFT):
            tile(j)

    def emit_scan(cscope):
        sn = lambda n: "sc_" + n
        NCH = nch
        NB = 4

        Sv = S.sb(sn("Sv"), [128, H * 128], scope=cscope)
        Sin = S.sb(sn("Sin"), [128, H * 128], scope=cscope)
        Sscr = S.sb(sn("Sscr"), [128, H * 128], scope=cscope)
        qa = [S.sb(sn("qa%d" % i), [128, 256], scope=cscope) for i in range(2)]
        ka = [S.sb(sn("ka%d" % i), [128, 256], scope=cscope) for i in range(2)]
        va = [S.sb(sn("va%d" % i), [128, 256], scope=cscope) for i in range(2)]
        aa = [S.sb(sn("aa%d" % i), [128, 256], scope=cscope) for i in range(2)]
        br = [S.sb(sn("br%d" % i), [1, 256], scope=cscope) for i in range(2)]
        Ab = S.sb(sn("A"), [128, 256], scope=cscope)
        Aip = S.sb(sn("Aip"), [128, 256], scope=cscope)
        rho = S.sb(sn("rho"), [128, 256], scope=cscope)
        Wb = S.sb(sn("W"), [128, 256], scope=cscope)
        Acc = S.sb(sn("Acc"), [128, 256], scope=cscope)
        tmp = S.sb(sn("tmp"), [128, 256], scope=cscope)
        sd0 = S.sb(sn("sd0"), [128, 256], scope=cscope)
        sd1 = S.sb(sn("sd1"), [128, 256], scope=cscope)
        ovb = S.sb(sn("ovb"), [128, 256], scope=cscope)
        Gs = S.sb(sn("Gs"), [CH, 256], scope=cscope)
        Kqs = S.sb(sn("Kqs"), [CH, 256], scope=cscope)
        KT = S.sb(sn("KT"), [CH, H * 128], scope=cscope)
        WT = S.sb(sn("WT"), [CH, H * 128], scope=cscope)
        mU = S.sb(sn("mU"), [CH, CH], scope=cscope)
        mUI = S.sb(sn("mUI"), [CH, CH], scope=cscope)
        nones = S.sb(sn("nones"), [1, 128], scope=cscope)

        nbar = S.barrier("sc")
        for _eng in ("sync", "tensor", "scalar", "gpsimd", "vector"):
            @P(_eng)
            def _(e, nbar=nbar):
                e.wait_ge(S.sem("pb"), nbar)

        @P("vector")
        def _(v):
            v.memset(Sv, 0.0)
            v.memset(Sin, 0.0)
            v.memset(nones, -1.0)
            v.tensor_scalar(mU, k_tri, -1.0, 1.0, ALU.mult, ALU.add)
            S.vb(v)
            v.tensor_add(mUI, mU, k_id32)
            S.vb(v)
            S.inc(sn("mz"), v.engine_nop())

        def one_pass(passid):
            if passid == "B" and not single:
                @P("vector")
                def _(v, nrd=S.val(sn("rd"))):
                    v.wait_ge(S.sem(sn("rd")), nrd)
                    v.tensor_scalar_mul(Sin, Sin, k_cfg[:, 0:1])
                    S.vb(v)
                    S.inc(sn("mz"), v.engine_nop())
            nmz = S.val(sn("mz"))

            @P("tensor")
            def _(t, nmz=nmz, nse0=S.val(sn("se")), ncc=S.val(sn("cc"))):
                t.wait_ge(S.sem(sn("mz")), nmz)
                if ncc:
                    t.wait_ge(S.sem(sn("cc")), ncc)
                for h in range(H):
                    i = t.transpose(ps[2 + h // 4][:, (h % 4) * 128:(h % 4) * 128 + 128],
                                    Sin[:, h * 128:(h + 1) * 128], k_id128)
                    S.inc(sn("svt"), i)

            @P("scalar")
            def _(sc, n=S.val(sn("svt")), nmz=nmz):
                sc.wait_ge(S.sem(sn("svt")), n)
                sc.wait_ge(S.sem(sn("mz")), nmz)
                for half in range(2):
                    i = sc.copy(Sv[:, half * 512:(half + 1) * 512],
                                ps[2 + half][:, 0:512])
                    S.inc(sn("se"), i)
            nsvi = S.val(sn("se"))

            ld_base = S.val(sn("ld"))

            def loads(c, gate):
                tc0 = 8 + 32 * c
                rc = c % 2

                @P("sync")
                def _(e, c=c, tc0=tc0, rc=rc, gate=gate, nld0=S.val(sn("ld"))):
                    for s_, v_ in gate:
                        e.wait_ge(S.sem(s_), v_)
                    if nld0:
                        e.wait_ge(S.sem(sn("ld")), nld0)
                    for buf, src in ((qa[rc], d_q), (ka[rc], d_k),
                                     (va[rc], d_v), (aa[rc], d_a)):
                        i = e.dma_start(
                            out=buf.rearrange("p (h t) -> p h t", h=H),
                            in_=src[:, :, tc0:tc0 + 32].rearrange("h p t -> p h t"))
                        S.inc(sn("ld"), i, 16)
                    i = e.dma_start(out=br[rc].rearrange("o (h t) -> o h t", h=H),
                                    in_=d_beta[:, tc0:tc0 + 32])
                    S.inc(sn("ld"), i, 16)

            def chunk(c):
                tc0 = 8 + 32 * c
                rc = c % 2
                stat = Sin if c == 0 else Sscr
                nld = ld_base + 80 * (c + 1)

                @P("tensor")
                def _(t, c=c, rc=rc, nld=nld, ngm=S.val(sn("gm")),
                      nse=(nsvi if c == 0 else S.val(sn("se"))),
                      nob=S.val(sn("ob")), nrh=S.val(sn("rh")),
                      ndel=S.val(sn("del"))):
                    t.wait_ge(S.sem(sn("ld")), nld)
                    if c >= 1:
                        t.wait_ge(S.sem(sn("gm")), ngm)
                    for h in range(H):
                        i = t.matmul(ps[1][0:CH, h * 32:(h + 1) * 32],
                                     ka[rc][:, h * 32:(h + 1) * 32],
                                     ka[rc][:, h * 32:(h + 1) * 32],
                                     start=True, stop=True)
                        S.inc(sn("gk"), i)
                        i = t.matmul(ps[1][0:CH, 256 + h * 32:256 + h * 32 + 32],
                                     ka[rc][:, h * 32:(h + 1) * 32],
                                     qa[rc][:, h * 32:(h + 1) * 32],
                                     start=True, stop=True)
                        S.inc(sn("gk"), i)
                    t.wait_ge(S.sem(sn("se")), nse)
                    for h in range(H):
                        i = t.transpose(ps[2 + h // 4][0:CH,
                                                       (h % 4) * 128:(h % 4) * 128 + 128],
                                        ka[rc][:, h * 32:(h + 1) * 32], k_id128)
                        S.inc(sn("kt"), i)
                    if c == 0:
                        t.wait_ge(S.sem(sn("se")), nsvi)
                        if nob:
                            t.wait_ge(S.sem(sn("ob")), nob)
                    elif nob:
                        t.wait_ge(S.sem(sn("ob")), nob)
                    for h in range(H):
                        i = t.matmul(ps[0][:, h * 64:h * 64 + 32],
                                     stat[:, h * 128:(h + 1) * 128],
                                     ka[rc][:, h * 32:(h + 1) * 32],
                                     start=True, stop=True)
                        S.inc(sn("dd"), i)
                    if c >= 1:
                        t.wait_ge(S.sem(sn("rh")), nrh)
                    i = t.matmul(ps[4][:, 0:256], nones, br[rc], start=True, stop=True)
                    S.inc(sn("bb"), i)

                @P("scalar")
                def _(sc, n=S.val(sn("kt")), ndel=S.val(sn("del"))):
                    sc.wait_ge(S.sem(sn("kt")), n)
                    if ndel:
                        sc.wait_ge(S.sem(sn("del")), ndel)
                    for half in range(2):
                        i = sc.copy(KT[:, half * 512:(half + 1) * 512],
                                    ps[2 + half][0:CH, 0:512])
                        S.inc(sn("ke"), i)

                @P("vector")
                def _(v, c=c, rc=rc, nld=nld, ngk=S.val(sn("gk")),
                      nrp=S.val(sn("rp")), nmx=S.val(sn("mx")),
                      noi=S.val(sn("oi")), ndd=S.val(sn("dd")),
                      nbb=S.val(sn("bb")), nmz=nmz):
                    v.wait_ge(S.sem(sn("gk")), ngk)
                    v.wait_ge(S.sem(sn("mz")), nmz)
                    if c >= 1:
                        v.wait_ge(S.sem(sn("rp")), nrp)
                        v.wait_ge(S.sem(sn("mx")), nmx)
                        v.wait_ge(S.sem(sn("oi")), noi)
                    mUr = bass.AP(mU.tensor, mU.offset, [list(mU.ap[0]), [0, H], [1, CH]])
                    mUIr = bass.AP(mUI.tensor, mUI.offset,
                                   [list(mUI.ap[0]), [0, H], [1, CH]])
                    v.tensor_tensor(Gs.rearrange("p (h t) -> p h t", h=H),
                                    ps[1][0:CH, 0:256].rearrange("p (h t) -> p h t", h=H),
                                    mUr, ALU.mult)
                    v.tensor_tensor(Kqs.rearrange("p (h t) -> p h t", h=H),
                                    ps[1][0:CH, 256:512].rearrange("p (h t) -> p h t", h=H),
                                    mUIr, ALU.mult)
                    S.vb(v)
                    S.inc(sn("gm"), v.engine_nop())
                    v.wait_ge(S.sem(sn("ld")), nld)
                    t0 = lambda b: bass.AP(b.tensor, b.offset,
                                           [list(b.ap[0]), [32, H], [1, 1]])
                    v.tensor_copy(sd0, aa[rc])
                    S.vb(v)
                    v.memset(t0(sd0), 0.0)
                    v.memset(sd1, 0.0)
                    S.vb(v)
                    v.tensor_copy(t0(sd1), t0(aa[rc]))
                    S.vb(v)
                    v.tensor_tensor_scan(Ab, sd0, sd1, 1.0, ALU.mult, ALU.add)
                    S.vb(v)
                    sh = lambda b, o, n: bass.AP(b.tensor, b.offset + o,
                                                 [list(b.ap[0]), [32, H], [1, n]])
                    v.tensor_copy(sh(tmp, 1, 31), sh(Ab, 0, 31))
                    v.memset(t0(tmp), 1.0)
                    S.vb(v)
                    v.reciprocal(Aip, tmp)
                    S.vb(v)
                    v.tensor_mul(sd1, tmp, Aip)
                    S.vb(v)
                    v.tensor_scalar(sd1, sd1, -1.0, 2.0, ALU.mult, ALU.add)
                    S.vb(v)
                    v.tensor_mul(Aip, Aip, sd1)
                    S.vb(v)
                    v.tensor_mul(Acc, va[rc], Aip)
                    S.vb(v)
                    v.wait_ge(S.sem(sn("dd")), ndd)
                    d0ap = bass.AP(ps[0].tensor, ps[0].offset,
                                   [list(ps[0].ap[0]), [64, H], [1, 32]])
                    v.tensor_tensor(Acc.rearrange("p (h t) -> p h t", h=H),
                                    Acc.rearrange("p (h t) -> p h t", h=H),
                                    d0ap, ALU.add)
                    S.vb(v)
                    v.reciprocal(sd0, aa[rc])
                    S.vb(v)
                    v.tensor_mul(sd1, aa[rc], sd0)
                    S.vb(v)
                    v.tensor_scalar(sd1, sd1, -1.0, 2.0, ALU.mult, ALU.add)
                    S.vb(v)
                    v.tensor_mul(sd0, sd0, sd1)
                    S.vb(v)
                    v.wait_ge(S.sem(sn("bb")), nbb)
                    v.tensor_mul(rho, sd0, ps[4][:, 0:256])
                    S.vb(v)
                    S.inc(sn("ac"), v.engine_nop())
                    S.inc(sn("rh"), v.engine_nop())

                nac = S.val(sn("ac"))

                @P("tensor")
                def _(t, nfr0=S.val(sn("fr")), ngm=S.val(sn("gm")),
                      nrh=S.val(sn("rh"))):
                    t.wait_ge(S.sem(sn("gm")), ngm)
                    t.wait_ge(S.sem(sn("rh")), nrh)
                    for st in range(CH - 1):
                        w = CH - 1 - st
                        if nfr0 + st > 0:
                            t.wait_ge(S.sem(sn("fr")), nfr0 + st)
                        sel = bass.AP(k_id32.tensor, k_id32.offset + st,
                                      [list(k_id32.ap[0]), [0, 128]])
                        mv = bass.AP(Gs.tensor, Gs.offset + st + 1,
                                     [list(Gs.ap[0]), [32, H], [1, w]])
                        i = t.matmul(ps[5][:, 0:H * w], sel, mv, start=True, stop=True)
                        S.inc(sn("rp"), i)

                @P("vector")
                def _(v, nrp0=S.val(sn("rp")) - (CH - 1), nac=nac):
                    v.wait_ge(S.sem(sn("ac")), nac)
                    for st in range(CH):
                        colW = bass.AP(Wb.tensor, Wb.offset + st,
                                       [list(Wb.ap[0]), [32, H], [1, 1]])
                        colR = bass.AP(rho.tensor, rho.offset + st,
                                       [list(rho.ap[0]), [32, H], [1, 1]])
                        colA = bass.AP(Acc.tensor, Acc.offset + st,
                                       [list(Acc.ap[0]), [32, H], [1, 1]])
                        v.tensor_tensor(colW, colR, colA, ALU.mult)
                        S.vb(v)
                        if st == CH - 1:
                            break
                        w = CH - 1 - st
                        v.wait_ge(S.sem(sn("rp")), nrp0 + st + 1)
                        wrep = bass.AP(Wb.tensor, Wb.offset + st,
                                       [list(Wb.ap[0]), [32, H], [0, w]])
                        v.tensor_tensor(tmp[:, 0:H * w], ps[5][:, 0:H * w], wrep,
                                        ALU.mult)
                        S.vb(v)
                        i = v.engine_nop()
                        S.inc(sn("fr"), i)
                        accw = bass.AP(Acc.tensor, Acc.offset + st + 1,
                                       [list(Acc.ap[0]), [32, H], [1, w]])
                        tmpw = bass.AP(tmp.tensor, tmp.offset,
                                       [list(tmp.ap[0]), [w, H], [1, w]])
                        v.tensor_tensor(accw, accw, tmpw, ALU.add)
                        S.vb(v)
                    i = v.engine_nop()
                    S.inc(sn("fs"), i)

                @P("tensor")
                def _(t, nfs=S.val(sn("fs")), nke=S.val(sn("ke"))):
                    t.wait_ge(S.sem(sn("fs")), nfs)
                    t.wait_ge(S.sem(sn("ke")), nke)
                    for h in range(H):
                        i = t.transpose(ps[2 + h // 4][0:CH,
                                                       (h % 4) * 128:(h % 4) * 128 + 128],
                                        Wb[:, h * 32:(h + 1) * 32], k_id128)
                        S.inc(sn("wt"), i)

                @P("scalar")
                def _(sc, n=S.val(sn("wt")), ndel=S.val(sn("del")),
                      noi=S.val(sn("oi"))):
                    sc.wait_ge(S.sem(sn("wt")), n)
                    if ndel:
                        sc.wait_ge(S.sem(sn("del")), ndel)
                        sc.wait_ge(S.sem(sn("oi")), noi)
                    for half in range(2):
                        i = sc.copy(WT[:, half * 512:(half + 1) * 512],
                                    ps[2 + half][0:CH, 0:512])
                        S.inc(sn("we"), i)

                @P("tensor")
                def _(t, c=c, rc=rc, n=S.val(sn("we")), nke=S.val(sn("ke")),
                      nac=nac):
                    t.wait_ge(S.sem(sn("we")), n)
                    t.wait_ge(S.sem(sn("ac")), nac)
                    stat2 = Sin if c == 0 else Sscr
                    for h in range(H):
                        t.matmul(ps[0][:, h * 64 + 32:h * 64 + 64],
                                 stat2[:, h * 128:(h + 1) * 128],
                                 qa[rc][:, h * 32:(h + 1) * 32],
                                 start=True, stop=False)
                        i = t.matmul(ps[0][:, h * 64 + 32:h * 64 + 64],
                                     WT[:, h * 128:(h + 1) * 128],
                                     Kqs[0:CH, h * 32:(h + 1) * 32],
                                     start=False, stop=True)
                        S.inc(sn("oi"), i)
                    t.wait_ge(S.sem(sn("ke")), nke)
                    for h in range(H):
                        i = t.matmul(ps[2 + h // 4][:, (h % 4) * 128:(h % 4) * 128 + 128],
                                     WT[:, h * 128:(h + 1) * 128],
                                     KT[0:CH, h * 128:(h + 1) * 128],
                                     start=True, stop=True)
                        S.inc(sn("del"), i)

                @P("vector")
                def _(v, c=c, noi=S.val(sn("oi")), ndel=S.val(sn("del")),
                      nos=S.val(sn("os")), nsvi=nsvi):
                    v.wait_ge(S.sem(sn("oi")), noi)
                    if nos:
                        v.wait_ge(S.sem(sn("os")), nos)
                    oap = bass.AP(ps[0].tensor, ps[0].offset + 32,
                                  [list(ps[0].ap[0]), [64, H], [1, 32]])
                    v.tensor_tensor(ovb.rearrange("p (h t) -> p h t", h=H),
                                    Ab.rearrange("p (h t) -> p h t", h=H),
                                    oap, ALU.mult)
                    S.vb(v)
                    i = v.engine_nop()
                    S.inc(sn("ob"), i)
                    v.wait_ge(S.sem(sn("del")), ndel)
                    if c == 0:
                        v.wait_ge(S.sem(sn("se")), nsvi)
                    dels = [bass.AP(ps[2 + half].tensor, ps[2 + half].offset,
                                    [list(ps[2 + half].ap[0]), [1, 512]])
                            for half in range(2)]
                    for half in range(2):
                        src0 = (Sin if c == 0 else Sv)[:, half * 512:(half + 1) * 512]
                        v.tensor_tensor(Sv[:, half * 512:(half + 1) * 512],
                                        src0, dels[half], ALU.add)
                        S.vb(v)
                    acrep = bass.AP(Ab.tensor, Ab.offset + 31,
                                   [list(Ab.ap[0]), [32, H], [0, 128]])
                    v.tensor_tensor(Sv.rearrange("p (h j) -> p h j", h=H),
                                    Sv.rearrange("p (h j) -> p h j", h=H),
                                    acrep, ALU.mult)
                    S.vb(v)
                    S.inc(sn("sv"), v.engine_nop())

                if c < NCH - 1:
                    @P("tensor")
                    def _(t, n=S.val(sn("sv"))):
                        t.wait_ge(S.sem(sn("sv")), n)
                        for h in range(H):
                            i = t.transpose(
                                ps[2 + h // 4][:, (h % 4) * 128:(h % 4) * 128 + 128],
                                Sv[:, h * 128:(h + 1) * 128], k_id128)
                            S.inc(sn("svt"), i)

                    @P("scalar")
                    def _(sc, n=S.val(sn("svt"))):
                        sc.wait_ge(S.sem(sn("svt")), n)
                        for half in range(2):
                            i = sc.copy(Sscr[:, half * 512:(half + 1) * 512],
                                        ps[2 + half][:, 0:512])
                            S.inc(sn("se"), i)

                if scan_dbg and c == 0:
                    @P("sync")
                    def _(e, nsv=S.val(sn("sv")), ngm=S.val(sn("gm")),
                          nfs=S.val(sn("fs"))):
                        e.wait_ge(S.sem(sn("sv")), nsv)
                        e.wait_ge(S.sem(sn("gm")), ngm)
                        e.wait_ge(S.sem(sn("fs")), nfs)
                        for sl, buf in enumerate((Ab, Aip, rho, Wb, Acc, ovb,
                                                  ka[0], aa[0], va[0], qa[0])):
                            i = e.dma_start(out=d_dbg[sl][:, 0:256], in_=buf)
                            S.inc(sn("dbg"), i, 16)
                        i = e.dma_start(out=d_dbg[10][0:CH, 0:256], in_=Gs)
                        S.inc(sn("dbg"), i, 16)
                        i = e.dma_start(out=d_dbg[11][0:CH, 0:256], in_=Kqs)
                        S.inc(sn("dbg"), i, 16)
                        i = e.dma_start(out=d_dbg[12], in_=Sv)
                        S.inc(sn("dbg"), i, 16)
                        e.wait_ge(S.sem(sn("dbg")), S.val(sn("dbg")))

                @P("sync")
                def _(e, c=c, n=S.val(sn("ob")), nos0=S.val(sn("os"))):
                    e.wait_ge(S.sem(sn("ob")), n)
                    if nos0:
                        e.wait_ge(S.sem(sn("os")), nos0)
                    i = e.dma_start(
                        out=d_o[:, :, 32 * c:32 * c + 32].rearrange("h p t -> p h t"),
                        in_=ovb.rearrange("p (h t) -> p h t", h=H))
                    S.inc(sn("os"), i, 16)

            gate0 = [(sn("dd"), S.val(sn("dd"))), (sn("kt"), S.val(sn("kt"))),
                     (sn("bb"), S.val(sn("bb"))), (sn("ac"), S.val(sn("ac")))]
            loads(0, gate0)
            loads(1, gate0)
            for c in range(NCH):
                chunk(c)
                if c + 2 < NCH:
                    loads(c + 2, [(sn("dd"), S.val(sn("dd"))),
                                  (sn("kt"), S.val(sn("kt"))),
                                  (sn("bb"), S.val(sn("bb"))),
                                  (sn("ac"), S.val(sn("ac")))])

        if not single:
            one_pass("A")

            @P("gpsimd")
            def _(g, nsv=S.val(sn("sv")), nos=S.val(sn("os"))):
                g.wait_ge(S.sem(sn("sv")), nsv)
                g.wait_ge(S.sem(sn("os")), nos)
                i = g.dma_start(out=d_cc1in, in_=Sv)
                S.inc(sn("cc"), i, 16)
                g.wait_ge(S.sem(sn("cc")), S.val(sn("cc")))
                i = g.collective_compute(
                    "AllGather", ALU.bypass,
                    replica_groups=[[0, 1], [2, 3], [4, 5], [6, 7]],
                    ins=[d_cc1in.opt()],
                    outs=[d_cc1out.opt()])
                S.inc(sn("ccg"), i)
                g.wait_ge(S.sem(sn("ccg")), S.val(sn("ccg")))
                i = g.dma_start(out=Sin, in_=d_cc1out[:, 0:H * 128])
                S.inc(sn("rd"), i, 16)
                g.wait_ge(S.sem(sn("rd")), S.val(sn("rd")))
        one_pass("B")

    def emit_p3(pscope3):
        sm = lambda n: ("aa_" + n) if n not in ('ote', 'ap', 'gt', 'ldr', 'ldt', 'tt', 'zz', 'zs', 'cq', 'xs', 'wl') else (sn(n))
        sn = lambda n: "p3_" + n
        NT3 = TOWN // FT
        wo = S.sb(sn("wo"), [128, KD * DM], scope=pscope3)
        ofm = S.sb(sn("ofm"), [128, H * FT], scope=pscope3)
        gfm = S.sb(sn("gfm"), [128, H * FT], scope=pscope3)
        zfm = S.sb(sn("zfm"), [128, H * FT], scope=pscope3)
        otok = S.sb(sn("otok"), [128, NSUB * H * 128], scope=pscope3)
        stats = S.sb(sn("stats"), [128, 64], scope=pscope3)
        srow = S.sb(sn("srow"), [1, 512], scope=pscope3)
        rbuf = S.sb(sn("rbuf"), [128, FT], scope=pscope3)
        xbuf = S.sb(sn("xbuf"), [128, FT], scope=pscope3)
        xtokf = S.sb(sn("xtokf"), [128, NSUB * DM], scope=pscope3)
        zb = S.sb(sn("zb"), [128, 2048], scope=pscope3)
        lngn = S.sb(sn("lngn"), [128, 1], scope=pscope3)
        nbar = S.barrier("p3")
        for _eng in ("sync", "tensor", "scalar", "gpsimd", "vector"):
            @P(_eng)
            def _(e, nbar=nbar):
                e.wait_ge(S.sem("pb"), nbar)

        @P("sync")
        def _(e):
            i = e.dma_start(out=wo.rearrange("p (k o) -> p k o", k=KD),
                            in_=wpi["wo"].rearrange("(k p) o -> p k o", p=128))
            S.inc(sm("wl"), i, 16)
        nwl = S.val(sm("wl"))

        @P("vector")
        def _(v):
            v.memset(zb, 0.0)
            v.tensor_scalar_mul(lngn, k_lng[:, 0:1], -1.0)
            S.vb(v)
            S.inc(sm("zz"), v.engine_nop())
        nzz = S.val(sm("zz"))

        @P("gpsimd")
        def _(g, nzz=nzz):
            g.wait_ge(S.sem(sm("zz")), nzz)
            zf = zb.rearrange("p (k t) -> p k t", k=KD)
            i = g.dma_start(out=d_xattn[:, :, 0:8].rearrange("k p t -> p k t"),
                            in_=zf[:, :, 0:8])
            S.inc(sm("zs"), i, 16)
            for q0 in range(OWN0 + TOWN, SLICE, 124):
                i = g.dma_start(
                    out=d_xattn[:, :, q0:q0 + 124].rearrange("k p t -> p k t"),
                    in_=zf[:, :, 0:124])
                S.inc(sm("zs"), i, 16)
            i = g.dma_start(out=d_xat[0:8, :], in_=zb[0:8, 0:DM])
            S.inc(sm("zs"), i, 16)
            i = g.dma_start(out=d_xat[OWN0 + TOWN:OWN0 + TOWN + 128, :],
                            in_=zb[:, 0:DM])
            S.inc(sm("zs"), i, 16)
            i = g.dma_start(out=d_xat[OWN0 + TOWN + 128:SLICE, :],
                            in_=zb[0:120, 0:DM])
            S.inc(sm("zs"), i, 16)

        def tile(j):
            c0 = j * FT
            cs = OWN0 + c0

            @P("sync")
            def _(e, j=j, c0=c0, cs=cs, nld0=S.val(sm("ld")),
                  ntq=S.val(sm("tq")), ngt=S.val(sm("gt"))):
                if j >= 1:
                    e.wait_ge(S.sem(sm("ld")), nld0)
                    e.wait_ge(S.sem(sm("tq")), ntq)
                    e.wait_ge(S.sem(sm("gt")), ngt)
                i = e.dma_start(out=ofm.rearrange("p (h t) -> p h t", h=H),
                                in_=d_o[:, :, c0:c0 + FT].rearrange("h p t -> p h t"))
                S.inc(sm("ld"), i, 16)
                i = e.dma_start(out=gfm.rearrange("p (h t) -> p h t", h=H),
                                in_=d_g[:, :, cs:cs + FT].rearrange("h p t -> p h t"))
                S.inc(sm("ld"), i, 16)
            nld = S.val(sm("ld"))

            baseT = S.val(sm("tq"))
            baseE = S.val(sm("ote"))

            @P("tensor")
            def _(t, nld=nld, baseE=baseE, nsq=S.val(sm("sq")),
                  nxte=S.val(sm("xte"))):
                t.wait_ge(S.sem(sm("ld")), nld)
                if nsq:
                    t.wait_ge(S.sem(sm("sq")), nsq)
                if nxte:
                    t.wait_ge(S.sem(sm("xte")), nxte)
                ov = ofm.rearrange("p (h t) -> p h t", h=H)
                k = 0
                for s_ in range(NSUB):
                    for h in range(H):
                        if baseE + k - 1 > 0:
                            t.wait_ge(S.sem(sm("ote")), baseE + k - 1)
                        i = t.transpose(ps[4 + (k % 2)][:, 0:128],
                                        ov[:, h, s_ * 128:(s_ + 1) * 128], k_id128)
                        S.inc(sm("tq"), i)
                        k += 1

            @P("scalar")
            def _(sc, baseT=baseT, j=j):
                ot = otok.rearrange("p (s h f) -> p s h f", s=NSUB, h=H)
                k = 0
                for s_ in range(NSUB):
                    for h in range(H):
                        sc.wait_ge(S.sem(sm("tq")), baseT + k + 1)
                        if k == 0 and j >= 1:
                            sc.wait_ge(S.sem(sm("st")), S.val(sm("st")))
                        i = sc.copy(ot[:, s_, h, :], ps[4 + (k % 2)][:, 0:128])
                        S.inc(sm("ote"), i)
                        k += 1
            note = S.val(sm("ote"))

            @P("scalar")
            def _(sc, n=note):
                sc.wait_ge(S.sem(sm("ote")), n)
                ot = otok.rearrange("p (s h f) -> p s h f", s=NSUB, h=H)
                k = 0
                for s_ in range(NSUB):
                    for h in range(H):
                        if k and k % 4 == 0:
                            sc.wait_ge(S.sem(sm("sq")), S.val(sm("sq")))
                        i = sc.activation(ps[6][:, (k % 4) * 128:(k % 4) * 128 + 128],
                                          ot[:, s_, h, :], ACT.Square,
                                          accum_out=stats[:, k:k + 1])
                        S.inc(sm("sq"), i)
                        k += 1
            nsq = S.val(sm("sq"))

            @P("vector")
            def _(v, n=note, nsq=nsq):
                v.wait_ge(S.sem(sm("ote")), n)
                ot = otok.rearrange("p (s h f) -> p s h f", s=NSUB, h=H)
                k = 0
                for s_ in range(NSUB):
                    for h in range(H):
                        v.reduce_sum(stats[:, 16 + k:17 + k], ot[:, s_, h, :], AX)
                        k += 1
                v.wait_ge(S.sem(sm("sq")), nsq)
                S.vb(v)
                v.tensor_scalar_mul(stats[:, 16:32], stats[:, 16:32], 1.0 / 128)
                v.tensor_scalar_mul(stats[:, 0:16], stats[:, 0:16], 1.0 / 128)
                S.vb(v)
                v.tensor_mul(stats[:, 32:48], stats[:, 16:32], stats[:, 16:32])
                S.vb(v)
                v.tensor_sub(stats[:, 0:16], stats[:, 0:16], stats[:, 32:48])
                S.vb(v)
                v.tensor_scalar_add(stats[:, 0:16], stats[:, 0:16], EPS)
                S.vb(v)
                v.reciprocal(stats[:, 32:48], stats[:, 0:16])
                S.vb(v)
                S.inc(sm("st"), v.engine_nop())

            @P("scalar")
            def _(sc, n=S.val(sm("st"))):
                sc.wait_ge(S.sem(sm("st")), n)
                i = sc.activation(stats[:, 48:64], stats[:, 32:48], ACT.Sqrt)
                S.inc(sm("sq"), i)

            @P("vector")
            def _(v, n=S.val(sm("sq"))):
                v.wait_ge(S.sem(sm("sq")), n)
                v.tensor_mul(stats[:, 32:48], stats[:, 48:64], stats[:, 48:64])
                S.vb(v)
                v.tensor_mul(stats[:, 32:48], stats[:, 32:48], stats[:, 0:16])
                S.vb(v)
                v.tensor_scalar(stats[:, 32:48], stats[:, 32:48], -1.0, 3.0,
                                ALU.mult, ALU.add)
                S.vb(v)
                v.tensor_mul(stats[:, 32:48], stats[:, 32:48], stats[:, 48:64])
                S.vb(v)
                v.tensor_scalar_mul(stats[:, 48:64], stats[:, 32:48], 0.5)
                S.vb(v)
                v.tensor_mul(stats[:, 32:48], stats[:, 16:32], stats[:, 48:64])
                S.vb(v)
                S.inc(sm("ct"), v.engine_nop())
            nct = S.val(sm("ct"))

            ntt0 = S.val(sm("tt"))
            for q in range(4):
                for rnd, col, bank in ((0, 48, 2), (1, 32, 3)):
                    @P("tensor")
                    def _(t, q=q, col=col, nct=nct, nrw=S.val(sm("rw"))):
                        t.wait_ge(S.sem(sm("ct")), nct)
                        if nrw:
                            t.wait_ge(S.sem(sm("rw")), nrw)
                        k = 0
                        for h in (2 * q, 2 * q + 1):
                            for s_ in range(NSUB):
                                i = t.transpose(
                                    ps[7][:1, k * 128:(k + 1) * 128],
                                    stats[:, col + s_ * 8 + h:
                                          col + s_ * 8 + h + 1],
                                    k_id128)
                                S.inc(sm("cq"), i)
                                k += 1

                    @P("scalar")
                    def _(sc, n=S.val(sm("cq")), ni1=S.val(sm("i1"))):
                        sc.wait_ge(S.sem(sm("cq")), n)
                        if ni1:
                            sc.wait_ge(S.sem(sm("i1")), ni1)
                        i = sc.copy(srow, ps[7][:1, 0:512])
                        S.inc(sm("rw"), i)

                    @P("tensor")
                    def _(t, bank=bank, n=S.val(sm("rw")), nap=S.val(sm("ap"))):
                        t.wait_ge(S.sem(sm("rw")), n)
                        if nap:
                            t.wait_ge(S.sem(sm("ap")), nap)
                        i = t.matmul(ps[bank][:, 0:512], k_ones, srow,
                                     start=True, stop=True)
                        S.inc(sm("i1"), i)

                @P("vector")
                def _(v, q=q, n=S.val(sm("i1")), nld=nld, nzz=nzz, ntt0=ntt0):
                    v.wait_ge(S.sem(sm("i1")), n)
                    v.wait_ge(S.sem(sm("ld")), nld)
                    v.wait_ge(S.sem(sm("zz")), nzz)
                    if q == 0 and ntt0:
                        v.wait_ge(S.sem(sm("tt")), ntt0)
                    sl = slice(q * 512, (q + 1) * 512)
                    v.tensor_tensor(zfm[:, sl], ofm[:, sl], ps[2][:, 0:512],
                                    ALU.mult)
                    S.vb(v)
                    v.tensor_tensor(zfm[:, sl], zfm[:, sl], ps[3][:, 0:512],
                                    ALU.subtract)
                    S.vb(v)
                    v.scalar_tensor_tensor(zfm[:, sl], zfm[:, sl], lngn,
                                           bfree(k_lnb[:, 0:1], 512),
                                           ALU.mult, ALU.add)
                    S.vb(v)
                    v.tensor_mul(zfm[:, sl], zfm[:, sl], gfm[:, sl])
                    S.vb(v)
                    i = v.engine_nop()
                    S.inc(sm("ap"), i)
                    if q == 3:
                        S.inc(sm("gt"), v.engine_nop())
            nap = S.val(sm("ap"))

            basem = S.val(sm("mm"))

            @P("tensor")
            def _(t, j=j, nap=nap, basem=basem, nwl=nwl, nrs=S.val(sm("res"))):
                t.wait_ge(S.sem(sm("ap")), nap)
                if j == 0:
                    t.wait_ge(S.sem(sm("wl")), nwl)
                wv = wo.rearrange("p (k o) -> p k o", k=KD)
                zv = zfm.rearrange("p (h t) -> p h t", h=H)
                for oc in range(KD):
                    if nrs + oc - 1 > 0:
                        t.wait_ge(S.sem(sm("res")), nrs + oc - 1)
                    i = None
                    for hc in range(KD):
                        i = t.matmul(ps[oc % 2][:, 0:FT],
                                     wv[:, hc, oc * 128:(oc + 1) * 128],
                                     zv[:, hc, :],
                                     start=(hc == 0), stop=(hc == KD - 1))
                    S.inc(sm("mm"), i)

            @P("sync")
            def _(e, j=j, cs=cs, nsp=S.val(sm("sp"))):
                for oc in range(KD):
                    if nsp + 16 * oc > 0:
                        e.wait_ge(S.sem(sm("sp")), nsp + 16 * oc)
                    i = e.dma_start(out=rbuf, in_=d_xmid[oc, :, cs:cs + FT])
                    S.inc(sm("ldr"), i, 16)

            @P("vector")
            def _(v, j=j, cs=cs, basem=basem, nldr=S.val(sm("ldr")) - 16 * KD,
                  nsp=S.val(sm("sp"))):
                for oc in range(KD):
                    v.wait_ge(S.sem(sm("mm")), basem + oc + 1)
                    v.wait_ge(S.sem(sm("ldr")), nldr + 16 * (oc + 1))
                    if nsp + 16 * oc > 0:
                        v.wait_ge(S.sem(sm("sp")), nsp + 16 * oc)
                    v.tensor_add(xbuf, ps[oc % 2][:, 0:FT], rbuf)
                    S.vb(v)
                    i = v.engine_nop()
                    S.inc(sm("res"), i)
            nres = S.val(sm("res"))

            @P("gpsimd")
            def _(g, j=j, cs=cs, nres0=nres - KD, nzs=S.val(sm("zs"))):
                if j == 0:
                    g.wait_ge(S.sem(sm("zs")), nzs)
                for oc in range(KD):
                    g.wait_ge(S.sem(sm("res")), nres0 + oc + 1)
                    i = g.dma_start(out=d_xattn[oc, :, cs:cs + FT], in_=xbuf)
                    S.inc(sm("sp"), i, 16)

            @P("sync")
            def _(e, j=j, cs=cs, nsp=S.val(sm("sp")), ntt=S.val(sm("tt"))):
                e.wait_ge(S.sem(sm("sp")), nsp)
                if ntt:
                    e.wait_ge(S.sem(sm("tt")), ntt)
                i = e.dma_start(
                    out=zfm.rearrange("p (k t) -> p k t", k=KD),
                    in_=d_xattn[:, :, cs:cs + FT].rearrange("k p t -> p k t"))
                S.inc(sm("ldt"), i, 16)

            @P("tensor")
            def _(t, n=S.val(sm("ldt")), baseE2=S.val(sm("xte"))):
                t.wait_ge(S.sem(sm("ldt")), n)
                xv = zfm.rearrange("p (k t) -> p k t", k=KD)
                k = 0
                for s_ in range(NSUB):
                    for kc in range(KD):
                        if baseE2 + k - 1 > 0:
                            t.wait_ge(S.sem(sm("xte")), baseE2 + k - 1)
                        i = t.transpose(ps[4 + (k % 2)][:, 0:128],
                                        xv[:, kc, s_ * 128:(s_ + 1) * 128],
                                        k_id128)
                        S.inc(sm("tt"), i)
                        k += 1

            @P("scalar")
            def _(sc, baseT2=S.val(sm("tt")) - 16, nxs=S.val(sm("xs"))):
                xs2 = xtokf.rearrange("p (s f) -> p s f", s=NSUB)
                k = 0
                for s_ in range(NSUB):
                    for kc in range(KD):
                        sc.wait_ge(S.sem(sm("tt")), baseT2 + k + 1)
                        if k == 0 and nxs:
                            sc.wait_ge(S.sem(sm("xs")), nxs)
                        i = sc.copy(xs2[:, s_, kc * 128:(kc + 1) * 128],
                                    ps[4 + (k % 2)][:, 0:128])
                        S.inc(sm("xte"), i)
                        k += 1

            @P("gpsimd")
            def _(g, j=j, cs=cs, n=S.val(sm("xte"))):
                g.wait_ge(S.sem(sm("xte")), n)
                i = g.dma_start(
                    out=d_xat[cs:cs + FT, :].rearrange("(s p) f -> p s f", p=128),
                    in_=xtokf.rearrange("p (s f) -> p s f", s=NSUB))
                S.inc(sm("xs"), i, 16)

        for j in range(NT3):
            tile(j)

    def emit_cc2(semBD0, semBD1):
        sn = lambda n: "c2_" + n
        bnd = S.sb(sn("bnd"), [128, 2 * KD])

        @P("gpsimd")
        def _(g, nsp=S.val("aa_sp"), nxs=S.val("p3_xs")):
            g.wait_ge(S.sem("aa_sp"), nsp)
            g.wait_ge(S.sem("p3_xs"), nxs)
            i = g.dma_start(out=bnd[:, 0:KD],
                            in_=d_xattn[:, :, OWN0].rearrange("k p -> p k"),
                            allow_slow_non_contiguous=True)
            S.inc(sn("ld"), i, 16)
            i = g.dma_start(out=bnd[:, KD:2 * KD],
                            in_=d_xattn[:, :, OWN0 + TOWN - 1].rearrange("k p -> p k"),
                            allow_slow_non_contiguous=True)
            S.inc(sn("ld"), i, 16)
            g.wait_ge(S.sem(sn("ld")), S.val(sn("ld")))
            i = g.dma_start(out=d_cc2in, in_=bnd)
            S.inc(sn("cin"), i, 16)
            g.wait_ge(S.sem(sn("cin")), S.val(sn("cin")))
            if not single:
                i = g.collective_compute(
                    "AllGather", ALU.bypass,
                    replica_groups=[[0, 1], [2, 3], [4, 5], [6, 7]],
                    ins=[d_cc2in.opt()],
                    outs=[d_cc2out.opt()])
                S.inc(sn("ccg"), i)
                g.wait_ge(S.sem(sn("ccg")), S.val(sn("ccg")))
            else:
                i = g.dma_start(out=d_cc2out[:, 0:2 * KD], in_=d_cc2in)
                S.inc(sn("cp"), i, 16)
                i = g.dma_start(out=d_cc2out[:, 2 * KD:4 * KD], in_=d_cc2in)
                S.inc(sn("cp"), i, 16)
                g.wait_ge(S.sem(sn("cp")), S.val(sn("cp")))
            i = g.dma_start(
                out=d_xat[7:8, :].rearrange("r (k p) -> (r k) p", k=KD),
                in_=d_cc2out[:, KD:2 * KD].rearrange("p k -> k p"),
                allow_slow_non_contiguous=True)
            S.inc(sn("pa"), i, 16)
            i = g.dma_start(
                out=d_xat[OWN0 + TOWN:OWN0 + TOWN + 1, :].rearrange(
                    "r (k p) -> (r k) p", k=KD),
                in_=d_cc2out[:, 2 * KD:3 * KD].rearrange("p k -> k p"),
                allow_slow_non_contiguous=True)
            S.inc(sn("pb"), i, 16)
            g.wait_ge(S.sem(sn("pa")), S.val(sn("pa")))
            g.wait_ge(S.sem(sn("pb")), S.val(sn("pb")))
            g.engine_nop().then_inc(semBD0, 1)
            g.engine_nop().then_inc(semBD1, 1)

    def emit_debug_copy(src_dram, waitsem, waitval):
        @P("sync")
        def _(e):
            e.wait_ge(S.sem(waitsem), waitval)
            i = e.dma_start(out=out,
                            in_=src_dram[:, :, OWN0:OWN0 + TOWN])
            S.inc("z_out", i, 16)

        @P("gpsimd")
        def _(g):
            g.wait_ge(S.sem("z_out"), S.val("z_out"))

    wscope = ExitStack()
    if "pre" in run_phases:
        w1q, w2q = emit_pre(wscope)
    if "f1" in run_phases:
        with ExitStack() as fscope:
            emit_fphase("f1", x0t, x0f, d_xmid, w1q, w2q, fscope)
    wscope.close()
    if "a1a" in run_phases:
        with ExitStack() as ascope:
            emit_a1a(ascope)
    if "a1b" in run_phases:
        with ExitStack() as bscope:
            emit_a1b(bscope)
    if "scan" in run_phases:
        with ExitStack() as cscope:
            emit_scan(cscope)
    if "p3" in run_phases:
        with ExitStack() as pscope3:
            emit_p3(pscope3)
    if "f2" in run_phases:
        semBD0 = S.ctx.enter_context(nc.semaphore("bd0"))
        semBD1 = S.ctx.enter_context(nc.semaphore("bd1"))
        nbarw = S.barrier("f2w")
        for _eng in ("sync", "tensor", "scalar", "gpsimd", "vector"):
            @P(_eng)
            def _(e, nbarw=nbarw):
                e.wait_ge(S.sem("pb"), nbarw)
        emit_cc2(semBD0, semBD1)
        w1q2 = S.sb("f2w1q", [128, KD * DH], BF16)
        w2q2 = S.sb("f2w2q", [128, HD * DM], BF16)

        @P("sync")
        def _(e, npo=S.val("p_out")):
            e.wait_ge(S.sem("p_out"), npo)
            i = e.dma_start(out=w1q2, in_=d_w1q)
            S.inc("f2wl", i, 16)
            i = e.dma_start(out=w2q2, in_=d_w2q)
            S.inc("f2wl", i, 16)
        with ExitStack() as fscope2:
                emit_fphase("f2", d_xat, d_xattn, d_xfin, w1q2, w2q2, fscope2,
                        wl=("f2wl", 32),
                        patch=dict(bd0=semBD0, bd1=semBD1))
        if debug_out is None:
            emit_debug_copy(d_xfin, "f_sp", S.val("f_sp"))

    if debug_out == "xmid":
        emit_debug_copy(d_xmid, "f_sp", S.val("f_sp"))
    elif debug_out == "xn":
        emit_debug_copy(d_xn, "aa_xsp", S.val("aa_xsp"))
    elif debug_out in ("q", "k", "v"):
        emit_debug_copy({"q": d_q, "k": d_k, "v": d_v}[debug_out],
                        "aa_sp", S.val("aa_sp"))
    elif debug_out in ("a", "g"):
        emit_debug_copy({"a": d_a, "g": d_g}[debug_out],
                        "aa_sp", S.val("aa_sp"))
    elif debug_out == "o":
        @P("sync")
        def _(e, n=S.val("sc_os")):
            e.wait_ge(S.sem("sc_os"), n)
            i = e.dma_start(out=out[:, :, 0:32 * nch],
                            in_=d_o[:, :, 0:32 * nch])
            S.inc("z_out", i, 16)

        @P("gpsimd")
        def _(g):
            g.wait_ge(S.sem("z_out"), S.val("z_out"))
    elif debug_out == "beta":
        @P("sync")
        def _(e, n=S.val("aa_sp")):
            e.wait_ge(S.sem("aa_sp"), n)
            i = e.dma_start(out=out[0][0:H, :],
                            in_=d_beta[:, OWN0:OWN0 + TOWN])
            S.inc("z_out", i, 16)

        @P("gpsimd")
        def _(g):
            g.wait_ge(S.sem("z_out"), S.val("z_out"))

    S.ctx.close()
    return nc


def _prep_inputs(inputs):
    x = np.asarray(inputs["L0_attn_x"])
    maps = []
    ident = np.eye(CH, dtype=np.float32)
    tri = np.tril(np.ones((CH, CH), np.float32))
    id128 = np.eye(128, dtype=np.float32)
    w1 = np.ascontiguousarray(inputs["L0_ffn_w1"], np.float32)
    w2 = np.ascontiguousarray(inputs["L0_ffn_w2"], np.float32)
    cw = np.ascontiguousarray(inputs["L0_ffn_conv_w"], np.float32).reshape(DH, 3)
    cwl = np.ascontiguousarray(cw.reshape(HD, 128, 3).transpose(1, 0, 2))
    cbl = np.ascontiguousarray(np.asarray(inputs["L0_ffn_conv_b"], np.float32)
                               .reshape(HD, 128).T)
    wmats = {n: np.ascontiguousarray(inputs["L0_attn_W" + n[-1]], np.float32)
             for n in ("wq", "wk", "wv", "wa", "wg", "wo")}
    wb = np.ascontiguousarray(inputs["L0_attn_Wb"], np.float32)
    ba = np.ascontiguousarray(np.asarray(inputs["L0_attn_ba"], np.float32)
                              .reshape(KD, 128).T)
    bb = np.ascontiguousarray(inputs["L0_attn_bb"], np.float32).reshape(H, 1)
    conv = {}
    for nm, wkey, bkey in (("cqw", "L0_attn_cqw", "L0_attn_cqb"),
                           ("ckw", "L0_attn_ckw", "L0_attn_ckb"),
                           ("cvw", "L0_attn_cvw", "L0_attn_cvb")):
        conv[nm] = np.ascontiguousarray(np.asarray(inputs[wkey], np.float32)
                                        .reshape(KD, 128, 4).transpose(1, 0, 2))
        conv[nm[:2] + "b"] = np.ascontiguousarray(np.asarray(inputs[bkey], np.float32)
                                                  .reshape(KD, 128).T)
    lng = np.ascontiguousarray(inputs["L0_attn_ln_g"], np.float32).reshape(128, 1)
    lnb = np.ascontiguousarray(inputs["L0_attn_ln_b"], np.float32).reshape(128, 1)

    for c in range(NCORE):
        b = c // 2
        half = c % 2
        g0 = half * TOWN - HALO
        sl = np.zeros((SLICE, DM), np.float32)
        lo = max(0, g0)
        hi = min(T, g0 + SLICE)
        sl[lo - g0:hi - g0] = x[b, lo:hi]
        x0t_ = np.ascontiguousarray(sl)
        x0f_ = np.ascontiguousarray(sl.T.reshape(KD, 128, SLICE))
        valid = np.zeros(SLICE, np.float32)
        valid[lo - g0:hi - g0] = 1.0
        vtok_ = np.ascontiguousarray(valid.reshape(SLICE // 128, 128).T)
        vrow_ = valid.reshape(1, SLICE).copy()
        cfg_ = np.zeros((128, 8), np.float32)
        cfg_[:, 0] = half
        m = dict(x0f=x0f_, x0t=x0t_, vtok=vtok_, vrow=vrow_, cfg=cfg_, ident=ident, tri=tri,
                 id128=id128, w1=w1, w2=w2, cw=cwl, cb=cbl,
                 wb=wb, ba=ba, bb=bb,
                 cqw=conv["cqw"], cqb=conv["cqb"], ckw=conv["ckw"], ckb=conv["ckb"],
                 cvw=conv["cvw"], cvb=conv["cvb"], lng=lng, lnb=lnb)
        for n in ("wq", "wk", "wv", "wa", "wg", "wo"):
            m[n] = wmats[n]
        maps.append(m)
    return maps


_NC_CACHE = {}


def _get_nc(key, **kw):
    if key not in _NC_CACHE:
        _NC_CACHE[key] = build(**kw)
    return _NC_CACHE[key]


def kernel(**inputs):
    maps = _prep_inputs(inputs)
    nc = _get_nc("full")
    res = run_bass_kernel_spmd(nc, maps, core_ids=list(range(NCORE)))
    full = np.zeros((B, T, DM), np.float32)
    for c in range(NCORE):
        b, half = c // 2, c % 2
        o = np.asarray(res.results[c]["out"])
        full[b, half * TOWN:(half + 1) * TOWN] = o.reshape(DM, TOWN).T
    return full


# version 03
# speedup vs baseline: 1.0381x; 1.0381x over previous
import numpy as np
import concourse.bass as bass
import concourse.mybir as mybir
from concourse.bass_utils import run_bass_kernel_spmd

F32 = mybir.dt.float32
BF16 = mybir.dt.bfloat16
ALU = mybir.AluOpType
ACT = mybir.ActivationFunctionType
AX = mybir.AxisListType.X

B, T, DM, DH = 4, 8192, 1024, 4096
H, D = 8, 128
NCORE = 8
TOWN = T // 2
HALO = 8
FT = 256
NFT = 17
SLICE = NFT * FT
OWN0 = HALO
CH = 32
NCHUNK = TOWN // CH
KD = DM // 128
HD = DH // 128
NSUB = FT // 128
RSQRT_D = float(np.float32(D ** (-0.5)))
C32V = 1.5 * 2.0 ** 23
C64V = 1.5 * 2.0 ** 52
EPS = 1e-5


def bfree(col_ap, n):
    return bass.AP(col_ap.tensor, col_ap.offset, list(col_ap.ap[:1]) + [[0, n]])


class Prog:
    def __init__(self):
        from contextlib import ExitStack
        self.ctx = ExitStack()
        self.nc = bass.Bass("TRN2", target_bir_lowering=False, debug=False)
        self.blocks = []
        self.sem_h = {}
        self.sem_v = {}

    def sem(self, name):
        if name not in self.sem_h:
            self.sem_h[name] = self.ctx.enter_context(self.nc.semaphore(name))
            self.sem_v[name] = 0
        return self.sem_h[name]

    def inc(self, name, instr=None, n=1):
        self.sem(name)
        self.sem_v[name] += n
        if instr is not None:
            instr.then_inc(self.sem_h[name], n)
        return self.sem_v[name]

    def val(self, name):
        self.sem(name)
        return self.sem_v[name]

    def barrier(self, tag):
        P = self.phase
        snap = [(n, v) for n, v in sorted(self.sem_v.items())
                if v > 0 and n != "pb" and not n.startswith("c2_")]

        @P("vector")
        def _(v, snap=snap):
            for n, val in snap:
                v.wait_ge(self.sem_h[n], val)
            self.inc("pb", v.engine_nop())
        return self.val("pb")

    def vb(self, eng, name="vch"):
        mknop = getattr(eng, "engine_nop", eng.nop)
        n = self.inc(name, mknop())
        eng.wait_ge(self.sem(name), n)

    def phase(self, engine):
        def reg(fn):
            getattr(self.block, engine)(fn)
            return fn
        return reg

    def sb(self, name, shape, dt=F32, scope=None):
        t = (scope or self.ctx).enter_context(self.nc.sbuf_tensor(name, shape, dt))
        return t[:]

    def psum(self, name, shape, scope=None):
        t = (scope or self.ctx).enter_context(self.nc.psum_tensor(name, shape, F32))
        return t[:]


def build(run_phases=("pre", "f1", "a1a", "a1b", "scan", "p3", "f2"), debug_out=None,
          single=False, nch=128, scan_dbg=False):
    from contextlib import ExitStack
    S = Prog()
    nc = S.nc
    P = S.phase

    def din(name, shape, dt=F32):
        return nc.dram_tensor(name, shape, dt, kind="ExternalInput").ap()

    def dscr(name, shape, dt=F32):
        return nc.dram_tensor(name, shape, dt).ap()

    x0f = din("x0f", [KD, 128, SLICE])
    x0t = din("x0t", [SLICE, DM])
    vrowi = din("vrow", [1, SLICE])
    vtok = din("vtok", [128, SLICE // 128])
    cfgi = din("cfg", [128, 8])
    identi = din("ident", [CH, CH])
    trii = din("tri", [CH, CH])
    id128i = din("id128", [128, 128])
    w1i = din("w1", [DH, DM])
    w2i = din("w2", [DM, DH])
    cwi = din("cw", [128, HD, 3])
    cbi = din("cb", [128, HD])
    wpi = {n: din(n, [DM, DM]) for n in ("wq", "wk", "wv", "wa", "wg", "wo")}
    wbi = din("wb", [DM, H])
    bai = din("ba", [128, KD])
    bbi = din("bb", [H, 1])
    convi = {"q": (din("cqw", [128, KD, 4]), din("cqb", [128, KD])),
             "k": (din("ckw", [128, KD, 4]), din("ckb", [128, KD])),
             "v": (din("cvw", [128, KD, 4]), din("cvb", [128, KD]))}
    lngi = din("lng", [128, 1])
    lnbi = din("lnb", [128, 1])
    out = nc.dram_tensor("out", [KD, 128, TOWN], F32, kind="ExternalOutput").ap()

    d_w1q = dscr("d_w1q", [128, KD * DH], BF16)
    d_w2q = dscr("d_w2q", [128, HD * DM], BF16)
    d_xmid = dscr("d_xmid", [KD, 128, SLICE])
    d_xn = dscr("d_xn", [KD, 128, SLICE])
    d_q = dscr("d_q", [H, 128, SLICE])
    d_k = dscr("d_k", [H, 128, SLICE])
    d_v = dscr("d_v", [H, 128, SLICE])
    d_a = dscr("d_a", [H, 128, SLICE])
    d_g = dscr("d_g", [H, 128, SLICE])
    d_beta = dscr("d_beta", [H, SLICE])
    d_o = dscr("d_o", [H, 128, TOWN])
    d_xattn = dscr("d_xattn", [KD, 128, SLICE])
    d_xat = dscr("d_xat", [SLICE, DM])
    d_cc1in = dscr("d_cc1in", [128, H * 128])
    d_cc1out = dscr("d_cc1out", [128, 2 * H * 128])
    d_cc2in = dscr("d_cc2in", [128, 2 * KD])
    d_cc2out = dscr("d_cc2out", [128, 4 * KD])
    d_xfin = dscr("d_xfin", [KD, 128, SLICE])
    d_dbg = dscr("d_dbg", [16, 128, 1024])

    S.block = S.ctx.enter_context(nc.Block())

    k_ones = S.sb("k_ones", [1, 128])
    k_onec = S.sb("k_onec", [128, 2])
    k_id32 = S.sb("k_id32", [CH, CH])
    k_tri = S.sb("k_tri", [CH, CH])
    k_id128 = S.sb("k_id128", [128, 128])
    k_cfg = S.sb("k_cfg", [128, 8])
    k_cr = S.sb("k_cr", [128, 2])
    k_vtok = S.sb("k_vtok", [128, SLICE // 128])
    k_cw = S.sb("k_cw", [128, 3 * HD])
    k_cb = S.sb("k_cb", [128, HD])
    k_aconv = {nm: (S.sb("k_c%sw" % nm, [128, 4 * KD]), S.sb("k_c%sb" % nm, [128, KD]))
               for nm in ("q", "k", "v")}
    k_ba = S.sb("k_ba", [128, KD])
    k_bb = S.sb("k_bb", [H, 2])
    k_lng = S.sb("k_lng", [128, 2])
    k_lnb = S.sb("k_lnb", [128, 2])
    k_sw = S.sb("k_sw", [128, 4])

    ps = [S.psum(f"ps{i}", [128, 512]) for i in range(8)]

    def emit_pre(wscope):
        w1q = S.sb("w1q", [128, KD * DH], BF16, scope=wscope)
        w2q = S.sb("w2q", [128, HD * DM], BF16, scope=wscope)

        @P("sync")
        def _(e):
            def ld(dst, src):
                S.inc("c_ld", e.dma_start(out=dst, in_=src), 16)
            ld(k_id32, identi)
            ld(k_tri, trii)
            ld(k_id128, id128i)
            ld(k_cfg, cfgi)
            ld(k_vtok, vtok)
            ld(k_cw, cwi.rearrange("p k c -> p (k c)"))
            ld(k_cb, cbi)
            for nm in ("q", "k", "v"):
                ld(k_aconv[nm][0], convi[nm][0].rearrange("p k c -> p (k c)"))
                ld(k_aconv[nm][1], convi[nm][1])
            ld(k_ba, bai)
            ld(k_bb[:, 0:1], bbi)
            ld(k_lng[:, 0:1], lngi)
            ld(k_lnb[:, 0:1], lnbi)
        nld = S.val("c_ld")

        @P("vector")
        def _(v):
            v.wait_ge(S.sem("c_ld"), nld)
            v.memset(k_ones, 1.0)
            v.memset(k_onec[:, 0:1], 1.0)
            v.memset(k_cr[:, 1:2], 1.0)
            S.vb(v)
            v.tensor_scalar_add(k_cr[:, 1:2], k_cr[:, 1:2], 2.0 ** -30)
            S.vb(v)
            v.tensor_scalar_add(k_cr[:, 1:2], k_cr[:, 1:2], -1.0)
            S.vb(v)
            v.tensor_scalar(k_cr[:, 0:1], k_cr[:, 1:2], 2.0 ** 30, C64V - C32V,
                            ALU.mult, ALU.mult)
            S.vb(v)
            v.tensor_scalar_add(k_cr[:, 0:1], k_cr[:, 0:1], C32V)
            S.vb(v)
            S.inc("c_pre", v.engine_nop())
        npre = S.val("c_pre")

        with ExitStack() as pscope:
            wbuf = S.sb("wbuf", [128, DH], scope=pscope)
            wacc = S.sb("wacc", [128, 36], scope=pscope)
            for wi, (wsrc, wres, dsp, rows, cols) in enumerate(
                    [(w1i, w1q, d_w1q, DH, DM), (w2i, w2q, d_w2q, DM, DH)]):
                nblk = rows // 128
                for bk in range(nblk):
                    need = S.val("p_dv")
                    needmm = S.val("p_mm")

                    @P("sync")
                    def _(e, bk=bk, need=need, needmm=needmm, wsrc=wsrc, cols=cols):
                        if need:
                            e.wait_ge(S.sem("p_dv"), need)
                        if needmm:
                            e.wait_ge(S.sem("p_mm"), needmm)
                        S.inc("p_ld", e.dma_start(
                            out=wbuf[:, :cols],
                            in_=wsrc[bk * 128:(bk + 1) * 128, :]), 16)

                    @P("vector")
                    def _(v, bk=bk, n=S.val("p_ld"), cols=cols):
                        v.wait_ge(S.sem("p_ld"), n)
                        i = v.reduce_sum(wacc[:, bk:bk + 1], wbuf[:, :cols], AX,
                                         apply_absolute_value=True)
                        S.inc("p_dv", i)

                @P("vector")
                def _(v, nblk=nblk, n=S.val("p_dv")):
                    v.wait_ge(S.sem("p_dv"), n)
                    v.reduce_sum(wacc[:, 33:34], wacc[:, 0:nblk], AX)
                    S.vb(v)
                    S.inc("p_dv", v.engine_nop())

                @P("tensor")
                def _(t, n=S.val("p_dv")):
                    t.wait_ge(S.sem("p_dv"), n)
                    t.wait_ge(S.sem("c_pre"), npre)
                    i = t.matmul(ps[0][:1, 0:1], k_onec[:, 0:1], wacc[:, 33:34],
                                 start=True, stop=True)
                    S.inc("p_mm", i)

                @P("vector")
                def _(v, n=S.val("p_mm"), denom=1.0 / (rows * cols)):
                    v.wait_ge(S.sem("p_mm"), n)
                    v.tensor_scalar_mul(wacc[:1, 34:35], ps[0][:1, 0:1], denom)
                    S.vb(v)
                    v.tensor_scalar_max(wacc[:1, 34:35], wacc[:1, 34:35], 1e-5)
                    S.vb(v)
                    v.reciprocal(wacc[:1, 35:36], wacc[:1, 34:35])
                    S.vb(v)
                    S.inc("p_dv", v.engine_nop())

                @P("tensor")
                def _(t, n=S.val("p_dv")):
                    t.wait_ge(S.sem("p_dv"), n)
                    i = t.matmul(ps[1][:, 0:2], k_ones, wacc[:1, 34:36],
                                 start=True, stop=True)
                    S.inc("p_mm", i)

                @P("vector")
                def _(v, n=S.val("p_mm"), wi=wi):
                    v.wait_ge(S.sem("p_mm"), n)
                    v.tensor_copy(k_sw[:, 2 * wi:2 * wi + 1], ps[1][:, 1:2])
                    v.tensor_copy(k_sw[:, 2 * wi + 1:2 * wi + 2], ps[1][:, 0:1])
                    S.vb(v)
                    S.inc("p_dv", v.engine_nop())

                for bk in range(nblk):
                    need = S.val("p_dv")
                    needsc = S.val("p_sc")
                    needmm = S.val("p_mm")

                    @P("sync")
                    def _(e, bk=bk, need=need, needsc=needsc, needmm=needmm,
                          wsrc=wsrc, cols=cols):
                        e.wait_ge(S.sem("p_dv"), need)
                        if needsc:
                            e.wait_ge(S.sem("p_sc"), needsc)
                        if needmm:
                            e.wait_ge(S.sem("p_mm"), needmm)
                        S.inc("p_ld", e.dma_start(
                            out=wbuf[:, :cols],
                            in_=wsrc[bk * 128:(bk + 1) * 128, :]), 16)

                    @P("vector")
                    def _(v, n=S.val("p_ld"), wi=wi, cols=cols):
                        v.wait_ge(S.sem("p_ld"), n)
                        v.tensor_scalar_mul(wbuf[:, :cols], wbuf[:, :cols],
                                            k_sw[:, 2 * wi:2 * wi + 1])
                        S.vb(v)
                        v.tensor_scalar(wbuf[:, :cols], wbuf[:, :cols],
                                        k_cr[:, 0:1], k_cr[:, 0:1],
                                        ALU.add, ALU.subtract)
                        S.vb(v)
                        v.tensor_scalar(wbuf[:, :cols], wbuf[:, :cols], -1.0, 1.0,
                                        ALU.max, ALU.min)
                        S.vb(v)
                        S.inc("p_dv", v.engine_nop())

                    nkc = cols // 128
                    base = S.val("p_mm")

                    @P("tensor")
                    def _(t, n=S.val("p_dv"), nkc=nkc, basc=S.val("p_sc")):
                        t.wait_ge(S.sem("p_dv"), n)
                        for kc in range(nkc):
                            if basc + kc - 1 > 0:
                                t.wait_ge(S.sem("p_sc"), basc + kc - 1)
                            i = t.transpose(ps[2 + (kc % 2)][:, 0:128],
                                            wbuf[:, kc * 128:(kc + 1) * 128], k_id128)
                            S.inc("p_mm", i)

                    @P("scalar")
                    def _(sc, base=base, nkc=nkc, bk=bk, wres=wres, rows=rows):
                        wv = wres.rearrange("p (k r) -> p k r", r=rows)
                        for kc in range(nkc):
                            sc.wait_ge(S.sem("p_mm"), base + kc + 1)
                            i = sc.copy(wv[:, kc, bk * 128:(bk + 1) * 128],
                                        ps[2 + (kc % 2)][:, 0:128])
                            S.inc("p_sc", i)

                @P("gpsimd")
                def _(g, n=S.val("p_sc"), wres=wres, dsp=dsp):
                    g.wait_ge(S.sem("p_sc"), n)
                    S.inc("p_out", g.dma_start(out=dsp, in_=wres), 16)
        return w1q, w2q

    def emit_fphase(tag, src_tok, src_fm, dst_fm, w1q, w2q, fscope,
                    wl=None, patch=None):
        _FB = {}
        """dst_fm = src_fm + 0.5*FFN(LN(src)). 1-tile stage skew."""
        sm = lambda n: ("f_" + n) if n not in () else (sn(n))
        sn = lambda n: tag + "_" + n
        _FB["s1r"] = S.val(sm("s1r"))
        _FB["mm1"] = S.val(sm("mm1"))
        xtok = S.sb(sn("xtok"), [128, NSUB * DM], scope=fscope)
        stats = S.sb(sn("stats"), [128, 16], scope=fscope)
        srow = S.sb(sn("srow"), [1, 2 * 272], scope=fscope)
        orow = S.sb(sn("orow"), [1, 2 * FT], scope=fscope)
        carr = S.sb(sn("carr"), [128, HD * FT], scope=fscope)
        hbuf = S.sb(sn("hbuf"), [128, 258], scope=fscope)
        s2s = S.sb(sn("s2s"), [128, 2 * FT], scope=fscope)
        i1s = S.sb(sn("i1s"), [128, 258], scope=fscope)
        rbuf = S.sb(sn("rbuf"), [128, FT], scope=fscope)
        obuf = S.sb(sn("obuf"), [128, FT], scope=fscope)
        maxc = S.sb(sn("maxc"), [128, 2 * FT], scope=fscope)
        xq = [S.sb(sn("xq%d" % i), [128, KD * 266], BF16, scope=fscope)
              for i in range(2)]
        cqa = S.sb(sn("cqa"), [128, HD * FT], BF16, scope=fscope)
        nbar = S.barrier(tag)
        for _eng in ("sync", "tensor", "scalar", "gpsimd", "vector"):
            @P(_eng)
            def _(e, nbar=nbar):
                e.wait_ge(S.sem("pb"), nbar)

        def stage1(j):
            c0 = j * FT
            pp = j % 2

            @P("sync")
            def _(e, j=j, c0=c0, n2=S.val(sm("st")), ntq=S.val(sm("tq"))):
                if j >= 1:
                    e.wait_ge(S.sem(sm("st")), n2)
                    e.wait_ge(S.sem(sm("tq")), ntq)
                xs = xtok.rearrange("p (s f) -> p s f", s=NSUB)

                def ld_rows(r0, r1):
                    t0 = r0 - c0
                    s_ = t0 // 128
                    p0 = t0 % 128
                    i = e.dma_start(out=xs[p0:p0 + (r1 - r0), s_:s_ + 1, :],
                                    in_=src_tok[r0:r1, :].rearrange(
                                        "(s p) f -> p s f", p=r1 - r0))
                    S.inc(sm("ld"), i, 16)
                    return i

                if patch is not None and j == 0:
                    ld_rows(8, 128)
                    ld_rows(128, 256)
                    e.wait_ge(patch["bd0"], 1)
                    ld_rows(0, 8)
                elif patch is not None and j == NFT - 1:
                    ld_rows(c0, OWN0 + TOWN)
                    ld_rows(OWN0 + TOWN + 1, c0 + 128)
                    ld_rows(c0 + 128, c0 + 256)
                    e.wait_ge(patch["bd1"], 1)
                    ld_rows(OWN0 + TOWN, OWN0 + TOWN + 1)
                else:
                    i = e.dma_start(out=xs,
                                    in_=src_tok[c0:c0 + FT, :].rearrange(
                                        "(s p) f -> p s f", p=128))
                    S.inc(sm("ld"), i, 16)
            nld = S.val(sm("ld"))

            @P("scalar")
            def _(sc, n=nld, ntq=S.val(sm("tq")), nxqe=S.val(sm("xqe"))):
                sc.wait_ge(S.sem(sm("ld")), n)
                if ntq:
                    sc.wait_ge(S.sem(sm("tq")), ntq)
                if nxqe:
                    sc.wait_ge(S.sem(sm("xqe")), nxqe)
                xs = xtok.rearrange("p (s f) -> p s f", s=NSUB)
                for s_ in range(NSUB):
                    if s_:
                        sc.wait_ge(S.sem(sm("sq")), S.val(sm("sq")))
                    for hf in range(2):
                        i = sc.activation(ps[4 + hf][:, 0:512],
                                          xs[:, s_, hf * 512:(hf + 1) * 512],
                                          ACT.Square,
                                          accum_out=stats[:, 2 * s_ + hf:2 * s_ + hf + 1])
                        S.inc(sm("sq"), i)
            nsq = S.val(sm("sq"))

            @P("vector")
            def _(v, n=nld, nsq=nsq, j=j):
                v.wait_ge(S.sem(sm("ld")), n)
                v.wait_ge(S.sem("c_pre"), S.val("c_pre"))
                xs = xtok.rearrange("p (s f) -> p s f", s=NSUB)
                for s_ in range(NSUB):
                    v.reduce_sum(stats[:, 4 + s_:5 + s_], xs[:, s_, :], AX)
                v.wait_ge(S.sem(sm("sq")), nsq)
                S.vb(v)
                v.tensor_add(stats[:, 0:1], stats[:, 0:1], stats[:, 1:2])
                S.vb(v)
                v.tensor_add(stats[:, 1:2], stats[:, 2:3], stats[:, 3:4])
                S.vb(v)
                v.tensor_copy(stats[:, 2:4], stats[:, 4:6])
                S.vb(v)
                v.tensor_scalar_mul(stats[:, 4:6], stats[:, 2:4], 1.0 / DM)
                v.tensor_scalar_mul(stats[:, 6:8], stats[:, 0:2], 1.0 / DM)
                S.vb(v)
                v.tensor_mul(stats[:, 8:10], stats[:, 4:6], stats[:, 4:6])
                S.vb(v)
                v.tensor_sub(stats[:, 6:8], stats[:, 6:8], stats[:, 8:10])
                S.vb(v)
                v.tensor_scalar_add(stats[:, 6:8], stats[:, 6:8], EPS)
                S.vb(v)
                v.reciprocal(stats[:, 8:10], stats[:, 6:8])
                S.vb(v)
                S.inc(sm("st"), v.engine_nop())

            @P("scalar")
            def _(sc, n=S.val(sm("st"))):
                sc.wait_ge(S.sem(sm("st")), n)
                i = sc.activation(stats[:, 10:12], stats[:, 8:10], ACT.Sqrt)
                S.inc(sm("sq"), i)

            @P("vector")
            def _(v, n=S.val(sm("sq")), j=j):
                v.wait_ge(S.sem(sm("sq")), n)
                v.tensor_mul(stats[:, 12:14], stats[:, 10:12], stats[:, 10:12])
                S.vb(v)
                v.tensor_mul(stats[:, 12:14], stats[:, 12:14], stats[:, 6:8])
                S.vb(v)
                v.tensor_scalar(stats[:, 12:14], stats[:, 12:14], -1.0, 3.0,
                                ALU.mult, ALU.add)
                S.vb(v)
                v.tensor_mul(stats[:, 12:14], stats[:, 12:14], stats[:, 10:12])
                S.vb(v)
                v.tensor_scalar_mul(stats[:, 10:12], stats[:, 12:14], 0.5)
                S.vb(v)
                xs = xtok.rearrange("p (s f) -> p s f", s=NSUB)
                for s_ in range(NSUB):
                    for hf in range(2):
                        v.tensor_scalar(ps[4 + hf][:, 0:512],
                                        xs[:, s_, hf * 512:(hf + 1) * 512],
                                        stats[:, 4 + s_:5 + s_], -1.0,
                                        ALU.subtract, ALU.mult)
                    S.vb(v)
                    v.reduce_max(stats[:, 12 + s_:13 + s_], ps[4][:, 0:512], AX,
                                 apply_absolute_value=True)
                    v.reduce_max(stats[:, 2:3], ps[5][:, 0:512], AX,
                                 apply_absolute_value=True)
                    S.vb(v)
                    v.tensor_max(stats[:, 12 + s_:13 + s_], stats[:, 12 + s_:13 + s_],
                                 stats[:, 2:3])
                    S.vb(v)
                v.tensor_mul(stats[:, 12:14], stats[:, 12:14], stats[:, 10:12])
                S.vb(v)
                v.tensor_scalar_max(stats[:, 12:14], stats[:, 12:14], 1e-5)
                S.vb(v)
                v.reciprocal(stats[:, 14:16], stats[:, 12:14])
                S.vb(v)
                v.tensor_scalar_mul(stats[:, 14:16], stats[:, 14:16], 127.0)
                S.vb(v)
                v.tensor_scalar_mul(stats[:, 12:14], stats[:, 12:14], 1.0 / 127.0)
                S.vb(v)
                v.tensor_scalar_mul(stats[:, 12:14], stats[:, 12:14], k_sw[:, 1:2])
                S.vb(v)
                v.tensor_mul(stats[:, 12:14], stats[:, 12:14],
                             k_vtok[:, j * NSUB:(j + 1) * NSUB])
                S.vb(v)
                v.tensor_mul(stats[:, 14:16], stats[:, 14:16], stats[:, 10:12])
                S.vb(v)
                v.tensor_mul(stats[:, 8:10], stats[:, 4:6], stats[:, 14:16])
                S.vb(v)
                v.tensor_scalar_mul(stats[:, 8:10], stats[:, 8:10], -1.0)
                S.vb(v)
                for s_ in range(NSUB):
                    for hf in range(2):
                        v.tensor_scalar(ps[4 + hf][:, 0:512],
                                        xs[:, s_, hf * 512:(hf + 1) * 512],
                                        stats[:, 14 + s_:15 + s_],
                                        stats[:, 8 + s_:9 + s_], ALU.mult, ALU.add)
                    S.vb(v)
                    for hf in range(2):
                        v.tensor_scalar(xs[:, s_, hf * 512:(hf + 1) * 512],
                                        ps[4 + hf][:, 0:512], k_cr[:, 0:1],
                                        k_cr[:, 0:1], ALU.add, ALU.subtract)
                    S.vb(v)
                S.inc(sm("st"), v.engine_nop())
            nst = S.val(sm("st"))

            baseT = S.val(sm("tq"))
            baseE = S.val(sm("xqe"))

            @P("tensor")
            def _(t, n=nst, baseT=baseT, baseE=baseE, j=j):
                t.wait_ge(S.sem(sm("st")), n)
                if j >= 2:
                    t.wait_ge(S.sem(sm("mm1")), _FB["mm1"] + (j - 1) * HD)
                xs = xtok.rearrange("p (s f) -> p s f", s=NSUB)
                k = 0
                for s_ in range(NSUB):
                    for kc in range(KD):
                        if baseE + k - 1 > 0:
                            t.wait_ge(S.sem(sm("xqe")), baseE + k - 1)
                        i = t.transpose(ps[4 + (k % 2)][:, 0:128],
                                        xs[:, s_, kc * 128:(kc + 1) * 128], k_id128)
                        S.inc(sm("tq"), i)
                        k += 1
                if j >= 1:
                    t.wait_ge(S.sem(sm("rwe")), S.val(sm("rwe")))
                for s_ in range(NSUB):
                    i = t.transpose(ps[7][:1, s_ * 128:(s_ + 1) * 128],
                                    stats[:, 12 + s_:13 + s_], k_id128)
                    S.inc(sm("tq"), i)

            @P("scalar")
            def _(sc, baseT=baseT, pp=pp, j=j):
                xv = xq[pp].rearrange("p (k t) -> p k t", k=KD)
                k = 0
                for s_ in range(NSUB):
                    for kc in range(KD):
                        sc.wait_ge(S.sem(sm("tq")), baseT + k + 1)
                        i = sc.copy(xv[:, kc, HALO + s_ * 128:HALO + (s_ + 1) * 128],
                                    ps[4 + (k % 2)][:, 0:128])
                        S.inc(sm("xqe"), i)
                        k += 1
                sc.wait_ge(S.sem(sm("tq")), baseT + k + NSUB)
                rr = srow.rearrange("o (b t) -> o b t", b=2)
                i = sc.copy(rr[:, j % 2, 8:8 + FT], ps[7][:1, 0:FT])
                S.inc(sm("rwe"), i)

            @P("vector")
            def _(v, n=S.val(sm("xqe")), nr=S.val(sm("rwe")), pp=pp, j=j):
                v.wait_ge(S.sem(sm("xqe")), n)
                v.wait_ge(S.sem(sm("rwe")), nr)
                xv = xq[pp].rearrange("p (k t) -> p k t", k=KD)
                rr = srow.rearrange("o (b t) -> o b t", b=2)
                if j == 0:
                    v.memset(xv[:, :, 0:HALO], 0.0)
                    v.memset(xq[0].rearrange("p (k t) -> p k t", k=KD)[:, :, 264:266], 0.0)
                    v.memset(xq[1].rearrange("p (k t) -> p k t", k=KD)[:, :, 264:266], 0.0)
                    v.memset(rr[:, 0, 0:8], 0.0)
                    v.memset(rr[:, 0, 264:272], 0.0)
                    v.memset(rr[:, 1, 264:272], 0.0)
                else:
                    xvp = xq[1 - pp].rearrange("p (k t) -> p k t", k=KD)
                    v.tensor_copy(xv[:, :, 0:HALO], xvp[:, :, FT:FT + HALO])
                    v.tensor_copy(rr[:, pp, 0:8], rr[:, 1 - pp, FT:FT + 8])
                    S.vb(v)
                    v.tensor_copy(xvp[:, :, 264:265], xv[:, :, HALO:HALO + 1])
                    v.tensor_copy(rr[:, 1 - pp, 264:265], rr[:, pp, 8:9])
                S.vb(v)
                S.inc(sm("s1r"), v.engine_nop())

        def stage2(j):
            c0 = j * FT
            pp = j % 2
            need_s1 = _FB["s1r"] + min(j + 2, NFT)

            nresb = S.val(sm("res"))

            @P("sync")
            def _(e, j=j, c0=c0, nresb=nresb):
                for oc in range(KD):
                    if nresb + oc > 0:
                        e.wait_ge(S.sem(sm("res")), nresb + oc)
                    i = e.dma_start(out=rbuf,
                                    in_=src_fm[oc, :, c0:c0 + FT])
                    S.inc(sm("ldf"), i, 16)
            nldf0 = S.val(sm("ldf")) - 16 * KD

            @P("tensor")
            def _(t, n1=need_s1, j=j, pp=pp, ns2e=S.val(sm("s2e"))):
                t.wait_ge(S.sem(sm("s1r")), n1)
                if ns2e:
                    t.wait_ge(S.sem(sm("s2e")), ns2e)
                rr = srow.rearrange("o (b t) -> o b t", b=2)
                i = t.matmul(ps[6][:, 0:258], k_ones, rr[:1, pp, 7:7 + 258],
                             start=True, stop=True)
                S.inc(sm("i1"), i)

            @P("scalar")
            def _(sc, n=S.val(sm("i1"))):
                sc.wait_ge(S.sem(sm("i1")), n)
                i = sc.copy(i1s, ps[6][:, 0:258])
                S.inc(sm("i1e"), i)

            basem = S.val(sm("mm1"))

            @P("tensor")
            def _(t, j=j, pp=pp, basem=basem, n1=need_s1):
                t.wait_ge(S.sem(sm("s1r")), n1)
                if j == 0:
                    if wl is None:
                        t.wait_ge(S.sem("p_out"), S.val("p_out"))
                    else:
                        t.wait_ge(S.sem(wl[0]), wl[1])
                xv = xq[pp].rearrange("p (k t) -> p k t", k=KD)
                w1v = w1q.rearrange("p (k r) -> p k r", r=DH)
                for hc in range(HD):
                    if basem + hc - 1 > 0:
                        t.wait_ge(S.sem(sm("cv")), basem + hc - 1)
                    i = None
                    for kc in range(KD):
                        i = t.matmul(ps[hc % 2][:, 0:258],
                                     w1v[:, kc, hc * 128:(hc + 1) * 128],
                                     xv[:, kc, HALO - 1:HALO + 257],
                                     start=(kc == 0), stop=(kc == KD - 1))
                    S.inc(sm("mm1"), i)

            @P("vector")
            def _(v, j=j, basem=basem, ni=S.val(sm("i1e")), nsi=S.val(sm("si"))):
                v.wait_ge(S.sem(sm("i1e")), ni)
                cwv = k_cw.rearrange("p (k c) -> p k c", k=HD)
                cbv = k_cb.rearrange("p (k c) -> p k c", k=HD)
                ca = carr.rearrange("p (h t) -> p h t", h=HD)
                basecv = S.val(sm("cv"))
                for hc in range(HD):
                    v.wait_ge(S.sem(sm("mm1")), basem + hc + 1)
                    if hc == 0 and j > 0:
                        v.wait_ge(S.sem(sm("si")), nsi)
                    if hc:
                        v.wait_ge(S.sem(sm("cv")), basecv + hc)
                    hb = hbuf
                    v.tensor_mul(hb[:, 0:258], ps[hc % 2][:, 0:258], i1s)
                    S.vb(v)
                    cc = ca[:, hc, :]
                    v.scalar_tensor_tensor(cc, hb[:, 0:256], cwv[:, hc, 0:1],
                                           bfree(cbv[:, hc, 0:1], FT), ALU.mult, ALU.add)
                    S.vb(v)
                    v.scalar_tensor_tensor(cc, hb[:, 1:257], cwv[:, hc, 1:2], cc,
                                           ALU.mult, ALU.add)
                    S.vb(v)
                    i = v.scalar_tensor_tensor(cc, hb[:, 2:258], cwv[:, hc, 2:3], cc,
                                               ALU.mult, ALU.add)
                    S.inc(sm("cv"), i)
            ncv = S.val(sm("cv"))

            @P("scalar")
            def _(sc, ncv=ncv):
                ca = carr.rearrange("p (h t) -> p h t", h=HD)
                for hc in range(HD):
                    sc.wait_ge(S.sem(sm("cv")), ncv - HD + hc + 1)
                    i = sc.activation(ca[:, hc, :], ca[:, hc, :], ACT.Silu)
                    S.inc(sm("si"), i)
            nsi = S.val(sm("si"))

            @P("vector")
            def _(v, nsi=nsi, j=j):
                ca = carr.rearrange("p (h t) -> p h t", h=HD)
                mxv = maxc.rearrange("p (b t) -> p b t", b=2)
                for hc in range(HD):
                    v.wait_ge(S.sem(sm("si")), nsi - HD + hc + 1)
                    if hc == 0:
                        v.tensor_copy(mxv[:, 0, :], ca[:, 0, :])
                        v.tensor_copy(mxv[:, 1, :], ca[:, 0, :])
                        S.vb(v)
                    else:
                        v.tensor_max(mxv[:, 0, :], mxv[:, 0, :], ca[:, hc, :])
                        v.tensor_tensor(mxv[:, 1, :], mxv[:, 1, :], ca[:, hc, :],
                                        ALU.min)
                        S.vb(v)
                v.tensor_scalar_mul(mxv[:, 1, :], mxv[:, 1, :], -1.0)
                S.vb(v)
                v.tensor_max(mxv[:, 0, :], mxv[:, 0, :], mxv[:, 1, :])
                S.vb(v)
                S.inc(sm("mx"), v.engine_nop())

            @P("tensor")
            def _(t, n=S.val(sm("mx")), nprev=S.val(sm("s2c"))):
                t.wait_ge(S.sem(sm("mx")), n)
                if nprev:
                    t.wait_ge(S.sem(sm("s2c")), nprev)
                for s_ in range(NSUB):
                    i = t.transpose(ps[5][:, s_ * 128:(s_ + 1) * 128],
                                    maxc.rearrange("p (b t) -> p b t", b=2)[:, 0, s_ * 128:(s_ + 1) * 128],
                                    k_id128)
                    S.inc(sm("mxt"), i)

            @P("vector")
            def _(v, n=S.val(sm("mxt"))):
                v.wait_ge(S.sem(sm("mxt")), n)
                for s_ in range(NSUB):
                    v.reduce_max(stats[:, 2 + s_:3 + s_],
                                 ps[5][:, s_ * 128:(s_ + 1) * 128], AX)
                S.vb(v)
                v.tensor_scalar_max(stats[:, 2:4], stats[:, 2:4], 1e-5)
                S.vb(v)
                v.reciprocal(stats[:, 4:6], stats[:, 2:4])
                S.vb(v)
                v.tensor_scalar_mul(stats[:, 4:6], stats[:, 4:6], 127.0)
                S.vb(v)
                v.tensor_scalar(stats[:, 6:8], stats[:, 2:4], 1.0 / 127.0, 0.5,
                                ALU.mult, ALU.mult)
                S.vb(v)
                v.tensor_scalar_mul(stats[:, 6:8], stats[:, 6:8], k_sw[:, 3:4])
                S.vb(v)
                S.inc(sm("s2c"), v.engine_nop())

            @P("tensor")
            def _(t, n=S.val(sm("s2c")), ni1e=S.val(sm("i1e"))):
                t.wait_ge(S.sem(sm("s2c")), n)
                t.wait_ge(S.sem(sm("i1e")), ni1e)
                k = 0
                for col in (4, 5, 6, 7):
                    i = t.transpose(ps[6][:1, k * 128:(k + 1) * 128],
                                    stats[:, col:col + 1], k_id128)
                    S.inc(sm("mxt"), i)
                    k += 1

            @P("scalar")
            def _(sc, n=S.val(sm("mxt"))):
                sc.wait_ge(S.sem(sm("mxt")), n)
                i = sc.copy(orow, ps[6][:1, 0:512])
                S.inc(sm("rw2"), i)
            nrw2 = S.val(sm("rw2"))

            @P("tensor")
            def _(t, n=nrw2):
                t.wait_ge(S.sem(sm("rw2")), n)
                i = t.matmul(ps[6][:, 0:512], k_ones, orow, start=True, stop=True)
                S.inc(sm("i1"), i)

            @P("scalar")
            def _(sc, n=S.val(sm("i1"))):
                sc.wait_ge(S.sem(sm("i1")), n)
                i = sc.copy(s2s, ps[6][:, 0:512])
                S.inc(sm("s2e"), i)

            @P("vector")
            def _(v, n=S.val(sm("s2e"))):
                v.wait_ge(S.sem(sm("s2e")), n)
                cav = carr.rearrange("p (h t) -> p h t", h=HD)
                s2m = bass.AP(s2s.tensor, s2s.offset,
                              [list(s2s.ap[0]), [0, HD], [1, FT]])
                v.tensor_tensor(cav, cav, s2m, ALU.mult)
                S.vb(v)
                v.tensor_scalar(cqa.rearrange("p (h t) -> p h t", h=HD), cav,
                                k_cr[:, 0:1], k_cr[:, 0:1], ALU.add, ALU.subtract)
                S.vb(v)
                S.inc(sm("q2"), v.engine_nop())
            nq2 = S.val(sm("q2"))

            basem2 = S.val(sm("mm2"))
            resbase = S.val(sm("res"))

            @P("tensor")
            def _(t, nq2=nq2, basem2=basem2, resbase=resbase):
                t.wait_ge(S.sem(sm("q2")), nq2)
                cv = cqa.rearrange("p (h t) -> p h t", h=HD)
                w2v = w2q.rearrange("p (k r) -> p k r", r=DM)
                for oc in range(KD):
                    if resbase + oc - 1 > 0:
                        t.wait_ge(S.sem(sm("res")), resbase + oc - 1)
                    i = None
                    for hc in range(HD):
                        i = t.matmul(ps[2 + (oc % 2)][:, 0:FT],
                                     w2v[:, hc, oc * 128:(oc + 1) * 128],
                                     cv[:, hc, :],
                                     start=(hc == 0), stop=(hc == HD - 1))
                    S.inc(sm("mm2"), i)

            nspb = S.val(sm("sp"))

            @P("vector")
            def _(v, basem2=basem2, nldf0=nldf0, j=j, c0=c0, nspb=nspb):
                s2i = bass.AP(s2s.tensor, s2s.offset + FT,
                              [list(s2s.ap[0]), [1, FT]])
                for oc in range(KD):
                    v.wait_ge(S.sem(sm("mm2")), basem2 + oc + 1)
                    v.wait_ge(S.sem(sm("ldf")), nldf0 + 16 * (oc + 1))
                    if nspb + 16 * oc > 0:
                        v.wait_ge(S.sem(sm("sp")), nspb + 16 * oc)
                    v.tensor_mul(obuf, ps[2 + (oc % 2)][:, 0:FT], s2i)
                    S.vb(v)
                    i = v.tensor_add(obuf, obuf, rbuf)
                    S.vb(v)
                    S.inc(sm("res"), i)
            nres = S.val(sm("res"))

            @P("gpsimd")
            def _(g, nres0=nres - KD, j=j, c0=c0):
                for oc in range(KD):
                    g.wait_ge(S.sem(sm("res")), nres0 + oc + 1)
                    i = g.dma_start(out=dst_fm[oc, :, c0:c0 + FT],
                                    in_=obuf)
                    S.inc(sm("sp"), i, 16)

        for j in range(NFT):
            stage1(j)
            if j >= 1:
                stage2(j - 1)
        stage2(NFT - 1)

    def emit_a1a(ascope):
        sn = lambda n: "aa_" + n
        wb_ = {nm: S.sb(sn(nm), [128, KD * DM], scope=ascope)
               for nm in ("wq", "wk", "wv")}
        xfm = [S.sb(sn("xfm%d" % i), [128, KD * FT], scope=ascope) for i in range(2)]
        xnf = [S.sb(sn("xnf%d" % i), [128, KD * FT], scope=ascope) for i in range(2)]
        xtok = S.sb(sn("xtok"), [128, NSUB * DM], scope=ascope)
        stats = S.sb(sn("stats"), [128, 16], scope=ascope)
        srow = S.sb(sn("srow"), [1, 512], scope=ascope)
        cbuf = {nm: S.sb(sn("cb" + nm), [128, H * 259], scope=ascope)
                for nm in "qkv"}
        obuf = {nm: S.sb(sn("ob" + nm), [128, H * FT], scope=ascope)
                for nm in "qkv"}
        nbar = S.barrier("aa")
        for _eng in ("sync", "tensor", "scalar", "gpsimd", "vector"):
            @P(_eng)
            def _(e, nbar=nbar):
                e.wait_ge(S.sem("pb"), nbar)

        @P("sync")
        def _(e):
            for nm, src in (("wq", wpi["wq"]), ("wk", wpi["wk"]), ("wv", wpi["wv"])):
                i = e.dma_start(
                    out=wb_[nm].rearrange("p (k o) -> p k o", k=KD),
                    in_=src.rearrange("(k p) o -> p k o", p=128))
                S.inc(sn("wl"), i, 16)
        nwl = S.val(sn("wl"))

        @P("vector")
        def _(v):
            for nm in "qkv":
                v.memset(cbuf[nm].rearrange("p (h c) -> p h c", h=H)[:, :, 0:3], 0.0)
            S.vb(v)
            S.inc(sn("hz"), v.engine_nop())

        def stage1(j):
            c0 = j * FT
            pp = j % 2

            @P("sync")
            def _(e, j=j, c0=c0, pp=pp, ntq=S.val(sn("tq")), nxn=S.val(sn("xnr")),
                  nld0=S.val(sn("ld"))):
                if j >= 2:
                    e.wait_ge(S.sem(sn("tq")), ntq)
                    e.wait_ge(S.sem(sn("xnr")), nxn)
                if j >= 1:
                    e.wait_ge(S.sem(sn("ld")), nld0)
                i = e.dma_start(
                    out=xfm[pp].rearrange("p (k t) -> p k t", k=KD),
                    in_=d_xmid[:, :, c0:c0 + FT].rearrange("k p t -> p k t"))
                S.inc(sn("ld"), i, 16)
            nld = S.val(sn("ld"))

            baseT = S.val(sn("tq"))
            baseE = S.val(sn("xte"))

            @P("tensor")
            def _(t, j=j, pp=pp, nld=nld, baseT=baseT, baseE=baseE,
                  nsq=S.val(sn("sq")), nrw=S.val(sn("rw"))):
                t.wait_ge(S.sem(sn("ld")), nld)
                if nsq:
                    t.wait_ge(S.sem(sn("sq")), nsq)
                xv = xfm[pp].rearrange("p (k t) -> p k t", k=KD)
                k = 0
                for s_ in range(NSUB):
                    for kc in range(KD):
                        if baseE + k - 1 > 0:
                            t.wait_ge(S.sem(sn("xte")), baseE + k - 1)
                        i = t.transpose(ps[4 + (k % 2)][:, 0:128],
                                        xv[:, kc, s_ * 128:(s_ + 1) * 128], k_id128)
                        S.inc(sn("tq"), i)
                        k += 1

            @P("scalar")
            def _(sc, baseT=baseT, j=j):
                xs = xtok.rearrange("p (s f) -> p s f", s=NSUB)
                k = 0
                for s_ in range(NSUB):
                    for kc in range(KD):
                        sc.wait_ge(S.sem(sn("tq")), baseT + k + 1)
                        if k == 0 and j >= 1:
                            sc.wait_ge(S.sem(sn("st")), S.val(sn("st")))
                        i = sc.copy(xs[:, s_, kc * 128:(kc + 1) * 128],
                                    ps[4 + (k % 2)][:, 0:128])
                        S.inc(sn("xte"), i)
                        k += 1
            nxte = S.val(sn("xte"))

            @P("scalar")
            def _(sc, n=nxte):
                sc.wait_ge(S.sem(sn("xte")), n)
                xs = xtok.rearrange("p (s f) -> p s f", s=NSUB)
                for s_ in range(NSUB):
                    if s_:
                        sc.wait_ge(S.sem(sn("sq")), S.val(sn("sq")))
                    for hf in range(2):
                        i = sc.activation(ps[4 + hf][:, 0:512],
                                          xs[:, s_, hf * 512:(hf + 1) * 512],
                                          ACT.Square,
                                          accum_out=stats[:, 2 * s_ + hf:2 * s_ + hf + 1])
                        S.inc(sn("sq"), i)
            nsq = S.val(sn("sq"))

            @P("vector")
            def _(v, n=nxte, nsq=nsq, j=j):
                v.wait_ge(S.sem(sn("xte")), n)
                v.wait_ge(S.sem("c_pre"), S.val("c_pre"))
                xs = xtok.rearrange("p (s f) -> p s f", s=NSUB)
                for s_ in range(NSUB):
                    v.reduce_sum(stats[:, 4 + s_:5 + s_], xs[:, s_, :], AX)
                v.wait_ge(S.sem(sn("sq")), nsq)
                S.vb(v)
                v.tensor_add(stats[:, 0:1], stats[:, 0:1], stats[:, 1:2])
                S.vb(v)
                v.tensor_add(stats[:, 1:2], stats[:, 2:3], stats[:, 3:4])
                S.vb(v)
                v.tensor_scalar_mul(stats[:, 4:6], stats[:, 4:6], 1.0 / DM)
                v.tensor_scalar_mul(stats[:, 6:8], stats[:, 0:2], 1.0 / DM)
                S.vb(v)
                v.tensor_mul(stats[:, 8:10], stats[:, 4:6], stats[:, 4:6])
                S.vb(v)
                v.tensor_sub(stats[:, 6:8], stats[:, 6:8], stats[:, 8:10])
                S.vb(v)
                v.tensor_scalar_add(stats[:, 6:8], stats[:, 6:8], EPS)
                S.vb(v)
                v.reciprocal(stats[:, 8:10], stats[:, 6:8])
                S.vb(v)
                S.inc(sn("st"), v.engine_nop())

            @P("scalar")
            def _(sc, n=S.val(sn("st"))):
                sc.wait_ge(S.sem(sn("st")), n)
                i = sc.activation(stats[:, 10:12], stats[:, 8:10], ACT.Sqrt)
                S.inc(sn("sq"), i)

            @P("vector")
            def _(v, n=S.val(sn("sq")), j=j):
                v.wait_ge(S.sem(sn("sq")), n)
                v.tensor_mul(stats[:, 12:14], stats[:, 10:12], stats[:, 10:12])
                S.vb(v)
                v.tensor_mul(stats[:, 12:14], stats[:, 12:14], stats[:, 6:8])
                S.vb(v)
                v.tensor_scalar(stats[:, 12:14], stats[:, 12:14], -1.0, 3.0,
                                ALU.mult, ALU.add)
                S.vb(v)
                v.tensor_mul(stats[:, 12:14], stats[:, 12:14], stats[:, 10:12])
                S.vb(v)
                v.tensor_scalar_mul(stats[:, 10:12], stats[:, 12:14], 0.5)
                S.vb(v)
                v.tensor_mul(stats[:, 12:14], stats[:, 10:12],
                             k_vtok[:, j * NSUB:(j + 1) * NSUB])
                S.vb(v)
                S.inc(sn("ct"), v.engine_nop())

            @P("tensor")
            def _(t, n=S.val(sn("ct")), nrw=S.val(sn("rw"))):
                t.wait_ge(S.sem(sn("ct")), n)
                if nrw:
                    t.wait_ge(S.sem(sn("rw")), nrw)
                k = 0
                for col in (4, 5, 12, 13):
                    i = t.transpose(ps[7][:1, k * 128:(k + 1) * 128],
                                    stats[:, col:col + 1], k_id128)
                    S.inc(sn("tq"), i)
                    k += 1

            @P("scalar")
            def _(sc, n=S.val(sn("tq")), ni1=S.val(sn("i1"))):
                sc.wait_ge(S.sem(sn("tq")), n)
                if ni1:
                    sc.wait_ge(S.sem(sn("i1")), ni1)
                i = sc.copy(srow, ps[7][:1, 0:512])
                S.inc(sn("rw"), i)

            @P("tensor")
            def _(t, n=S.val(sn("rw")), nxn=S.val(sn("xnr"))):
                t.wait_ge(S.sem(sn("rw")), n)
                if nxn:
                    t.wait_ge(S.sem(sn("xnr")), nxn)
                i = t.matmul(ps[6][:, 0:512], k_ones, srow, start=True, stop=True)
                S.inc(sn("i1"), i)

            @P("vector")
            def _(v, n=S.val(sn("i1")), pp=pp, nld=nld,
                  nxsp=max(0, S.val(sn("xsp")) - 16)):
                v.wait_ge(S.sem(sn("i1")), n)
                v.wait_ge(S.sem(sn("ld")), nld)
                if nxsp:
                    v.wait_ge(S.sem(sn("xsp")), nxsp)
                murep = bass.AP(ps[6].tensor, ps[6].offset,
                                [list(ps[6].ap[0]), [0, KD], [1, FT]])
                rsrep = bass.AP(ps[6].tensor, ps[6].offset + FT,
                                [list(ps[6].ap[0]), [0, KD], [1, FT]])
                xv = xfm[pp].rearrange("p (k t) -> p k t", k=KD)
                nv = xnf[pp].rearrange("p (k t) -> p k t", k=KD)
                v.tensor_tensor(nv, xv, murep, ALU.subtract)
                S.vb(v)
                v.tensor_tensor(nv, nv, rsrep, ALU.mult)
                S.vb(v)
                i = v.engine_nop()
                S.inc(sn("xnr"), i)
            nxnr = S.val(sn("xnr"))

            @P("gpsimd")
            def _(g, n=nxnr, pp=pp, c0=c0, nxsp0=S.val(sn("xsp"))):
                g.wait_ge(S.sem(sn("xnr")), n)
                if nxsp0:
                    g.wait_ge(S.sem(sn("xsp")), nxsp0)
                i = g.dma_start(
                    out=d_xn[:, :, c0:c0 + FT].rearrange("k p t -> p k t"),
                    in_=xnf[pp].rearrange("p (k t) -> p k t", k=KD))
                S.inc(sn("xsp"), i, 16)

        def stage2(j):
            c0 = j * FT
            pp = j % 2
            basem = S.val(sn("mm"))
            basee = S.val(sn("ev"))

            @P("tensor")
            def _(t, j=j, pp=pp, basem=basem, basee=basee,
                  nxn=S.val(sn("xnr")), nwl=nwl):
                t.wait_ge(S.sem(sn("xnr")), nxn)
                if j == 0:
                    t.wait_ge(S.sem(sn("wl")), nwl)
                nv = xnf[pp].rearrange("p (k t) -> p k t", k=KD)
                r = 0
                for nm in ("wq", "wk", "wv"):
                    wv = wb_[nm].rearrange("p (k o) -> p k o", k=KD)
                    for oc in range(KD):
                        if basee + r - 1 > 0:
                            t.wait_ge(S.sem(sn("ev")), basee + r - 1)
                        i = None
                        for fc in range(KD):
                            i = t.matmul(ps[r % 2][:, 0:FT],
                                         wv[:, fc, oc * 128:(oc + 1) * 128],
                                         nv[:, fc, :],
                                         start=(fc == 0), stop=(fc == KD - 1))
                        S.inc(sn("mm"), i)
                        r += 1

            @P("scalar")
            def _(sc, j=j, basem=basem, ncr=S.val(sn("cr")), nhz=S.val(sn("hz"))):
                r = 0
                sc.wait_ge(S.sem(sn("hz")), nhz)
                for nm in "qkv":
                    cv = cbuf[nm].rearrange("p (h c) -> p h c", h=H)
                    for oc in range(KD):
                        sc.wait_ge(S.sem(sn("mm")), basem + r + 1)
                        if j >= 1:
                            sc.wait_ge(S.sem(sn("cr")), ncr - 24 + r + 1)
                        i = sc.copy(cv[:, oc, 3:259], ps[r % 2][:, 0:FT])
                        S.inc(sn("ev"), i)
                        r += 1

            @P("vector")
            def _(v, j=j, basee=basee, nsp=S.val(sn("sp"))):
                r = 0
                for nm in "qkv":
                    cv = cbuf[nm].rearrange("p (h c) -> p h c", h=H)
                    ov = obuf[nm].rearrange("p (h t) -> p h t", h=H)
                    cw = k_aconv[nm][0].rearrange("p (h c) -> p h c", h=H)
                    cb2 = k_aconv[nm][1].rearrange("p (h c) -> p h c", h=H)
                    for oc in range(KD):
                        v.wait_ge(S.sem(sn("ev")), basee + r + 1)
                        if j >= 1 and r == 0:
                            v.wait_ge(S.sem(sn("sp")), nsp)
                        cc = ov[:, oc, :]
                        v.scalar_tensor_tensor(cc, cv[:, oc, 0:FT], cw[:, oc, 0:1],
                                               bfree(cb2[:, oc, 0:1], FT),
                                               ALU.mult, ALU.add)
                        S.vb(v)
                        for jj in range(1, 4):
                            v.scalar_tensor_tensor(cc, cv[:, oc, jj:jj + FT],
                                                   cw[:, oc, jj:jj + 1], cc,
                                                   ALU.mult, ALU.add)
                            S.vb(v)
                        v.tensor_copy(cv[:, oc, 0:3], cv[:, oc, 256:259])
                        S.vb(v)
                        i = v.engine_nop()
                        S.inc(sn("cr"), i)
                        r += 1

            @P("scalar")
            def _(sc, basec=S.val(sn("cr")) - 24):
                r = 0
                for nm in "qkv":
                    ov = obuf[nm].rearrange("p (h t) -> p h t", h=H)
                    for oc in range(KD):
                        sc.wait_ge(S.sem(sn("cr")), basec + r + 1)
                        i = sc.activation(ov[:, oc, :], ov[:, oc, :], ACT.Silu)
                        S.inc(sn("si"), i)
                        r += 1
            nsi = S.val(sn("si"))

            @P("vector")
            def _(v, nsi=nsi):
                v.wait_ge(S.sem(sn("si")), nsi - 8)
                i = v.tensor_scalar_mul(obuf["k"], obuf["k"], 128 ** -0.5)
                S.vb(v)
                S.inc(sn("ks"), v.engine_nop())
            nks = S.val(sn("ks"))

            @P("gpsimd")
            def _(g, j=j, c0=c0, nsi=nsi, nks=nks):
                for nm, dst in (("q", d_q), ("k", d_k), ("v", d_v)):
                    if nm == "k":
                        g.wait_ge(S.sem(sn("ks")), nks)
                    elif nm == "q":
                        g.wait_ge(S.sem(sn("si")), nsi - 16)
                    else:
                        g.wait_ge(S.sem(sn("si")), nsi)
                    i = g.dma_start(
                        out=dst[:, :, c0:c0 + FT].rearrange("h p t -> p h t"),
                        in_=obuf[nm].rearrange("p (h t) -> p h t", h=H))
                    S.inc(sn("sp"), i, 16)

        for j in range(NFT):
            stage1(j)
            if j >= 1:
                stage2(j - 1)
        stage2(NFT - 1)

    def emit_a1b(bscope):
        sm = lambda n: ("aa_" + n) if n not in ('bm',) else (sn(n))
        sn = lambda n: "ab_" + n
        wb_ = {nm: S.sb(sn(nm), [128, KD * DM], scope=bscope)
               for nm in ("wa", "wg")}
        wbb = S.sb(sn("wbb"), [128, KD * H], scope=bscope)
        xnf = [S.sb(sn("xnf%d" % i), [128, KD * FT], scope=bscope) for i in range(2)]
        obuf = {nm: S.sb(sn("ob" + nm), [128, H * FT], scope=bscope)
                for nm in ("a", "g")}
        bbuf = S.sb(sn("bbuf"), [H, FT], scope=bscope)
        svr = S.sb(sn("svr"), [1, FT], scope=bscope)
        nbar = S.barrier("ab")
        for _eng in ("sync", "tensor", "scalar", "gpsimd", "vector"):
            @P(_eng)
            def _(e, nbar=nbar):
                e.wait_ge(S.sem("pb"), nbar)

        @P("sync")
        def _(e):
            for nm, src in (("wa", wpi["wa"]), ("wg", wpi["wg"])):
                i = e.dma_start(
                    out=wb_[nm].rearrange("p (k o) -> p k o", k=KD),
                    in_=src.rearrange("(k p) o -> p k o", p=128))
                S.inc(sm("wl"), i, 16)
            i = e.dma_start(out=wbb.rearrange("p (k o) -> p k o", k=KD),
                            in_=wbi.rearrange("(k p) o -> p k o", p=128))
            S.inc(sm("wl"), i, 16)
        nwl = S.val(sm("wl"))

        def tile(j):
            c0 = j * FT
            pp = j % 2

            @P("sync")
            def _(e, j=j, c0=c0, pp=pp, nmm=S.val(sm("mm")), nld0=S.val(sm("ld"))):
                if j >= 2:
                    e.wait_ge(S.sem(sm("mm")), nmm)
                if j >= 1:
                    e.wait_ge(S.sem(sm("ld")), nld0)
                i = e.dma_start(
                    out=xnf[pp].rearrange("p (k t) -> p k t", k=KD),
                    in_=d_xn[:, :, c0:c0 + FT].rearrange("k p t -> p k t"))
                S.inc(sm("ld"), i, 16)
                if j >= 1:
                    e.wait_ge(S.sem(sm("mm")), S.val(sm("mm")))
                i = e.dma_start(out=svr, in_=vrowi[:, c0:c0 + FT])
                S.inc(sm("ld"), i, 16)
            nld = S.val(sm("ld"))

            basem = S.val(sm("mm"))
            basee = S.val(sm("ev"))

            @P("tensor")
            def _(t, j=j, pp=pp, nld=nld, basem=basem, basee=basee, nwl=nwl,
                  nbm0=S.val(sm("bm"))):
                t.wait_ge(S.sem(sm("ld")), nld)
                if j == 0:
                    t.wait_ge(S.sem(sm("wl")), nwl)
                nv = xnf[pp].rearrange("p (k t) -> p k t", k=KD)
                r = 0
                for nm in ("wa", "wg"):
                    wv = wb_[nm].rearrange("p (k o) -> p k o", k=KD)
                    for oc in range(KD):
                        if basee + r - 1 > 0:
                            t.wait_ge(S.sem(sm("ev")), basee + r - 1)
                        i = None
                        for fc in range(KD):
                            i = t.matmul(ps[r % 2][:, 0:FT],
                                         wv[:, fc, oc * 128:(oc + 1) * 128],
                                         nv[:, fc, :],
                                         start=(fc == 0), stop=(fc == KD - 1))
                        S.inc(sm("mm"), i)
                        r += 1
                if basee:
                    t.wait_ge(S.sem(sm("ev")), basee)
                wv = wbb.rearrange("p (k o) -> p k o", k=KD)
                i = None
                for fc in range(KD):
                    i = t.matmul(ps[2][0:H, 0:FT], wv[:, fc, :], nv[:, fc, :],
                                 start=(fc == 0), stop=(fc == KD - 1))
                S.inc(sm("mm"), i)
                if nbm0:
                    t.wait_ge(S.sem(sm("bm")), nbm0)
                i = t.matmul(ps[3][0:H, 0:FT], k_ones[:, 0:H], svr,
                             start=True, stop=True)
                S.inc(sm("mm"), i)

            @P("scalar")
            def _(sc, j=j, basem=basem, nsp=S.val(sm("sp"))):
                r = 0
                for nm, bias in (("a", k_ba), ("g", None)):
                    ov = obuf[nm].rearrange("p (h t) -> p h t", h=H)
                    for oc in range(KD):
                        sc.wait_ge(S.sem(sm("mm")), basem + r + 1)
                        if j >= 1 and r == 0:
                            sc.wait_ge(S.sem(sm("sp")), nsp)
                        if bias is not None:
                            i = sc.activation(ov[:, oc, :], ps[r % 2][:, 0:FT],
                                              ACT.Sigmoid,
                                              bias=bias[:, oc:oc + 1])
                        else:
                            i = sc.activation(ov[:, oc, :], ps[r % 2][:, 0:FT],
                                              ACT.Sigmoid)
                        S.inc(sm("ev"), i)
                        r += 1
                sc.wait_ge(S.sem(sm("mm")), basem + 18)
                i = sc.activation(bbuf, ps[2][0:H, 0:FT], ACT.Sigmoid,
                                  bias=k_bb[:, 0:1])
                S.inc(sm("ev"), i)

            @P("vector")
            def _(v, basee=basee):
                v.wait_ge(S.sem(sm("ev")), basee + 17)
                i = v.tensor_mul(bbuf, bbuf, ps[3][0:H, 0:FT])
                S.vb(v)
                S.inc(sm("bm"), v.engine_nop())

            @P("gpsimd")
            def _(g, j=j, c0=c0, basee=basee, nbm=S.val(sm("bm"))):
                for nm, dst, nev in (("a", d_a, basee + 8), ("g", d_g, basee + 16)):
                    g.wait_ge(S.sem(sm("ev")), nev)
                    i = g.dma_start(
                        out=dst[:, :, c0:c0 + FT].rearrange("h p t -> p h t"),
                        in_=obuf[nm].rearrange("p (h t) -> p h t", h=H))
                    S.inc(sm("sp"), i, 16)
                g.wait_ge(S.sem(sm("bm")), nbm)
                i = g.dma_start(out=d_beta[:, c0:c0 + FT], in_=bbuf)
                S.inc(sm("sp"), i, 16)

        for j in range(NFT):
            tile(j)

    def emit_scan(cscope):
        sn = lambda n: "sc_" + n
        NCH = nch
        NB = 4

        Sv = S.sb(sn("Sv"), [128, H * 128], scope=cscope)
        Sin = S.sb(sn("Sin"), [128, H * 128], scope=cscope)
        Sscr = S.sb(sn("Sscr"), [128, H * 128], scope=cscope)
        qa = [S.sb(sn("qa%d" % i), [128, 256], scope=cscope) for i in range(2)]
        ka = [S.sb(sn("ka%d" % i), [128, 256], scope=cscope) for i in range(2)]
        va = [S.sb(sn("va%d" % i), [128, 256], scope=cscope) for i in range(2)]
        aa = [S.sb(sn("aa%d" % i), [128, 256], scope=cscope) for i in range(2)]
        br = [S.sb(sn("br%d" % i), [1, 256], scope=cscope) for i in range(2)]
        Ab = S.sb(sn("A"), [128, 256], scope=cscope)
        Aip = S.sb(sn("Aip"), [128, 256], scope=cscope)
        rho = S.sb(sn("rho"), [128, 256], scope=cscope)
        Wb = S.sb(sn("W"), [128, 256], scope=cscope)
        Acc = S.sb(sn("Acc"), [128, 256], scope=cscope)
        tmp = S.sb(sn("tmp"), [128, 256], scope=cscope)
        sd0 = S.sb(sn("sd0"), [128, 256], scope=cscope)
        sd1 = S.sb(sn("sd1"), [128, 256], scope=cscope)
        ovb = S.sb(sn("ovb"), [128, 256], scope=cscope)
        Gs = S.sb(sn("Gs"), [CH, 256], scope=cscope)
        Kqs = S.sb(sn("Kqs"), [CH, 256], scope=cscope)
        KT = S.sb(sn("KT"), [CH, H * 128], scope=cscope)
        WT = S.sb(sn("WT"), [CH, H * 128], scope=cscope)
        mU = S.sb(sn("mU"), [CH, CH], scope=cscope)
        mUI = S.sb(sn("mUI"), [CH, CH], scope=cscope)
        nones = S.sb(sn("nones"), [1, 128], scope=cscope)

        nbar = S.barrier("sc")
        for _eng in ("sync", "tensor", "scalar", "gpsimd", "vector"):
            @P(_eng)
            def _(e, nbar=nbar):
                e.wait_ge(S.sem("pb"), nbar)

        @P("vector")
        def _(v):
            v.memset(Sv, 0.0)
            v.memset(Sin, 0.0)
            v.memset(nones, -1.0)
            v.tensor_scalar(mU, k_tri, -1.0, 1.0, ALU.mult, ALU.add)
            S.vb(v)
            v.tensor_add(mUI, mU, k_id32)
            S.vb(v)
            S.inc(sn("mz"), v.engine_nop())

        def one_pass(passid):
            if passid == "B" and not single:
                @P("vector")
                def _(v, nrd=S.val(sn("rd"))):
                    v.wait_ge(S.sem(sn("rd")), nrd)
                    v.tensor_scalar_mul(Sin, Sin, k_cfg[:, 0:1])
                    S.vb(v)
                    S.inc(sn("mz"), v.engine_nop())
            nmz = S.val(sn("mz"))

            @P("tensor")
            def _(t, nmz=nmz, nse0=S.val(sn("se")), ncc=S.val(sn("cc"))):
                t.wait_ge(S.sem(sn("mz")), nmz)
                if ncc:
                    t.wait_ge(S.sem(sn("cc")), ncc)
                for h in range(H):
                    i = t.transpose(ps[2 + h // 4][:, (h % 4) * 128:(h % 4) * 128 + 128],
                                    Sin[:, h * 128:(h + 1) * 128], k_id128)
                    S.inc(sn("svt"), i)

            @P("scalar")
            def _(sc, n=S.val(sn("svt")), nmz=nmz):
                sc.wait_ge(S.sem(sn("svt")), n)
                sc.wait_ge(S.sem(sn("mz")), nmz)
                for half in range(2):
                    i = sc.copy(Sv[:, half * 512:(half + 1) * 512],
                                ps[2 + half][:, 0:512])
                    S.inc(sn("se"), i)
            nsvi = S.val(sn("se"))

            ld_base = S.val(sn("ld"))

            def loads(c, gate):
                tc0 = 8 + 32 * c
                rc = c % 2

                @P("sync")
                def _(e, c=c, tc0=tc0, rc=rc, gate=gate, nld0=S.val(sn("ld"))):
                    for s_, v_ in gate:
                        e.wait_ge(S.sem(s_), v_)
                    if nld0:
                        e.wait_ge(S.sem(sn("ld")), nld0)
                    for buf, src in ((qa[rc], d_q), (ka[rc], d_k),
                                     (va[rc], d_v), (aa[rc], d_a)):
                        i = e.dma_start(
                            out=buf.rearrange("p (h t) -> p h t", h=H),
                            in_=src[:, :, tc0:tc0 + 32].rearrange("h p t -> p h t"))
                        S.inc(sn("ld"), i, 16)
                    i = e.dma_start(out=br[rc].rearrange("o (h t) -> o h t", h=H),
                                    in_=d_beta[:, tc0:tc0 + 32])
                    S.inc(sn("ld"), i, 16)

            def chunk(c):
                tc0 = 8 + 32 * c
                rc = c % 2
                stat = Sin if c == 0 else Sscr
                nld = ld_base + 80 * (c + 1)

                @P("tensor")
                def _(t, c=c, rc=rc, nld=nld, ngm=S.val(sn("gm")),
                      nse=(nsvi if c == 0 else S.val(sn("se"))),
                      nob=S.val(sn("ob")), nrh=S.val(sn("rh")),
                      ndel=S.val(sn("del"))):
                    t.wait_ge(S.sem(sn("ld")), nld)
                    if c >= 1:
                        t.wait_ge(S.sem(sn("gm")), ngm)
                    for h in range(H):
                        i = t.matmul(ps[1][0:CH, h * 32:(h + 1) * 32],
                                     ka[rc][:, h * 32:(h + 1) * 32],
                                     ka[rc][:, h * 32:(h + 1) * 32],
                                     start=True, stop=True)
                        S.inc(sn("gk"), i)
                        i = t.matmul(ps[1][0:CH, 256 + h * 32:256 + h * 32 + 32],
                                     ka[rc][:, h * 32:(h + 1) * 32],
                                     qa[rc][:, h * 32:(h + 1) * 32],
                                     start=True, stop=True)
                        S.inc(sn("gk"), i)
                    t.wait_ge(S.sem(sn("se")), nse)
                    for h in range(H):
                        i = t.transpose(ps[2 + h // 4][0:CH,
                                                       (h % 4) * 128:(h % 4) * 128 + 128],
                                        ka[rc][:, h * 32:(h + 1) * 32], k_id128)
                        S.inc(sn("kt"), i)
                    if c == 0:
                        t.wait_ge(S.sem(sn("se")), nsvi)
                        if nob:
                            t.wait_ge(S.sem(sn("ob")), nob)
                    elif nob:
                        t.wait_ge(S.sem(sn("ob")), nob)
                    for h in range(H):
                        i = t.matmul(ps[0][:, h * 64:h * 64 + 32],
                                     stat[:, h * 128:(h + 1) * 128],
                                     ka[rc][:, h * 32:(h + 1) * 32],
                                     start=True, stop=True)
                        S.inc(sn("dd"), i)
                    if c >= 1:
                        t.wait_ge(S.sem(sn("rh")), nrh)
                    i = t.matmul(ps[4][:, 0:256], nones, br[rc], start=True, stop=True)
                    S.inc(sn("bb"), i)

                @P("scalar")
                def _(sc, n=S.val(sn("kt")), ndel=S.val(sn("del"))):
                    sc.wait_ge(S.sem(sn("kt")), n)
                    if ndel:
                        sc.wait_ge(S.sem(sn("del")), ndel)
                    for half in range(2):
                        i = sc.copy(KT[:, half * 512:(half + 1) * 512],
                                    ps[2 + half][0:CH, 0:512])
                        S.inc(sn("ke"), i)

                @P("vector")
                def _(v, c=c, rc=rc, nld=nld, ngk=S.val(sn("gk")),
                      nrp=S.val(sn("rp")), nmx=S.val(sn("mx")),
                      noi=S.val(sn("oi")), ndd=S.val(sn("dd")),
                      nbb=S.val(sn("bb")), nmz=nmz):
                    v.wait_ge(S.sem(sn("gk")), ngk)
                    v.wait_ge(S.sem(sn("mz")), nmz)
                    if c >= 1:
                        v.wait_ge(S.sem(sn("rp")), nrp)
                        v.wait_ge(S.sem(sn("mx")), nmx)
                        v.wait_ge(S.sem(sn("oi")), noi)
                    mUr = bass.AP(mU.tensor, mU.offset, [list(mU.ap[0]), [0, H], [1, CH]])
                    mUIr = bass.AP(mUI.tensor, mUI.offset,
                                   [list(mUI.ap[0]), [0, H], [1, CH]])
                    v.tensor_tensor(Gs.rearrange("p (h t) -> p h t", h=H),
                                    ps[1][0:CH, 0:256].rearrange("p (h t) -> p h t", h=H),
                                    mUr, ALU.mult)
                    v.tensor_tensor(Kqs.rearrange("p (h t) -> p h t", h=H),
                                    ps[1][0:CH, 256:512].rearrange("p (h t) -> p h t", h=H),
                                    mUIr, ALU.mult)
                    S.vb(v)
                    S.inc(sn("gm"), v.engine_nop())
                    v.wait_ge(S.sem(sn("ld")), nld)
                    t0 = lambda b: bass.AP(b.tensor, b.offset,
                                           [list(b.ap[0]), [32, H], [1, 1]])
                    v.tensor_copy(sd0, aa[rc])
                    S.vb(v)
                    v.memset(t0(sd0), 0.0)
                    v.memset(sd1, 0.0)
                    S.vb(v)
                    v.tensor_copy(t0(sd1), t0(aa[rc]))
                    S.vb(v)
                    v.tensor_tensor_scan(Ab, sd0, sd1, 1.0, ALU.mult, ALU.add)
                    S.vb(v)
                    sh = lambda b, o, n: bass.AP(b.tensor, b.offset + o,
                                                 [list(b.ap[0]), [32, H], [1, n]])
                    v.tensor_copy(sh(tmp, 1, 31), sh(Ab, 0, 31))
                    v.memset(t0(tmp), 1.0)
                    S.vb(v)
                    v.reciprocal(Aip, tmp)
                    S.vb(v)
                    v.tensor_mul(sd1, tmp, Aip)
                    S.vb(v)
                    v.tensor_scalar(sd1, sd1, -1.0, 2.0, ALU.mult, ALU.add)
                    S.vb(v)
                    v.tensor_mul(Aip, Aip, sd1)
                    S.vb(v)
                    v.tensor_mul(Acc, va[rc], Aip)
                    S.vb(v)
                    v.wait_ge(S.sem(sn("dd")), ndd)
                    d0ap = bass.AP(ps[0].tensor, ps[0].offset,
                                   [list(ps[0].ap[0]), [64, H], [1, 32]])
                    v.tensor_tensor(Acc.rearrange("p (h t) -> p h t", h=H),
                                    Acc.rearrange("p (h t) -> p h t", h=H),
                                    d0ap, ALU.add)
                    S.vb(v)
                    v.reciprocal(sd0, aa[rc])
                    S.vb(v)
                    v.tensor_mul(sd1, aa[rc], sd0)
                    S.vb(v)
                    v.tensor_scalar(sd1, sd1, -1.0, 2.0, ALU.mult, ALU.add)
                    S.vb(v)
                    v.tensor_mul(sd0, sd0, sd1)
                    S.vb(v)
                    v.wait_ge(S.sem(sn("bb")), nbb)
                    v.tensor_mul(rho, sd0, ps[4][:, 0:256])
                    S.vb(v)
                    S.inc(sn("ac"), v.engine_nop())
                    S.inc(sn("rh"), v.engine_nop())

                nac = S.val(sn("ac"))

                @P("tensor")
                def _(t, nfr0=S.val(sn("fr")), ngm=S.val(sn("gm")),
                      nrh=S.val(sn("rh"))):
                    t.wait_ge(S.sem(sn("gm")), ngm)
                    t.wait_ge(S.sem(sn("rh")), nrh)
                    for st in range(CH - 1):
                        w = CH - 1 - st
                        if nfr0 + st > 0:
                            t.wait_ge(S.sem(sn("fr")), nfr0 + st)
                        sel = bass.AP(k_id32.tensor, k_id32.offset + st,
                                      [list(k_id32.ap[0]), [0, 128]])
                        mv = bass.AP(Gs.tensor, Gs.offset + st + 1,
                                     [list(Gs.ap[0]), [32, H], [1, w]])
                        i = t.matmul(ps[5][:, 0:H * w], sel, mv, start=True, stop=True)
                        S.inc(sn("rp"), i)

                @P("vector")
                def _(v, nrp0=S.val(sn("rp")) - (CH - 1), nac=nac):
                    v.wait_ge(S.sem(sn("ac")), nac)
                    for st in range(CH):
                        colW = bass.AP(Wb.tensor, Wb.offset + st,
                                       [list(Wb.ap[0]), [32, H], [1, 1]])
                        colR = bass.AP(rho.tensor, rho.offset + st,
                                       [list(rho.ap[0]), [32, H], [1, 1]])
                        colA = bass.AP(Acc.tensor, Acc.offset + st,
                                       [list(Acc.ap[0]), [32, H], [1, 1]])
                        v.tensor_tensor(colW, colR, colA, ALU.mult)
                        S.vb(v)
                        if st == CH - 1:
                            break
                        w = CH - 1 - st
                        v.wait_ge(S.sem(sn("rp")), nrp0 + st + 1)
                        wrep = bass.AP(Wb.tensor, Wb.offset + st,
                                       [list(Wb.ap[0]), [32, H], [0, w]])
                        v.tensor_tensor(tmp[:, 0:H * w], ps[5][:, 0:H * w], wrep,
                                        ALU.mult)
                        S.vb(v)
                        i = v.engine_nop()
                        S.inc(sn("fr"), i)
                        accw = bass.AP(Acc.tensor, Acc.offset + st + 1,
                                       [list(Acc.ap[0]), [32, H], [1, w]])
                        tmpw = bass.AP(tmp.tensor, tmp.offset,
                                       [list(tmp.ap[0]), [w, H], [1, w]])
                        v.tensor_tensor(accw, accw, tmpw, ALU.add)
                        S.vb(v)
                    i = v.engine_nop()
                    S.inc(sn("fs"), i)

                @P("tensor")
                def _(t, nfs=S.val(sn("fs")), nke=S.val(sn("ke"))):
                    t.wait_ge(S.sem(sn("fs")), nfs)
                    t.wait_ge(S.sem(sn("ke")), nke)
                    for h in range(H):
                        i = t.transpose(ps[2 + h // 4][0:CH,
                                                       (h % 4) * 128:(h % 4) * 128 + 128],
                                        Wb[:, h * 32:(h + 1) * 32], k_id128)
                        S.inc(sn("wt"), i)

                @P("scalar")
                def _(sc, n=S.val(sn("wt")), ndel=S.val(sn("del")),
                      noi=S.val(sn("oi"))):
                    sc.wait_ge(S.sem(sn("wt")), n)
                    if ndel:
                        sc.wait_ge(S.sem(sn("del")), ndel)
                        sc.wait_ge(S.sem(sn("oi")), noi)
                    for half in range(2):
                        i = sc.copy(WT[:, half * 512:(half + 1) * 512],
                                    ps[2 + half][0:CH, 0:512])
                        S.inc(sn("we"), i)

                @P("tensor")
                def _(t, c=c, rc=rc, n=S.val(sn("we")), nke=S.val(sn("ke")),
                      nac=nac):
                    t.wait_ge(S.sem(sn("we")), n)
                    t.wait_ge(S.sem(sn("ac")), nac)
                    stat2 = Sin if c == 0 else Sscr
                    if passid == "B":
                        for h in range(H):
                            t.matmul(ps[0][:, h * 64 + 32:h * 64 + 64],
                                     stat2[:, h * 128:(h + 1) * 128],
                                     qa[rc][:, h * 32:(h + 1) * 32],
                                     start=True, stop=False)
                            i = t.matmul(ps[0][:, h * 64 + 32:h * 64 + 64],
                                         WT[:, h * 128:(h + 1) * 128],
                                         Kqs[0:CH, h * 32:(h + 1) * 32],
                                         start=False, stop=True)
                            S.inc(sn("oi"), i)
                    t.wait_ge(S.sem(sn("ke")), nke)
                    for h in range(H):
                        i = t.matmul(ps[2 + h // 4][:, (h % 4) * 128:(h % 4) * 128 + 128],
                                     WT[:, h * 128:(h + 1) * 128],
                                     KT[0:CH, h * 128:(h + 1) * 128],
                                     start=True, stop=True)
                        S.inc(sn("del"), i)

                @P("vector")
                def _(v, c=c, noi=S.val(sn("oi")), ndel=S.val(sn("del")),
                      nos=S.val(sn("os")), nsvi=nsvi):
                    if passid == "B":
                        v.wait_ge(S.sem(sn("oi")), noi)
                        if nos:
                            v.wait_ge(S.sem(sn("os")), nos)
                        oap = bass.AP(ps[0].tensor, ps[0].offset + 32,
                                      [list(ps[0].ap[0]), [64, H], [1, 32]])
                        v.tensor_tensor(ovb.rearrange("p (h t) -> p h t", h=H),
                                        Ab.rearrange("p (h t) -> p h t", h=H),
                                        oap, ALU.mult)
                        S.vb(v)
                        i = v.engine_nop()
                        S.inc(sn("ob"), i)
                    v.wait_ge(S.sem(sn("del")), ndel)
                    if c == 0:
                        v.wait_ge(S.sem(sn("se")), nsvi)
                    dels = [bass.AP(ps[2 + half].tensor, ps[2 + half].offset,
                                    [list(ps[2 + half].ap[0]), [1, 512]])
                            for half in range(2)]
                    for half in range(2):
                        src0 = (Sin if c == 0 else Sv)[:, half * 512:(half + 1) * 512]
                        v.tensor_tensor(Sv[:, half * 512:(half + 1) * 512],
                                        src0, dels[half], ALU.add)
                        S.vb(v)
                    acrep = bass.AP(Ab.tensor, Ab.offset + 31,
                                   [list(Ab.ap[0]), [32, H], [0, 128]])
                    v.tensor_tensor(Sv.rearrange("p (h j) -> p h j", h=H),
                                    Sv.rearrange("p (h j) -> p h j", h=H),
                                    acrep, ALU.mult)
                    S.vb(v)
                    S.inc(sn("sv"), v.engine_nop())

                if c < NCH - 1:
                    @P("tensor")
                    def _(t, n=S.val(sn("sv"))):
                        t.wait_ge(S.sem(sn("sv")), n)
                        for h in range(H):
                            i = t.transpose(
                                ps[2 + h // 4][:, (h % 4) * 128:(h % 4) * 128 + 128],
                                Sv[:, h * 128:(h + 1) * 128], k_id128)
                            S.inc(sn("svt"), i)

                    @P("scalar")
                    def _(sc, n=S.val(sn("svt"))):
                        sc.wait_ge(S.sem(sn("svt")), n)
                        for half in range(2):
                            i = sc.copy(Sscr[:, half * 512:(half + 1) * 512],
                                        ps[2 + half][:, 0:512])
                            S.inc(sn("se"), i)

                if scan_dbg and c == 0:
                    @P("sync")
                    def _(e, nsv=S.val(sn("sv")), ngm=S.val(sn("gm")),
                          nfs=S.val(sn("fs"))):
                        e.wait_ge(S.sem(sn("sv")), nsv)
                        e.wait_ge(S.sem(sn("gm")), ngm)
                        e.wait_ge(S.sem(sn("fs")), nfs)
                        for sl, buf in enumerate((Ab, Aip, rho, Wb, Acc, ovb,
                                                  ka[0], aa[0], va[0], qa[0])):
                            i = e.dma_start(out=d_dbg[sl][:, 0:256], in_=buf)
                            S.inc(sn("dbg"), i, 16)
                        i = e.dma_start(out=d_dbg[10][0:CH, 0:256], in_=Gs)
                        S.inc(sn("dbg"), i, 16)
                        i = e.dma_start(out=d_dbg[11][0:CH, 0:256], in_=Kqs)
                        S.inc(sn("dbg"), i, 16)
                        i = e.dma_start(out=d_dbg[12], in_=Sv)
                        S.inc(sn("dbg"), i, 16)
                        e.wait_ge(S.sem(sn("dbg")), S.val(sn("dbg")))

                if passid == "B":
                    @P("sync")
                    def _(e, c=c, n=S.val(sn("ob")), nos0=S.val(sn("os"))):
                        e.wait_ge(S.sem(sn("ob")), n)
                        if nos0:
                            e.wait_ge(S.sem(sn("os")), nos0)
                        i = e.dma_start(
                            out=d_o[:, :, 32 * c:32 * c + 32].rearrange(
                                "h p t -> p h t"),
                            in_=ovb.rearrange("p (h t) -> p h t", h=H))
                        S.inc(sn("os"), i, 16)

            gate0 = [(sn("dd"), S.val(sn("dd"))), (sn("kt"), S.val(sn("kt"))),
                     (sn("bb"), S.val(sn("bb"))), (sn("ac"), S.val(sn("ac")))]
            loads(0, gate0)
            loads(1, gate0)
            for c in range(NCH):
                chunk(c)
                if c + 2 < NCH:
                    loads(c + 2, [(sn("dd"), S.val(sn("dd"))),
                                  (sn("kt"), S.val(sn("kt"))),
                                  (sn("bb"), S.val(sn("bb"))),
                                  (sn("oi"), S.val(sn("oi"))),
                                  (sn("ac"), S.val(sn("ac")))])

        if not single:
            one_pass("A")

            @P("gpsimd")
            def _(g, nsv=S.val(sn("sv")), nos=S.val(sn("os"))):
                g.wait_ge(S.sem(sn("sv")), nsv)
                g.wait_ge(S.sem(sn("os")), nos)
                i = g.dma_start(out=d_cc1in, in_=Sv)
                S.inc(sn("cc"), i, 16)
                g.wait_ge(S.sem(sn("cc")), S.val(sn("cc")))
                i = g.collective_compute(
                    "AllGather", ALU.bypass,
                    replica_groups=[[0, 1], [2, 3], [4, 5], [6, 7]],
                    ins=[d_cc1in.opt()],
                    outs=[d_cc1out.opt()])
                S.inc(sn("ccg"), i)
                g.wait_ge(S.sem(sn("ccg")), S.val(sn("ccg")))
                i = g.dma_start(out=Sin, in_=d_cc1out[:, 0:H * 128])
                S.inc(sn("rd"), i, 16)
                g.wait_ge(S.sem(sn("rd")), S.val(sn("rd")))
        one_pass("B")

    def emit_p3(pscope3):
        sm = lambda n: ("aa_" + n) if n not in ('ote', 'ap', 'gt', 'ldr', 'ldt', 'tt', 'zz', 'zs', 'cq', 'xs', 'wl') else (sn(n))
        sn = lambda n: "p3_" + n
        NT3 = TOWN // FT
        wo = S.sb(sn("wo"), [128, KD * DM], scope=pscope3)
        ofm = S.sb(sn("ofm"), [128, H * FT], scope=pscope3)
        gfm = S.sb(sn("gfm"), [128, H * FT], scope=pscope3)
        zfm = S.sb(sn("zfm"), [128, H * FT], scope=pscope3)
        otok = S.sb(sn("otok"), [128, NSUB * H * 128], scope=pscope3)
        stats = S.sb(sn("stats"), [128, 64], scope=pscope3)
        srow = S.sb(sn("srow"), [1, 512], scope=pscope3)
        rbuf = S.sb(sn("rbuf"), [128, FT], scope=pscope3)
        xbuf = S.sb(sn("xbuf"), [128, FT], scope=pscope3)
        xtokf = S.sb(sn("xtokf"), [128, NSUB * DM], scope=pscope3)
        zb = S.sb(sn("zb"), [128, 2048], scope=pscope3)
        lngn = S.sb(sn("lngn"), [128, 1], scope=pscope3)
        nbar = S.barrier("p3")
        for _eng in ("sync", "tensor", "scalar", "gpsimd", "vector"):
            @P(_eng)
            def _(e, nbar=nbar):
                e.wait_ge(S.sem("pb"), nbar)

        @P("sync")
        def _(e):
            i = e.dma_start(out=wo.rearrange("p (k o) -> p k o", k=KD),
                            in_=wpi["wo"].rearrange("(k p) o -> p k o", p=128))
            S.inc(sm("wl"), i, 16)
        nwl = S.val(sm("wl"))

        @P("vector")
        def _(v):
            v.memset(zb, 0.0)
            v.tensor_scalar_mul(lngn, k_lng[:, 0:1], -1.0)
            S.vb(v)
            S.inc(sm("zz"), v.engine_nop())
        nzz = S.val(sm("zz"))

        @P("gpsimd")
        def _(g, nzz=nzz):
            g.wait_ge(S.sem(sm("zz")), nzz)
            zf = zb.rearrange("p (k t) -> p k t", k=KD)
            i = g.dma_start(out=d_xattn[:, :, 0:8].rearrange("k p t -> p k t"),
                            in_=zf[:, :, 0:8])
            S.inc(sm("zs"), i, 16)
            for q0 in range(OWN0 + TOWN, SLICE, 124):
                i = g.dma_start(
                    out=d_xattn[:, :, q0:q0 + 124].rearrange("k p t -> p k t"),
                    in_=zf[:, :, 0:124])
                S.inc(sm("zs"), i, 16)
            i = g.dma_start(out=d_xat[0:8, :], in_=zb[0:8, 0:DM])
            S.inc(sm("zs"), i, 16)
            i = g.dma_start(out=d_xat[OWN0 + TOWN:OWN0 + TOWN + 128, :],
                            in_=zb[:, 0:DM])
            S.inc(sm("zs"), i, 16)
            i = g.dma_start(out=d_xat[OWN0 + TOWN + 128:SLICE, :],
                            in_=zb[0:120, 0:DM])
            S.inc(sm("zs"), i, 16)

        def tile(j):
            c0 = j * FT
            cs = OWN0 + c0

            @P("sync")
            def _(e, j=j, c0=c0, cs=cs, nld0=S.val(sm("ld")),
                  ntq=S.val(sm("tq")), ngt=S.val(sm("gt"))):
                if j >= 1:
                    e.wait_ge(S.sem(sm("ld")), nld0)
                    e.wait_ge(S.sem(sm("tq")), ntq)
                    e.wait_ge(S.sem(sm("gt")), ngt)
                i = e.dma_start(out=ofm.rearrange("p (h t) -> p h t", h=H),
                                in_=d_o[:, :, c0:c0 + FT].rearrange("h p t -> p h t"))
                S.inc(sm("ld"), i, 16)
                i = e.dma_start(out=gfm.rearrange("p (h t) -> p h t", h=H),
                                in_=d_g[:, :, cs:cs + FT].rearrange("h p t -> p h t"))
                S.inc(sm("ld"), i, 16)
            nld = S.val(sm("ld"))

            baseT = S.val(sm("tq"))
            baseE = S.val(sm("ote"))

            @P("tensor")
            def _(t, nld=nld, baseE=baseE, nsq=S.val(sm("sq")),
                  nxte=S.val(sm("xte"))):
                t.wait_ge(S.sem(sm("ld")), nld)
                if nsq:
                    t.wait_ge(S.sem(sm("sq")), nsq)
                if nxte:
                    t.wait_ge(S.sem(sm("xte")), nxte)
                ov = ofm.rearrange("p (h t) -> p h t", h=H)
                k = 0
                for s_ in range(NSUB):
                    for h in range(H):
                        if baseE + k - 1 > 0:
                            t.wait_ge(S.sem(sm("ote")), baseE + k - 1)
                        i = t.transpose(ps[4 + (k % 2)][:, 0:128],
                                        ov[:, h, s_ * 128:(s_ + 1) * 128], k_id128)
                        S.inc(sm("tq"), i)
                        k += 1

            @P("scalar")
            def _(sc, baseT=baseT, j=j):
                ot = otok.rearrange("p (s h f) -> p s h f", s=NSUB, h=H)
                k = 0
                for s_ in range(NSUB):
                    for h in range(H):
                        sc.wait_ge(S.sem(sm("tq")), baseT + k + 1)
                        if k == 0 and j >= 1:
                            sc.wait_ge(S.sem(sm("st")), S.val(sm("st")))
                        i = sc.copy(ot[:, s_, h, :], ps[4 + (k % 2)][:, 0:128])
                        S.inc(sm("ote"), i)
                        k += 1
            note = S.val(sm("ote"))

            @P("scalar")
            def _(sc, n=note):
                sc.wait_ge(S.sem(sm("ote")), n)
                ot = otok.rearrange("p (s h f) -> p s h f", s=NSUB, h=H)
                k = 0
                for s_ in range(NSUB):
                    for h in range(H):
                        if k and k % 4 == 0:
                            sc.wait_ge(S.sem(sm("sq")), S.val(sm("sq")))
                        i = sc.activation(ps[6][:, (k % 4) * 128:(k % 4) * 128 + 128],
                                          ot[:, s_, h, :], ACT.Square,
                                          accum_out=stats[:, k:k + 1])
                        S.inc(sm("sq"), i)
                        k += 1
            nsq = S.val(sm("sq"))

            @P("vector")
            def _(v, n=note, nsq=nsq):
                v.wait_ge(S.sem(sm("ote")), n)
                ot = otok.rearrange("p (s h f) -> p s h f", s=NSUB, h=H)
                k = 0
                for s_ in range(NSUB):
                    for h in range(H):
                        v.reduce_sum(stats[:, 16 + k:17 + k], ot[:, s_, h, :], AX)
                        k += 1
                v.wait_ge(S.sem(sm("sq")), nsq)
                S.vb(v)
                v.tensor_scalar_mul(stats[:, 16:32], stats[:, 16:32], 1.0 / 128)
                v.tensor_scalar_mul(stats[:, 0:16], stats[:, 0:16], 1.0 / 128)
                S.vb(v)
                v.tensor_mul(stats[:, 32:48], stats[:, 16:32], stats[:, 16:32])
                S.vb(v)
                v.tensor_sub(stats[:, 0:16], stats[:, 0:16], stats[:, 32:48])
                S.vb(v)
                v.tensor_scalar_add(stats[:, 0:16], stats[:, 0:16], EPS)
                S.vb(v)
                v.reciprocal(stats[:, 32:48], stats[:, 0:16])
                S.vb(v)
                S.inc(sm("st"), v.engine_nop())

            @P("scalar")
            def _(sc, n=S.val(sm("st"))):
                sc.wait_ge(S.sem(sm("st")), n)
                i = sc.activation(stats[:, 48:64], stats[:, 32:48], ACT.Sqrt)
                S.inc(sm("sq"), i)

            @P("vector")
            def _(v, n=S.val(sm("sq"))):
                v.wait_ge(S.sem(sm("sq")), n)
                v.tensor_mul(stats[:, 32:48], stats[:, 48:64], stats[:, 48:64])
                S.vb(v)
                v.tensor_mul(stats[:, 32:48], stats[:, 32:48], stats[:, 0:16])
                S.vb(v)
                v.tensor_scalar(stats[:, 32:48], stats[:, 32:48], -1.0, 3.0,
                                ALU.mult, ALU.add)
                S.vb(v)
                v.tensor_mul(stats[:, 32:48], stats[:, 32:48], stats[:, 48:64])
                S.vb(v)
                v.tensor_scalar_mul(stats[:, 48:64], stats[:, 32:48], 0.5)
                S.vb(v)
                v.tensor_mul(stats[:, 32:48], stats[:, 16:32], stats[:, 48:64])
                S.vb(v)
                S.inc(sm("ct"), v.engine_nop())
            nct = S.val(sm("ct"))

            ntt0 = S.val(sm("tt"))
            for q in range(4):
                for rnd, col, bank in ((0, 48, 2), (1, 32, 3)):
                    @P("tensor")
                    def _(t, q=q, col=col, nct=nct, nrw=S.val(sm("rw"))):
                        t.wait_ge(S.sem(sm("ct")), nct)
                        if nrw:
                            t.wait_ge(S.sem(sm("rw")), nrw)
                        k = 0
                        for h in (2 * q, 2 * q + 1):
                            for s_ in range(NSUB):
                                i = t.transpose(
                                    ps[7][:1, k * 128:(k + 1) * 128],
                                    stats[:, col + s_ * 8 + h:
                                          col + s_ * 8 + h + 1],
                                    k_id128)
                                S.inc(sm("cq"), i)
                                k += 1

                    @P("scalar")
                    def _(sc, n=S.val(sm("cq")), ni1=S.val(sm("i1"))):
                        sc.wait_ge(S.sem(sm("cq")), n)
                        if ni1:
                            sc.wait_ge(S.sem(sm("i1")), ni1)
                        i = sc.copy(srow, ps[7][:1, 0:512])
                        S.inc(sm("rw"), i)

                    @P("tensor")
                    def _(t, bank=bank, n=S.val(sm("rw")), nap=S.val(sm("ap"))):
                        t.wait_ge(S.sem(sm("rw")), n)
                        if nap:
                            t.wait_ge(S.sem(sm("ap")), nap)
                        i = t.matmul(ps[bank][:, 0:512], k_ones, srow,
                                     start=True, stop=True)
                        S.inc(sm("i1"), i)

                @P("vector")
                def _(v, q=q, n=S.val(sm("i1")), nld=nld, nzz=nzz, ntt0=ntt0):
                    v.wait_ge(S.sem(sm("i1")), n)
                    v.wait_ge(S.sem(sm("ld")), nld)
                    v.wait_ge(S.sem(sm("zz")), nzz)
                    if q == 0 and ntt0:
                        v.wait_ge(S.sem(sm("tt")), ntt0)
                    sl = slice(q * 512, (q + 1) * 512)
                    v.tensor_tensor(zfm[:, sl], ofm[:, sl], ps[2][:, 0:512],
                                    ALU.mult)
                    S.vb(v)
                    v.tensor_tensor(zfm[:, sl], zfm[:, sl], ps[3][:, 0:512],
                                    ALU.subtract)
                    S.vb(v)
                    v.scalar_tensor_tensor(zfm[:, sl], zfm[:, sl], lngn,
                                           bfree(k_lnb[:, 0:1], 512),
                                           ALU.mult, ALU.add)
                    S.vb(v)
                    v.tensor_mul(zfm[:, sl], zfm[:, sl], gfm[:, sl])
                    S.vb(v)
                    i = v.engine_nop()
                    S.inc(sm("ap"), i)
                    if q == 3:
                        S.inc(sm("gt"), v.engine_nop())
            nap = S.val(sm("ap"))

            basem = S.val(sm("mm"))

            @P("tensor")
            def _(t, j=j, nap=nap, basem=basem, nwl=nwl, nrs=S.val(sm("res"))):
                t.wait_ge(S.sem(sm("ap")), nap)
                if j == 0:
                    t.wait_ge(S.sem(sm("wl")), nwl)
                wv = wo.rearrange("p (k o) -> p k o", k=KD)
                zv = zfm.rearrange("p (h t) -> p h t", h=H)
                for oc in range(KD):
                    if nrs + oc - 1 > 0:
                        t.wait_ge(S.sem(sm("res")), nrs + oc - 1)
                    i = None
                    for hc in range(KD):
                        i = t.matmul(ps[oc % 2][:, 0:FT],
                                     wv[:, hc, oc * 128:(oc + 1) * 128],
                                     zv[:, hc, :],
                                     start=(hc == 0), stop=(hc == KD - 1))
                    S.inc(sm("mm"), i)

            @P("sync")
            def _(e, j=j, cs=cs, nsp=S.val(sm("sp"))):
                for oc in range(KD):
                    if nsp + 16 * oc > 0:
                        e.wait_ge(S.sem(sm("sp")), nsp + 16 * oc)
                    i = e.dma_start(out=rbuf, in_=d_xmid[oc, :, cs:cs + FT])
                    S.inc(sm("ldr"), i, 16)

            @P("vector")
            def _(v, j=j, cs=cs, basem=basem, nldr=S.val(sm("ldr")) - 16 * KD,
                  nsp=S.val(sm("sp"))):
                for oc in range(KD):
                    v.wait_ge(S.sem(sm("mm")), basem + oc + 1)
                    v.wait_ge(S.sem(sm("ldr")), nldr + 16 * (oc + 1))
                    if nsp + 16 * oc > 0:
                        v.wait_ge(S.sem(sm("sp")), nsp + 16 * oc)
                    v.tensor_add(xbuf, ps[oc % 2][:, 0:FT], rbuf)
                    S.vb(v)
                    i = v.engine_nop()
                    S.inc(sm("res"), i)
            nres = S.val(sm("res"))

            @P("gpsimd")
            def _(g, j=j, cs=cs, nres0=nres - KD, nzs=S.val(sm("zs"))):
                if j == 0:
                    g.wait_ge(S.sem(sm("zs")), nzs)
                for oc in range(KD):
                    g.wait_ge(S.sem(sm("res")), nres0 + oc + 1)
                    i = g.dma_start(out=d_xattn[oc, :, cs:cs + FT], in_=xbuf)
                    S.inc(sm("sp"), i, 16)

            @P("sync")
            def _(e, j=j, cs=cs, nsp=S.val(sm("sp")), ntt=S.val(sm("tt"))):
                e.wait_ge(S.sem(sm("sp")), nsp)
                if ntt:
                    e.wait_ge(S.sem(sm("tt")), ntt)
                i = e.dma_start(
                    out=zfm.rearrange("p (k t) -> p k t", k=KD),
                    in_=d_xattn[:, :, cs:cs + FT].rearrange("k p t -> p k t"))
                S.inc(sm("ldt"), i, 16)

            @P("tensor")
            def _(t, n=S.val(sm("ldt")), baseE2=S.val(sm("xte"))):
                t.wait_ge(S.sem(sm("ldt")), n)
                xv = zfm.rearrange("p (k t) -> p k t", k=KD)
                k = 0
                for s_ in range(NSUB):
                    for kc in range(KD):
                        if baseE2 + k - 1 > 0:
                            t.wait_ge(S.sem(sm("xte")), baseE2 + k - 1)
                        i = t.transpose(ps[4 + (k % 2)][:, 0:128],
                                        xv[:, kc, s_ * 128:(s_ + 1) * 128],
                                        k_id128)
                        S.inc(sm("tt"), i)
                        k += 1

            @P("scalar")
            def _(sc, baseT2=S.val(sm("tt")) - 16, nxs=S.val(sm("xs"))):
                xs2 = xtokf.rearrange("p (s f) -> p s f", s=NSUB)
                k = 0
                for s_ in range(NSUB):
                    for kc in range(KD):
                        sc.wait_ge(S.sem(sm("tt")), baseT2 + k + 1)
                        if k == 0 and nxs:
                            sc.wait_ge(S.sem(sm("xs")), nxs)
                        i = sc.copy(xs2[:, s_, kc * 128:(kc + 1) * 128],
                                    ps[4 + (k % 2)][:, 0:128])
                        S.inc(sm("xte"), i)
                        k += 1

            @P("gpsimd")
            def _(g, j=j, cs=cs, n=S.val(sm("xte"))):
                g.wait_ge(S.sem(sm("xte")), n)
                i = g.dma_start(
                    out=d_xat[cs:cs + FT, :].rearrange("(s p) f -> p s f", p=128),
                    in_=xtokf.rearrange("p (s f) -> p s f", s=NSUB))
                S.inc(sm("xs"), i, 16)

        for j in range(NT3):
            tile(j)

    def emit_cc2(semBD0, semBD1):
        sn = lambda n: "c2_" + n
        bnd = S.sb(sn("bnd"), [128, 2 * KD])

        @P("gpsimd")
        def _(g, nsp=S.val("aa_sp"), nxs=S.val("p3_xs")):
            g.wait_ge(S.sem("aa_sp"), nsp)
            g.wait_ge(S.sem("p3_xs"), nxs)
            i = g.dma_start(out=bnd[:, 0:KD],
                            in_=d_xattn[:, :, OWN0].rearrange("k p -> p k"),
                            allow_slow_non_contiguous=True)
            S.inc(sn("ld"), i, 16)
            i = g.dma_start(out=bnd[:, KD:2 * KD],
                            in_=d_xattn[:, :, OWN0 + TOWN - 1].rearrange("k p -> p k"),
                            allow_slow_non_contiguous=True)
            S.inc(sn("ld"), i, 16)
            g.wait_ge(S.sem(sn("ld")), S.val(sn("ld")))
            i = g.dma_start(out=d_cc2in, in_=bnd)
            S.inc(sn("cin"), i, 16)
            g.wait_ge(S.sem(sn("cin")), S.val(sn("cin")))
            if not single:
                i = g.collective_compute(
                    "AllGather", ALU.bypass,
                    replica_groups=[[0, 1], [2, 3], [4, 5], [6, 7]],
                    ins=[d_cc2in.opt()],
                    outs=[d_cc2out.opt()])
                S.inc(sn("ccg"), i)
                g.wait_ge(S.sem(sn("ccg")), S.val(sn("ccg")))
            else:
                i = g.dma_start(out=d_cc2out[:, 0:2 * KD], in_=d_cc2in)
                S.inc(sn("cp"), i, 16)
                i = g.dma_start(out=d_cc2out[:, 2 * KD:4 * KD], in_=d_cc2in)
                S.inc(sn("cp"), i, 16)
                g.wait_ge(S.sem(sn("cp")), S.val(sn("cp")))
            i = g.dma_start(
                out=d_xat[7:8, :].rearrange("r (k p) -> (r k) p", k=KD),
                in_=d_cc2out[:, KD:2 * KD].rearrange("p k -> k p"),
                allow_slow_non_contiguous=True)
            S.inc(sn("pa"), i, 16)
            i = g.dma_start(
                out=d_xat[OWN0 + TOWN:OWN0 + TOWN + 1, :].rearrange(
                    "r (k p) -> (r k) p", k=KD),
                in_=d_cc2out[:, 2 * KD:3 * KD].rearrange("p k -> k p"),
                allow_slow_non_contiguous=True)
            S.inc(sn("pb"), i, 16)
            g.wait_ge(S.sem(sn("pa")), S.val(sn("pa")))
            g.wait_ge(S.sem(sn("pb")), S.val(sn("pb")))
            g.engine_nop().then_inc(semBD0, 1)
            g.engine_nop().then_inc(semBD1, 1)

    def emit_debug_copy(src_dram, waitsem, waitval):
        @P("sync")
        def _(e):
            e.wait_ge(S.sem(waitsem), waitval)
            i = e.dma_start(out=out,
                            in_=src_dram[:, :, OWN0:OWN0 + TOWN])
            S.inc("z_out", i, 16)

        @P("gpsimd")
        def _(g):
            g.wait_ge(S.sem("z_out"), S.val("z_out"))

    wscope = ExitStack()
    if "pre" in run_phases:
        w1q, w2q = emit_pre(wscope)
    if "f1" in run_phases:
        with ExitStack() as fscope:
            emit_fphase("f1", x0t, x0f, d_xmid, w1q, w2q, fscope)
    wscope.close()
    if "a1a" in run_phases:
        with ExitStack() as ascope:
            emit_a1a(ascope)
    if "a1b" in run_phases:
        with ExitStack() as bscope:
            emit_a1b(bscope)
    if "scan" in run_phases:
        with ExitStack() as cscope:
            emit_scan(cscope)
    if "p3" in run_phases:
        with ExitStack() as pscope3:
            emit_p3(pscope3)
    if "f2" in run_phases:
        semBD0 = S.ctx.enter_context(nc.semaphore("bd0"))
        semBD1 = S.ctx.enter_context(nc.semaphore("bd1"))
        nbarw = S.barrier("f2w")
        for _eng in ("sync", "tensor", "scalar", "gpsimd", "vector"):
            @P(_eng)
            def _(e, nbarw=nbarw):
                e.wait_ge(S.sem("pb"), nbarw)
        emit_cc2(semBD0, semBD1)
        w1q2 = S.sb("f2w1q", [128, KD * DH], BF16)
        w2q2 = S.sb("f2w2q", [128, HD * DM], BF16)

        @P("sync")
        def _(e, npo=S.val("p_out")):
            e.wait_ge(S.sem("p_out"), npo)
            i = e.dma_start(out=w1q2, in_=d_w1q)
            S.inc("f2wl", i, 16)
            i = e.dma_start(out=w2q2, in_=d_w2q)
            S.inc("f2wl", i, 16)
        with ExitStack() as fscope2:
                emit_fphase("f2", d_xat, d_xattn, d_xfin, w1q2, w2q2, fscope2,
                        wl=("f2wl", 32),
                        patch=dict(bd0=semBD0, bd1=semBD1))
        if debug_out is None:
            emit_debug_copy(d_xfin, "f_sp", S.val("f_sp"))

    if debug_out == "xmid":
        emit_debug_copy(d_xmid, "f_sp", S.val("f_sp"))
    elif debug_out == "xn":
        emit_debug_copy(d_xn, "aa_xsp", S.val("aa_xsp"))
    elif debug_out in ("q", "k", "v"):
        emit_debug_copy({"q": d_q, "k": d_k, "v": d_v}[debug_out],
                        "aa_sp", S.val("aa_sp"))
    elif debug_out in ("a", "g"):
        emit_debug_copy({"a": d_a, "g": d_g}[debug_out],
                        "aa_sp", S.val("aa_sp"))
    elif debug_out == "o":
        @P("sync")
        def _(e, n=S.val("sc_os")):
            e.wait_ge(S.sem("sc_os"), n)
            i = e.dma_start(out=out[:, :, 0:32 * nch],
                            in_=d_o[:, :, 0:32 * nch])
            S.inc("z_out", i, 16)

        @P("gpsimd")
        def _(g):
            g.wait_ge(S.sem("z_out"), S.val("z_out"))
    elif debug_out == "beta":
        @P("sync")
        def _(e, n=S.val("aa_sp")):
            e.wait_ge(S.sem("aa_sp"), n)
            i = e.dma_start(out=out[0][0:H, :],
                            in_=d_beta[:, OWN0:OWN0 + TOWN])
            S.inc("z_out", i, 16)

        @P("gpsimd")
        def _(g):
            g.wait_ge(S.sem("z_out"), S.val("z_out"))

    S.ctx.close()
    return nc


def _prep_inputs(inputs):
    x = np.asarray(inputs["L0_attn_x"])
    maps = []
    ident = np.eye(CH, dtype=np.float32)
    tri = np.tril(np.ones((CH, CH), np.float32))
    id128 = np.eye(128, dtype=np.float32)
    w1 = np.ascontiguousarray(inputs["L0_ffn_w1"], np.float32)
    w2 = np.ascontiguousarray(inputs["L0_ffn_w2"], np.float32)
    cw = np.ascontiguousarray(inputs["L0_ffn_conv_w"], np.float32).reshape(DH, 3)
    cwl = np.ascontiguousarray(cw.reshape(HD, 128, 3).transpose(1, 0, 2))
    cbl = np.ascontiguousarray(np.asarray(inputs["L0_ffn_conv_b"], np.float32)
                               .reshape(HD, 128).T)
    wmats = {n: np.ascontiguousarray(inputs["L0_attn_W" + n[-1]], np.float32)
             for n in ("wq", "wk", "wv", "wa", "wg", "wo")}
    wb = np.ascontiguousarray(inputs["L0_attn_Wb"], np.float32)
    ba = np.ascontiguousarray(np.asarray(inputs["L0_attn_ba"], np.float32)
                              .reshape(KD, 128).T)
    bb = np.ascontiguousarray(inputs["L0_attn_bb"], np.float32).reshape(H, 1)
    conv = {}
    for nm, wkey, bkey in (("cqw", "L0_attn_cqw", "L0_attn_cqb"),
                           ("ckw", "L0_attn_ckw", "L0_attn_ckb"),
                           ("cvw", "L0_attn_cvw", "L0_attn_cvb")):
        conv[nm] = np.ascontiguousarray(np.asarray(inputs[wkey], np.float32)
                                        .reshape(KD, 128, 4).transpose(1, 0, 2))
        conv[nm[:2] + "b"] = np.ascontiguousarray(np.asarray(inputs[bkey], np.float32)
                                                  .reshape(KD, 128).T)
    lng = np.ascontiguousarray(inputs["L0_attn_ln_g"], np.float32).reshape(128, 1)
    lnb = np.ascontiguousarray(inputs["L0_attn_ln_b"], np.float32).reshape(128, 1)

    for c in range(NCORE):
        b = c // 2
        half = c % 2
        g0 = half * TOWN - HALO
        sl = np.zeros((SLICE, DM), np.float32)
        lo = max(0, g0)
        hi = min(T, g0 + SLICE)
        sl[lo - g0:hi - g0] = x[b, lo:hi]
        x0t_ = np.ascontiguousarray(sl)
        x0f_ = np.ascontiguousarray(sl.T.reshape(KD, 128, SLICE))
        valid = np.zeros(SLICE, np.float32)
        valid[lo - g0:hi - g0] = 1.0
        vtok_ = np.ascontiguousarray(valid.reshape(SLICE // 128, 128).T)
        vrow_ = valid.reshape(1, SLICE).copy()
        cfg_ = np.zeros((128, 8), np.float32)
        cfg_[:, 0] = half
        m = dict(x0f=x0f_, x0t=x0t_, vtok=vtok_, vrow=vrow_, cfg=cfg_, ident=ident, tri=tri,
                 id128=id128, w1=w1, w2=w2, cw=cwl, cb=cbl,
                 wb=wb, ba=ba, bb=bb,
                 cqw=conv["cqw"], cqb=conv["cqb"], ckw=conv["ckw"], ckb=conv["ckb"],
                 cvw=conv["cvw"], cvb=conv["cvb"], lng=lng, lnb=lnb)
        for n in ("wq", "wk", "wv", "wa", "wg", "wo"):
            m[n] = wmats[n]
        maps.append(m)
    return maps


_NC_CACHE = {}


def _get_nc(key, **kw):
    if key not in _NC_CACHE:
        _NC_CACHE[key] = build(**kw)
    return _NC_CACHE[key]


def kernel(**inputs):
    maps = _prep_inputs(inputs)
    nc = _get_nc("full")
    res = run_bass_kernel_spmd(nc, maps, core_ids=list(range(NCORE)))
    full = np.zeros((B, T, DM), np.float32)
    for c in range(NCORE):
        b, half = c // 2, c % 2
        o = np.asarray(res.results[c]["out"])
        full[b, half * TOWN:(half + 1) * TOWN] = o.reshape(DM, TOWN).T
    return full


# version 04 (final)
# speedup vs baseline: 1.0483x; 1.0098x over previous
import numpy as np
import concourse.bass as bass
import concourse.mybir as mybir
from concourse.bass_utils import run_bass_kernel_spmd

F32 = mybir.dt.float32
BF16 = mybir.dt.bfloat16
ALU = mybir.AluOpType
ACT = mybir.ActivationFunctionType
AX = mybir.AxisListType.X

B, T, DM, DH = 4, 8192, 1024, 4096
H, D = 8, 128
NCORE = 8
TOWN = T // 2
HALO = 8
FT = 256
NFT = 17
SLICE = NFT * FT
OWN0 = HALO
CH = 32
NCHUNK = TOWN // CH
KD = DM // 128
HD = DH // 128
NSUB = FT // 128
RSQRT_D = float(np.float32(D ** (-0.5)))
C32V = 1.5 * 2.0 ** 23
C64V = 1.5 * 2.0 ** 52
EPS = 1e-5


def bfree(col_ap, n):
    return bass.AP(col_ap.tensor, col_ap.offset, list(col_ap.ap[:1]) + [[0, n]])


class Prog:
    def __init__(self):
        from contextlib import ExitStack
        self.ctx = ExitStack()
        self.nc = bass.Bass("TRN2", target_bir_lowering=False, debug=False)
        self.blocks = []
        self.sem_h = {}
        self.sem_v = {}

    def sem(self, name):
        if name not in self.sem_h:
            self.sem_h[name] = self.ctx.enter_context(self.nc.semaphore(name))
            self.sem_v[name] = 0
        return self.sem_h[name]

    def inc(self, name, instr=None, n=1):
        self.sem(name)
        self.sem_v[name] += n
        if instr is not None:
            instr.then_inc(self.sem_h[name], n)
        return self.sem_v[name]

    def val(self, name):
        self.sem(name)
        return self.sem_v[name]

    def barrier(self, tag):
        P = self.phase
        snap = [(n, v) for n, v in sorted(self.sem_v.items())
                if v > 0 and n != "pb" and not n.startswith("c2_")]

        @P("vector")
        def _(v, snap=snap):
            for n, val in snap:
                v.wait_ge(self.sem_h[n], val)
            self.inc("pb", v.engine_nop())
        return self.val("pb")

    def vb(self, eng, name="vch"):
        mknop = getattr(eng, "engine_nop", eng.nop)
        n = self.inc(name, mknop())
        eng.wait_ge(self.sem(name), n)

    def phase(self, engine):
        def reg(fn):
            getattr(self.block, engine)(fn)
            return fn
        return reg

    def sb(self, name, shape, dt=F32, scope=None):
        t = (scope or self.ctx).enter_context(self.nc.sbuf_tensor(name, shape, dt))
        return t[:]

    def psum(self, name, shape, scope=None):
        t = (scope or self.ctx).enter_context(self.nc.psum_tensor(name, shape, F32))
        return t[:]


def build(run_phases=("pre", "f1", "a1a", "a1b", "scan", "p3", "f2"), debug_out=None,
          single=False, nch=128, scan_dbg=False):
    from contextlib import ExitStack
    S = Prog()
    nc = S.nc
    P = S.phase

    def din(name, shape, dt=F32):
        return nc.dram_tensor(name, shape, dt, kind="ExternalInput").ap()

    def dscr(name, shape, dt=F32):
        return nc.dram_tensor(name, shape, dt).ap()

    x0f = din("x0f", [KD, 128, SLICE])
    x0t = din("x0t", [SLICE, DM])
    vrowi = din("vrow", [1, SLICE])
    vtok = din("vtok", [128, SLICE // 128])
    cfgi = din("cfg", [128, 8])
    identi = din("ident", [CH, CH])
    trii = din("tri", [CH, CH])
    id128i = din("id128", [128, 128])
    w1i = din("w1", [DH, DM])
    w2i = din("w2", [DM, DH])
    cwi = din("cw", [128, HD, 3])
    cbi = din("cb", [128, HD])
    wpi = {n: din(n, [DM, DM]) for n in ("wq", "wk", "wv", "wa", "wg", "wo")}
    wbi = din("wb", [DM, H])
    bai = din("ba", [128, KD])
    bbi = din("bb", [H, 1])
    convi = {"q": (din("cqw", [128, KD, 4]), din("cqb", [128, KD])),
             "k": (din("ckw", [128, KD, 4]), din("ckb", [128, KD])),
             "v": (din("cvw", [128, KD, 4]), din("cvb", [128, KD]))}
    lngi = din("lng", [128, 1])
    lnbi = din("lnb", [128, 1])
    out = nc.dram_tensor("out", [KD, 128, TOWN], F32, kind="ExternalOutput").ap()

    d_w1q = dscr("d_w1q", [128, KD * DH], BF16)
    d_w2q = dscr("d_w2q", [128, HD * DM], BF16)
    d_xmid = dscr("d_xmid", [KD, 128, SLICE])
    d_xn = dscr("d_xn", [KD, 128, SLICE])
    d_q = dscr("d_q", [H, 128, SLICE])
    d_k = dscr("d_k", [H, 128, SLICE])
    d_v = dscr("d_v", [H, 128, SLICE])
    d_a = dscr("d_a", [H, 128, SLICE])
    d_g = dscr("d_g", [H, 128, SLICE])
    d_beta = dscr("d_beta", [H, SLICE])
    d_o = dscr("d_o", [H, 128, TOWN])
    d_xattn = dscr("d_xattn", [KD, 128, SLICE])
    d_xat = dscr("d_xat", [SLICE, DM])
    d_cc1in = dscr("d_cc1in", [128, H * 128])
    d_cc1out = dscr("d_cc1out", [128, 2 * H * 128])
    d_cc2in = dscr("d_cc2in", [128, 2 * KD])
    d_cc2out = dscr("d_cc2out", [128, 4 * KD])
    d_xfin = dscr("d_xfin", [KD, 128, SLICE])
    d_dbg = dscr("d_dbg", [16, 128, 1024])

    S.block = S.ctx.enter_context(nc.Block())

    k_ones = S.sb("k_ones", [1, 128])
    k_onec = S.sb("k_onec", [128, 2])
    k_id32 = S.sb("k_id32", [CH, CH])
    k_tri = S.sb("k_tri", [CH, CH])
    k_id128 = S.sb("k_id128", [128, 128])
    k_cfg = S.sb("k_cfg", [128, 8])
    k_cr = S.sb("k_cr", [128, 2])
    k_vtok = S.sb("k_vtok", [128, SLICE // 128])
    k_cw = S.sb("k_cw", [128, 3 * HD])
    k_cb = S.sb("k_cb", [128, HD])
    k_aconv = {nm: (S.sb("k_c%sw" % nm, [128, 4 * KD]), S.sb("k_c%sb" % nm, [128, KD]))
               for nm in ("q", "k", "v")}
    k_ba = S.sb("k_ba", [128, KD])
    k_bb = S.sb("k_bb", [H, 2])
    k_lng = S.sb("k_lng", [128, 2])
    k_lnb = S.sb("k_lnb", [128, 2])
    k_sw = S.sb("k_sw", [128, 4])

    ps = [S.psum(f"ps{i}", [128, 512]) for i in range(8)]

    def emit_pre(wscope):
        w1q = S.sb("w1q", [128, KD * DH], BF16, scope=wscope)
        w2q = S.sb("w2q", [128, HD * DM], BF16, scope=wscope)

        @P("sync")
        def _(e):
            def ld(dst, src):
                S.inc("c_ld", e.dma_start(out=dst, in_=src), 16)
            ld(k_id32, identi)
            ld(k_tri, trii)
            ld(k_id128, id128i)
            ld(k_cfg, cfgi)
            ld(k_vtok, vtok)
            ld(k_cw, cwi.rearrange("p k c -> p (k c)"))
            ld(k_cb, cbi)
            for nm in ("q", "k", "v"):
                ld(k_aconv[nm][0], convi[nm][0].rearrange("p k c -> p (k c)"))
                ld(k_aconv[nm][1], convi[nm][1])
            ld(k_ba, bai)
            ld(k_bb[:, 0:1], bbi)
            ld(k_lng[:, 0:1], lngi)
            ld(k_lnb[:, 0:1], lnbi)
        nld = S.val("c_ld")

        @P("vector")
        def _(v):
            v.wait_ge(S.sem("c_ld"), nld)
            v.memset(k_ones, 1.0)
            v.memset(k_onec[:, 0:1], 1.0)
            v.memset(k_cr[:, 1:2], 1.0)
            S.vb(v)
            v.tensor_scalar_add(k_cr[:, 1:2], k_cr[:, 1:2], 2.0 ** -30)
            S.vb(v)
            v.tensor_scalar_add(k_cr[:, 1:2], k_cr[:, 1:2], -1.0)
            S.vb(v)
            v.tensor_scalar(k_cr[:, 0:1], k_cr[:, 1:2], 2.0 ** 30, C64V - C32V,
                            ALU.mult, ALU.mult)
            S.vb(v)
            v.tensor_scalar_add(k_cr[:, 0:1], k_cr[:, 0:1], C32V)
            S.vb(v)
            S.inc("c_pre", v.engine_nop())
        npre = S.val("c_pre")

        with ExitStack() as pscope:
            wbuf = S.sb("wbuf", [128, DH], scope=pscope)
            wacc = S.sb("wacc", [128, 36], scope=pscope)
            for wi, (wsrc, wres, dsp, rows, cols) in enumerate(
                    [(w1i, w1q, d_w1q, DH, DM), (w2i, w2q, d_w2q, DM, DH)]):
                nblk = rows // 128
                for bk in range(nblk):
                    need = S.val("p_dv")
                    needmm = S.val("p_mm")

                    @P("sync")
                    def _(e, bk=bk, need=need, needmm=needmm, wsrc=wsrc, cols=cols):
                        if need:
                            e.wait_ge(S.sem("p_dv"), need)
                        if needmm:
                            e.wait_ge(S.sem("p_mm"), needmm)
                        S.inc("p_ld", e.dma_start(
                            out=wbuf[:, :cols],
                            in_=wsrc[bk * 128:(bk + 1) * 128, :]), 16)

                    @P("vector")
                    def _(v, bk=bk, n=S.val("p_ld"), cols=cols):
                        v.wait_ge(S.sem("p_ld"), n)
                        i = v.reduce_sum(wacc[:, bk:bk + 1], wbuf[:, :cols], AX,
                                         apply_absolute_value=True)
                        S.inc("p_dv", i)

                @P("vector")
                def _(v, nblk=nblk, n=S.val("p_dv")):
                    v.wait_ge(S.sem("p_dv"), n)
                    v.reduce_sum(wacc[:, 33:34], wacc[:, 0:nblk], AX)
                    S.vb(v)
                    S.inc("p_dv", v.engine_nop())

                @P("tensor")
                def _(t, n=S.val("p_dv")):
                    t.wait_ge(S.sem("p_dv"), n)
                    t.wait_ge(S.sem("c_pre"), npre)
                    i = t.matmul(ps[0][:1, 0:1], k_onec[:, 0:1], wacc[:, 33:34],
                                 start=True, stop=True)
                    S.inc("p_mm", i)

                @P("vector")
                def _(v, n=S.val("p_mm"), denom=1.0 / (rows * cols)):
                    v.wait_ge(S.sem("p_mm"), n)
                    v.tensor_scalar_mul(wacc[:1, 34:35], ps[0][:1, 0:1], denom)
                    S.vb(v)
                    v.tensor_scalar_max(wacc[:1, 34:35], wacc[:1, 34:35], 1e-5)
                    S.vb(v)
                    v.reciprocal(wacc[:1, 35:36], wacc[:1, 34:35])
                    S.vb(v)
                    S.inc("p_dv", v.engine_nop())

                @P("tensor")
                def _(t, n=S.val("p_dv")):
                    t.wait_ge(S.sem("p_dv"), n)
                    i = t.matmul(ps[1][:, 0:2], k_ones, wacc[:1, 34:36],
                                 start=True, stop=True)
                    S.inc("p_mm", i)

                @P("vector")
                def _(v, n=S.val("p_mm"), wi=wi):
                    v.wait_ge(S.sem("p_mm"), n)
                    v.tensor_copy(k_sw[:, 2 * wi:2 * wi + 1], ps[1][:, 1:2])
                    v.tensor_copy(k_sw[:, 2 * wi + 1:2 * wi + 2], ps[1][:, 0:1])
                    S.vb(v)
                    S.inc("p_dv", v.engine_nop())

                for bk in range(nblk):
                    need = S.val("p_dv")
                    needsc = S.val("p_sc")
                    needmm = S.val("p_mm")

                    @P("sync")
                    def _(e, bk=bk, need=need, needsc=needsc, needmm=needmm,
                          wsrc=wsrc, cols=cols):
                        e.wait_ge(S.sem("p_dv"), need)
                        if needsc:
                            e.wait_ge(S.sem("p_sc"), needsc)
                        if needmm:
                            e.wait_ge(S.sem("p_mm"), needmm)
                        S.inc("p_ld", e.dma_start(
                            out=wbuf[:, :cols],
                            in_=wsrc[bk * 128:(bk + 1) * 128, :]), 16)

                    @P("vector")
                    def _(v, n=S.val("p_ld"), wi=wi, cols=cols):
                        v.wait_ge(S.sem("p_ld"), n)
                        v.tensor_scalar_mul(wbuf[:, :cols], wbuf[:, :cols],
                                            k_sw[:, 2 * wi:2 * wi + 1])
                        S.vb(v)
                        v.tensor_scalar(wbuf[:, :cols], wbuf[:, :cols],
                                        k_cr[:, 0:1], k_cr[:, 0:1],
                                        ALU.add, ALU.subtract)
                        S.vb(v)
                        v.tensor_scalar(wbuf[:, :cols], wbuf[:, :cols], -1.0, 1.0,
                                        ALU.max, ALU.min)
                        S.vb(v)
                        S.inc("p_dv", v.engine_nop())

                    nkc = cols // 128
                    base = S.val("p_mm")

                    @P("tensor")
                    def _(t, n=S.val("p_dv"), nkc=nkc, basc=S.val("p_sc")):
                        t.wait_ge(S.sem("p_dv"), n)
                        for kc in range(nkc):
                            if basc + kc - 1 > 0:
                                t.wait_ge(S.sem("p_sc"), basc + kc - 1)
                            i = t.transpose(ps[2 + (kc % 2)][:, 0:128],
                                            wbuf[:, kc * 128:(kc + 1) * 128], k_id128)
                            S.inc("p_mm", i)

                    @P("scalar")
                    def _(sc, base=base, nkc=nkc, bk=bk, wres=wres, rows=rows):
                        wv = wres.rearrange("p (k r) -> p k r", r=rows)
                        for kc in range(nkc):
                            sc.wait_ge(S.sem("p_mm"), base + kc + 1)
                            i = sc.copy(wv[:, kc, bk * 128:(bk + 1) * 128],
                                        ps[2 + (kc % 2)][:, 0:128])
                            S.inc("p_sc", i)

                @P("gpsimd")
                def _(g, n=S.val("p_sc"), wres=wres, dsp=dsp):
                    g.wait_ge(S.sem("p_sc"), n)
                    S.inc("p_out", g.dma_start(out=dsp, in_=wres), 16)
        return w1q, w2q

    def emit_fphase(tag, src_tok, src_fm, dst_fm, w1q, w2q, fscope,
                    wl=None, patch=None):
        _FB = {}
        """dst_fm = src_fm + 0.5*FFN(LN(src)). 1-tile stage skew."""
        sm = lambda n: ("f_" + n) if n not in () else (sn(n))
        sn = lambda n: tag + "_" + n
        _FB["s1r"] = S.val(sm("s1r"))
        _FB["mm1"] = S.val(sm("mm1"))
        xtok = S.sb(sn("xtok"), [128, NSUB * DM], scope=fscope)
        stats = S.sb(sn("stats"), [128, 16], scope=fscope)
        srow = S.sb(sn("srow"), [1, 2 * 272], scope=fscope)
        orow = S.sb(sn("orow"), [1, 2 * FT], scope=fscope)
        carr = S.sb(sn("carr"), [128, HD * FT], scope=fscope)
        hbuf = S.sb(sn("hbuf"), [128, 258], scope=fscope)
        s2s = S.sb(sn("s2s"), [128, 2 * FT], scope=fscope)
        i1s = S.sb(sn("i1s"), [128, 258], scope=fscope)
        rbuf = S.sb(sn("rbuf"), [128, FT], scope=fscope)
        obuf = S.sb(sn("obuf"), [128, FT], scope=fscope)
        maxc = S.sb(sn("maxc"), [128, 2 * FT], scope=fscope)
        xq = [S.sb(sn("xq%d" % i), [128, KD * 266], BF16, scope=fscope)
              for i in range(2)]
        cqa = S.sb(sn("cqa"), [128, HD * FT], BF16, scope=fscope)
        nbar = S.barrier(tag)
        for _eng in ("sync", "tensor", "scalar", "gpsimd", "vector"):
            @P(_eng)
            def _(e, nbar=nbar):
                e.wait_ge(S.sem("pb"), nbar)

        def stage1(j):
            c0 = j * FT
            pp = j % 2

            @P("sync")
            def _(e, j=j, c0=c0, n2=S.val(sm("st")), ntq=S.val(sm("tq"))):
                if j >= 1:
                    e.wait_ge(S.sem(sm("st")), n2)
                    e.wait_ge(S.sem(sm("tq")), ntq)
                xs = xtok.rearrange("p (s f) -> p s f", s=NSUB)

                def ld_rows(r0, r1):
                    t0 = r0 - c0
                    s_ = t0 // 128
                    p0 = t0 % 128
                    i = e.dma_start(out=xs[p0:p0 + (r1 - r0), s_:s_ + 1, :],
                                    in_=src_tok[r0:r1, :].rearrange(
                                        "(s p) f -> p s f", p=r1 - r0))
                    S.inc(sm("ld"), i, 16)
                    return i

                if patch is not None and j == 0:
                    ld_rows(8, 128)
                    ld_rows(128, 256)
                    e.wait_ge(patch["bd0"], 1)
                    ld_rows(0, 8)
                elif patch is not None and j == NFT - 1:
                    ld_rows(c0, OWN0 + TOWN)
                    ld_rows(OWN0 + TOWN + 1, c0 + 128)
                    ld_rows(c0 + 128, c0 + 256)
                    e.wait_ge(patch["bd1"], 1)
                    ld_rows(OWN0 + TOWN, OWN0 + TOWN + 1)
                else:
                    i = e.dma_start(out=xs,
                                    in_=src_tok[c0:c0 + FT, :].rearrange(
                                        "(s p) f -> p s f", p=128))
                    S.inc(sm("ld"), i, 16)
            nld = S.val(sm("ld"))

            @P("scalar")
            def _(sc, n=nld, ntq=S.val(sm("tq")), nxqe=S.val(sm("xqe"))):
                sc.wait_ge(S.sem(sm("ld")), n)
                if ntq:
                    sc.wait_ge(S.sem(sm("tq")), ntq)
                if nxqe:
                    sc.wait_ge(S.sem(sm("xqe")), nxqe)
                xs = xtok.rearrange("p (s f) -> p s f", s=NSUB)
                for s_ in range(NSUB):
                    if s_:
                        sc.wait_ge(S.sem(sm("sq")), S.val(sm("sq")))
                    for hf in range(2):
                        i = sc.activation(ps[4 + hf][:, 0:512],
                                          xs[:, s_, hf * 512:(hf + 1) * 512],
                                          ACT.Square,
                                          accum_out=stats[:, 2 * s_ + hf:2 * s_ + hf + 1])
                        S.inc(sm("sq"), i)
            nsq = S.val(sm("sq"))

            @P("vector")
            def _(v, n=nld, nsq=nsq, j=j):
                v.wait_ge(S.sem(sm("ld")), n)
                v.wait_ge(S.sem("c_pre"), S.val("c_pre"))
                xs = xtok.rearrange("p (s f) -> p s f", s=NSUB)
                for s_ in range(NSUB):
                    v.reduce_sum(stats[:, 4 + s_:5 + s_], xs[:, s_, :], AX)
                v.wait_ge(S.sem(sm("sq")), nsq)
                S.vb(v)
                v.tensor_add(stats[:, 0:1], stats[:, 0:1], stats[:, 1:2])
                S.vb(v)
                v.tensor_add(stats[:, 1:2], stats[:, 2:3], stats[:, 3:4])
                S.vb(v)
                v.tensor_copy(stats[:, 2:4], stats[:, 4:6])
                S.vb(v)
                v.tensor_scalar_mul(stats[:, 4:6], stats[:, 2:4], 1.0 / DM)
                v.tensor_scalar_mul(stats[:, 6:8], stats[:, 0:2], 1.0 / DM)
                S.vb(v)
                v.tensor_mul(stats[:, 8:10], stats[:, 4:6], stats[:, 4:6])
                S.vb(v)
                v.tensor_sub(stats[:, 6:8], stats[:, 6:8], stats[:, 8:10])
                S.vb(v)
                v.tensor_scalar_add(stats[:, 6:8], stats[:, 6:8], EPS)
                S.vb(v)
                v.reciprocal(stats[:, 8:10], stats[:, 6:8])
                S.vb(v)
                S.inc(sm("st"), v.engine_nop())

            @P("scalar")
            def _(sc, n=S.val(sm("st"))):
                sc.wait_ge(S.sem(sm("st")), n)
                i = sc.activation(stats[:, 10:12], stats[:, 8:10], ACT.Sqrt)
                S.inc(sm("sq"), i)

            @P("vector")
            def _(v, n=S.val(sm("sq")), j=j):
                v.wait_ge(S.sem(sm("sq")), n)
                v.tensor_mul(stats[:, 12:14], stats[:, 10:12], stats[:, 10:12])
                S.vb(v)
                v.tensor_mul(stats[:, 12:14], stats[:, 12:14], stats[:, 6:8])
                S.vb(v)
                v.tensor_scalar(stats[:, 12:14], stats[:, 12:14], -1.0, 3.0,
                                ALU.mult, ALU.add)
                S.vb(v)
                v.tensor_mul(stats[:, 12:14], stats[:, 12:14], stats[:, 10:12])
                S.vb(v)
                v.tensor_scalar_mul(stats[:, 10:12], stats[:, 12:14], 0.5)
                S.vb(v)
                xs = xtok.rearrange("p (s f) -> p s f", s=NSUB)
                for s_ in range(NSUB):
                    for hf in range(2):
                        v.tensor_scalar(ps[4 + hf][:, 0:512],
                                        xs[:, s_, hf * 512:(hf + 1) * 512],
                                        stats[:, 4 + s_:5 + s_], -1.0,
                                        ALU.subtract, ALU.mult)
                    S.vb(v)
                    v.reduce_max(stats[:, 12 + s_:13 + s_], ps[4][:, 0:512], AX,
                                 apply_absolute_value=True)
                    v.reduce_max(stats[:, 2:3], ps[5][:, 0:512], AX,
                                 apply_absolute_value=True)
                    S.vb(v)
                    v.tensor_max(stats[:, 12 + s_:13 + s_], stats[:, 12 + s_:13 + s_],
                                 stats[:, 2:3])
                    S.vb(v)
                v.tensor_mul(stats[:, 12:14], stats[:, 12:14], stats[:, 10:12])
                S.vb(v)
                v.tensor_scalar_max(stats[:, 12:14], stats[:, 12:14], 1e-5)
                S.vb(v)
                v.reciprocal(stats[:, 14:16], stats[:, 12:14])
                S.vb(v)
                v.tensor_scalar_mul(stats[:, 14:16], stats[:, 14:16], 127.0)
                S.vb(v)
                v.tensor_scalar_mul(stats[:, 12:14], stats[:, 12:14], 1.0 / 127.0)
                S.vb(v)
                v.tensor_scalar_mul(stats[:, 12:14], stats[:, 12:14], k_sw[:, 1:2])
                S.vb(v)
                v.tensor_mul(stats[:, 12:14], stats[:, 12:14],
                             k_vtok[:, j * NSUB:(j + 1) * NSUB])
                S.vb(v)
                v.tensor_mul(stats[:, 14:16], stats[:, 14:16], stats[:, 10:12])
                S.vb(v)
                v.tensor_mul(stats[:, 8:10], stats[:, 4:6], stats[:, 14:16])
                S.vb(v)
                v.tensor_scalar_mul(stats[:, 8:10], stats[:, 8:10], -1.0)
                S.vb(v)
                for s_ in range(NSUB):
                    for hf in range(2):
                        v.tensor_scalar(ps[4 + hf][:, 0:512],
                                        xs[:, s_, hf * 512:(hf + 1) * 512],
                                        stats[:, 14 + s_:15 + s_],
                                        stats[:, 8 + s_:9 + s_], ALU.mult, ALU.add)
                    S.vb(v)
                    for hf in range(2):
                        v.tensor_scalar(xs[:, s_, hf * 512:(hf + 1) * 512],
                                        ps[4 + hf][:, 0:512], k_cr[:, 0:1],
                                        k_cr[:, 0:1], ALU.add, ALU.subtract)
                    S.vb(v)
                S.inc(sm("st"), v.engine_nop())
            nst = S.val(sm("st"))

            baseT = S.val(sm("tq"))
            baseE = S.val(sm("xqe"))

            @P("tensor")
            def _(t, n=nst, baseT=baseT, baseE=baseE, j=j):
                t.wait_ge(S.sem(sm("st")), n)
                if j >= 2:
                    t.wait_ge(S.sem(sm("mm1")), _FB["mm1"] + (j - 1) * HD)
                xs = xtok.rearrange("p (s f) -> p s f", s=NSUB)
                k = 0
                for s_ in range(NSUB):
                    for kc in range(KD):
                        if baseE + k - 1 > 0:
                            t.wait_ge(S.sem(sm("xqe")), baseE + k - 1)
                        i = t.transpose(ps[4 + (k % 2)][:, 0:128],
                                        xs[:, s_, kc * 128:(kc + 1) * 128], k_id128)
                        S.inc(sm("tq"), i)
                        k += 1
                if j >= 1:
                    t.wait_ge(S.sem(sm("rwe")), S.val(sm("rwe")))
                for s_ in range(NSUB):
                    i = t.transpose(ps[7][:1, s_ * 128:(s_ + 1) * 128],
                                    stats[:, 12 + s_:13 + s_], k_id128)
                    S.inc(sm("tq"), i)

            @P("scalar")
            def _(sc, baseT=baseT, pp=pp, j=j):
                xv = xq[pp].rearrange("p (k t) -> p k t", k=KD)
                k = 0
                for s_ in range(NSUB):
                    for kc in range(KD):
                        sc.wait_ge(S.sem(sm("tq")), baseT + k + 1)
                        i = sc.copy(xv[:, kc, HALO + s_ * 128:HALO + (s_ + 1) * 128],
                                    ps[4 + (k % 2)][:, 0:128])
                        S.inc(sm("xqe"), i)
                        k += 1
                sc.wait_ge(S.sem(sm("tq")), baseT + k + NSUB)
                rr = srow.rearrange("o (b t) -> o b t", b=2)
                i = sc.copy(rr[:, j % 2, 8:8 + FT], ps[7][:1, 0:FT])
                S.inc(sm("rwe"), i)

            @P("vector")
            def _(v, n=S.val(sm("xqe")), nr=S.val(sm("rwe")), pp=pp, j=j):
                v.wait_ge(S.sem(sm("xqe")), n)
                v.wait_ge(S.sem(sm("rwe")), nr)
                xv = xq[pp].rearrange("p (k t) -> p k t", k=KD)
                rr = srow.rearrange("o (b t) -> o b t", b=2)
                if j == 0:
                    v.memset(xv[:, :, 0:HALO], 0.0)
                    v.memset(xq[0].rearrange("p (k t) -> p k t", k=KD)[:, :, 264:266], 0.0)
                    v.memset(xq[1].rearrange("p (k t) -> p k t", k=KD)[:, :, 264:266], 0.0)
                    v.memset(rr[:, 0, 0:8], 0.0)
                    v.memset(rr[:, 0, 264:272], 0.0)
                    v.memset(rr[:, 1, 264:272], 0.0)
                else:
                    xvp = xq[1 - pp].rearrange("p (k t) -> p k t", k=KD)
                    v.tensor_copy(xv[:, :, 0:HALO], xvp[:, :, FT:FT + HALO])
                    v.tensor_copy(rr[:, pp, 0:8], rr[:, 1 - pp, FT:FT + 8])
                    S.vb(v)
                    v.tensor_copy(xvp[:, :, 264:265], xv[:, :, HALO:HALO + 1])
                    v.tensor_copy(rr[:, 1 - pp, 264:265], rr[:, pp, 8:9])
                S.vb(v)
                S.inc(sm("s1r"), v.engine_nop())

        def stage2(j):
            c0 = j * FT
            pp = j % 2
            need_s1 = _FB["s1r"] + min(j + 2, NFT)

            nresb = S.val(sm("res"))

            @P("sync")
            def _(e, j=j, c0=c0, nresb=nresb):
                for oc in range(KD):
                    if nresb + oc > 0:
                        e.wait_ge(S.sem(sm("res")), nresb + oc)
                    i = e.dma_start(out=rbuf,
                                    in_=src_fm[oc, :, c0:c0 + FT])
                    S.inc(sm("ldf"), i, 16)
            nldf0 = S.val(sm("ldf")) - 16 * KD

            @P("tensor")
            def _(t, n1=need_s1, j=j, pp=pp, ns2e=S.val(sm("s2e"))):
                t.wait_ge(S.sem(sm("s1r")), n1)
                if ns2e:
                    t.wait_ge(S.sem(sm("s2e")), ns2e)
                rr = srow.rearrange("o (b t) -> o b t", b=2)
                i = t.matmul(ps[6][:, 0:258], k_ones, rr[:1, pp, 7:7 + 258],
                             start=True, stop=True)
                S.inc(sm("i1"), i)

            @P("scalar")
            def _(sc, n=S.val(sm("i1"))):
                sc.wait_ge(S.sem(sm("i1")), n)
                i = sc.copy(i1s, ps[6][:, 0:258])
                S.inc(sm("i1e"), i)

            basem = S.val(sm("mm1"))

            @P("tensor")
            def _(t, j=j, pp=pp, basem=basem, n1=need_s1):
                t.wait_ge(S.sem(sm("s1r")), n1)
                if j == 0:
                    if wl is None:
                        t.wait_ge(S.sem("p_out"), S.val("p_out"))
                    else:
                        t.wait_ge(S.sem(wl[0]), wl[1])
                xv = xq[pp].rearrange("p (k t) -> p k t", k=KD)
                w1v = w1q.rearrange("p (k r) -> p k r", r=DH)
                for hc in range(HD):
                    if basem + hc - 1 > 0:
                        t.wait_ge(S.sem(sm("cv")), basem + hc - 1)
                    i = None
                    for kc in range(KD):
                        i = t.matmul(ps[hc % 2][:, 0:258],
                                     w1v[:, kc, hc * 128:(hc + 1) * 128],
                                     xv[:, kc, HALO - 1:HALO + 257],
                                     start=(kc == 0), stop=(kc == KD - 1))
                    S.inc(sm("mm1"), i)

            @P("vector")
            def _(v, j=j, basem=basem, ni=S.val(sm("i1e")), nsi=S.val(sm("si"))):
                v.wait_ge(S.sem(sm("i1e")), ni)
                cwv = k_cw.rearrange("p (k c) -> p k c", k=HD)
                cbv = k_cb.rearrange("p (k c) -> p k c", k=HD)
                ca = carr.rearrange("p (h t) -> p h t", h=HD)
                basecv = S.val(sm("cv"))
                for hc in range(HD):
                    v.wait_ge(S.sem(sm("mm1")), basem + hc + 1)
                    if hc == 0 and j > 0:
                        v.wait_ge(S.sem(sm("si")), nsi)
                    if hc:
                        v.wait_ge(S.sem(sm("cv")), basecv + hc)
                    hb = hbuf
                    v.tensor_mul(hb[:, 0:258], ps[hc % 2][:, 0:258], i1s)
                    S.vb(v)
                    cc = ca[:, hc, :]
                    v.scalar_tensor_tensor(cc, hb[:, 0:256], cwv[:, hc, 0:1],
                                           bfree(cbv[:, hc, 0:1], FT), ALU.mult, ALU.add)
                    S.vb(v)
                    v.scalar_tensor_tensor(cc, hb[:, 1:257], cwv[:, hc, 1:2], cc,
                                           ALU.mult, ALU.add)
                    S.vb(v)
                    i = v.scalar_tensor_tensor(cc, hb[:, 2:258], cwv[:, hc, 2:3], cc,
                                               ALU.mult, ALU.add)
                    S.inc(sm("cv"), i)
            ncv = S.val(sm("cv"))

            @P("scalar")
            def _(sc, ncv=ncv):
                ca = carr.rearrange("p (h t) -> p h t", h=HD)
                for hc in range(HD):
                    sc.wait_ge(S.sem(sm("cv")), ncv - HD + hc + 1)
                    i = sc.activation(ca[:, hc, :], ca[:, hc, :], ACT.Silu)
                    S.inc(sm("si"), i)
            nsi = S.val(sm("si"))

            @P("vector")
            def _(v, nsi=nsi, j=j):
                ca = carr.rearrange("p (h t) -> p h t", h=HD)
                mxv = maxc.rearrange("p (b t) -> p b t", b=2)
                for hc in range(HD):
                    v.wait_ge(S.sem(sm("si")), nsi - HD + hc + 1)
                    if hc == 0:
                        v.tensor_copy(mxv[:, 0, :], ca[:, 0, :])
                        v.tensor_copy(mxv[:, 1, :], ca[:, 0, :])
                        S.vb(v)
                    else:
                        v.tensor_max(mxv[:, 0, :], mxv[:, 0, :], ca[:, hc, :])
                        v.tensor_tensor(mxv[:, 1, :], mxv[:, 1, :], ca[:, hc, :],
                                        ALU.min)
                        S.vb(v)
                v.tensor_scalar_mul(mxv[:, 1, :], mxv[:, 1, :], -1.0)
                S.vb(v)
                v.tensor_max(mxv[:, 0, :], mxv[:, 0, :], mxv[:, 1, :])
                S.vb(v)
                S.inc(sm("mx"), v.engine_nop())

            @P("tensor")
            def _(t, n=S.val(sm("mx")), nprev=S.val(sm("s2c"))):
                t.wait_ge(S.sem(sm("mx")), n)
                if nprev:
                    t.wait_ge(S.sem(sm("s2c")), nprev)
                for s_ in range(NSUB):
                    i = t.transpose(ps[5][:, s_ * 128:(s_ + 1) * 128],
                                    maxc.rearrange("p (b t) -> p b t", b=2)[:, 0, s_ * 128:(s_ + 1) * 128],
                                    k_id128)
                    S.inc(sm("mxt"), i)

            @P("vector")
            def _(v, n=S.val(sm("mxt"))):
                v.wait_ge(S.sem(sm("mxt")), n)
                for s_ in range(NSUB):
                    v.reduce_max(stats[:, 2 + s_:3 + s_],
                                 ps[5][:, s_ * 128:(s_ + 1) * 128], AX)
                S.vb(v)
                v.tensor_scalar_max(stats[:, 2:4], stats[:, 2:4], 1e-5)
                S.vb(v)
                v.reciprocal(stats[:, 4:6], stats[:, 2:4])
                S.vb(v)
                v.tensor_scalar_mul(stats[:, 4:6], stats[:, 4:6], 127.0)
                S.vb(v)
                v.tensor_scalar(stats[:, 6:8], stats[:, 2:4], 1.0 / 127.0, 0.5,
                                ALU.mult, ALU.mult)
                S.vb(v)
                v.tensor_scalar_mul(stats[:, 6:8], stats[:, 6:8], k_sw[:, 3:4])
                S.vb(v)
                S.inc(sm("s2c"), v.engine_nop())

            @P("tensor")
            def _(t, n=S.val(sm("s2c")), ni1e=S.val(sm("i1e"))):
                t.wait_ge(S.sem(sm("s2c")), n)
                t.wait_ge(S.sem(sm("i1e")), ni1e)
                k = 0
                for col in (4, 5, 6, 7):
                    i = t.transpose(ps[6][:1, k * 128:(k + 1) * 128],
                                    stats[:, col:col + 1], k_id128)
                    S.inc(sm("mxt"), i)
                    k += 1

            @P("scalar")
            def _(sc, n=S.val(sm("mxt"))):
                sc.wait_ge(S.sem(sm("mxt")), n)
                i = sc.copy(orow, ps[6][:1, 0:512])
                S.inc(sm("rw2"), i)
            nrw2 = S.val(sm("rw2"))

            @P("tensor")
            def _(t, n=nrw2):
                t.wait_ge(S.sem(sm("rw2")), n)
                i = t.matmul(ps[6][:, 0:512], k_ones, orow, start=True, stop=True)
                S.inc(sm("i1"), i)

            @P("scalar")
            def _(sc, n=S.val(sm("i1"))):
                sc.wait_ge(S.sem(sm("i1")), n)
                i = sc.copy(s2s, ps[6][:, 0:512])
                S.inc(sm("s2e"), i)

            @P("vector")
            def _(v, n=S.val(sm("s2e"))):
                v.wait_ge(S.sem(sm("s2e")), n)
                cav = carr.rearrange("p (h t) -> p h t", h=HD)
                s2m = bass.AP(s2s.tensor, s2s.offset,
                              [list(s2s.ap[0]), [0, HD], [1, FT]])
                v.tensor_tensor(cav, cav, s2m, ALU.mult)
                S.vb(v)
                v.tensor_scalar(cqa.rearrange("p (h t) -> p h t", h=HD), cav,
                                k_cr[:, 0:1], k_cr[:, 0:1], ALU.add, ALU.subtract)
                S.vb(v)
                S.inc(sm("q2"), v.engine_nop())
            nq2 = S.val(sm("q2"))

            basem2 = S.val(sm("mm2"))
            resbase = S.val(sm("res"))

            @P("tensor")
            def _(t, nq2=nq2, basem2=basem2, resbase=resbase):
                t.wait_ge(S.sem(sm("q2")), nq2)
                cv = cqa.rearrange("p (h t) -> p h t", h=HD)
                w2v = w2q.rearrange("p (k r) -> p k r", r=DM)
                for oc in range(KD):
                    if resbase + oc - 1 > 0:
                        t.wait_ge(S.sem(sm("res")), resbase + oc - 1)
                    i = None
                    for hc in range(HD):
                        i = t.matmul(ps[2 + (oc % 2)][:, 0:FT],
                                     w2v[:, hc, oc * 128:(oc + 1) * 128],
                                     cv[:, hc, :],
                                     start=(hc == 0), stop=(hc == HD - 1))
                    S.inc(sm("mm2"), i)

            nspb = S.val(sm("sp"))

            @P("vector")
            def _(v, basem2=basem2, nldf0=nldf0, j=j, c0=c0, nspb=nspb):
                s2i = bass.AP(s2s.tensor, s2s.offset + FT,
                              [list(s2s.ap[0]), [1, FT]])
                for oc in range(KD):
                    v.wait_ge(S.sem(sm("mm2")), basem2 + oc + 1)
                    v.wait_ge(S.sem(sm("ldf")), nldf0 + 16 * (oc + 1))
                    if nspb + 16 * oc > 0:
                        v.wait_ge(S.sem(sm("sp")), nspb + 16 * oc)
                    v.tensor_mul(obuf, ps[2 + (oc % 2)][:, 0:FT], s2i)
                    S.vb(v)
                    i = v.tensor_add(obuf, obuf, rbuf)
                    S.vb(v)
                    S.inc(sm("res"), i)
            nres = S.val(sm("res"))

            @P("gpsimd")
            def _(g, nres0=nres - KD, j=j, c0=c0):
                for oc in range(KD):
                    g.wait_ge(S.sem(sm("res")), nres0 + oc + 1)
                    i = g.dma_start(out=dst_fm[oc, :, c0:c0 + FT],
                                    in_=obuf)
                    S.inc(sm("sp"), i, 16)

        for j in range(NFT):
            stage1(j)
            if j >= 1:
                stage2(j - 1)
        stage2(NFT - 1)

    def emit_a1a(ascope):
        sn = lambda n: "aa_" + n
        wb_ = {nm: S.sb(sn(nm), [128, KD * DM], scope=ascope)
               for nm in ("wq", "wk", "wv")}
        xfm = [S.sb(sn("xfm%d" % i), [128, KD * FT], scope=ascope) for i in range(2)]
        xnf = [S.sb(sn("xnf%d" % i), [128, KD * FT], scope=ascope) for i in range(2)]
        xtok = S.sb(sn("xtok"), [128, NSUB * DM], scope=ascope)
        stats = S.sb(sn("stats"), [128, 16], scope=ascope)
        srow = S.sb(sn("srow"), [1, 512], scope=ascope)
        cbuf = {nm: S.sb(sn("cb" + nm), [128, H * 259], scope=ascope)
                for nm in "qkv"}
        obuf = {nm: S.sb(sn("ob" + nm), [128, H * FT], scope=ascope)
                for nm in "qkv"}
        nbar = S.barrier("aa")
        for _eng in ("sync", "tensor", "scalar", "gpsimd", "vector"):
            @P(_eng)
            def _(e, nbar=nbar):
                e.wait_ge(S.sem("pb"), nbar)

        @P("sync")
        def _(e):
            for nm, src in (("wq", wpi["wq"]), ("wk", wpi["wk"]), ("wv", wpi["wv"])):
                i = e.dma_start(
                    out=wb_[nm].rearrange("p (k o) -> p k o", k=KD),
                    in_=src.rearrange("(k p) o -> p k o", p=128))
                S.inc(sn("wl"), i, 16)
        nwl = S.val(sn("wl"))

        @P("vector")
        def _(v):
            for nm in "qkv":
                v.memset(cbuf[nm].rearrange("p (h c) -> p h c", h=H)[:, :, 0:3], 0.0)
            S.vb(v)
            S.inc(sn("hz"), v.engine_nop())

        def stage1(j):
            c0 = j * FT
            pp = j % 2

            @P("sync")
            def _(e, j=j, c0=c0, pp=pp, ntq=S.val(sn("tq")), nxn=S.val(sn("xnr")),
                  nld0=S.val(sn("ld"))):
                if j >= 2:
                    e.wait_ge(S.sem(sn("tq")), ntq)
                    e.wait_ge(S.sem(sn("xnr")), nxn)
                if j >= 1:
                    e.wait_ge(S.sem(sn("ld")), nld0)
                i = e.dma_start(
                    out=xfm[pp].rearrange("p (k t) -> p k t", k=KD),
                    in_=d_xmid[:, :, c0:c0 + FT].rearrange("k p t -> p k t"))
                S.inc(sn("ld"), i, 16)
            nld = S.val(sn("ld"))

            baseT = S.val(sn("tq"))
            baseE = S.val(sn("xte"))

            @P("tensor")
            def _(t, j=j, pp=pp, nld=nld, baseT=baseT, baseE=baseE,
                  nsq=S.val(sn("sq")), nrw=S.val(sn("rw"))):
                t.wait_ge(S.sem(sn("ld")), nld)
                if nsq:
                    t.wait_ge(S.sem(sn("sq")), nsq)
                xv = xfm[pp].rearrange("p (k t) -> p k t", k=KD)
                k = 0
                for s_ in range(NSUB):
                    for kc in range(KD):
                        if baseE + k - 1 > 0:
                            t.wait_ge(S.sem(sn("xte")), baseE + k - 1)
                        i = t.transpose(ps[4 + (k % 2)][:, 0:128],
                                        xv[:, kc, s_ * 128:(s_ + 1) * 128], k_id128)
                        S.inc(sn("tq"), i)
                        k += 1

            @P("scalar")
            def _(sc, baseT=baseT, j=j):
                xs = xtok.rearrange("p (s f) -> p s f", s=NSUB)
                k = 0
                for s_ in range(NSUB):
                    for kc in range(KD):
                        sc.wait_ge(S.sem(sn("tq")), baseT + k + 1)
                        if k == 0 and j >= 1:
                            sc.wait_ge(S.sem(sn("st")), S.val(sn("st")))
                        i = sc.copy(xs[:, s_, kc * 128:(kc + 1) * 128],
                                    ps[4 + (k % 2)][:, 0:128])
                        S.inc(sn("xte"), i)
                        k += 1
            nxte = S.val(sn("xte"))

            @P("scalar")
            def _(sc, n=nxte):
                sc.wait_ge(S.sem(sn("xte")), n)
                xs = xtok.rearrange("p (s f) -> p s f", s=NSUB)
                for s_ in range(NSUB):
                    if s_:
                        sc.wait_ge(S.sem(sn("sq")), S.val(sn("sq")))
                    for hf in range(2):
                        i = sc.activation(ps[4 + hf][:, 0:512],
                                          xs[:, s_, hf * 512:(hf + 1) * 512],
                                          ACT.Square,
                                          accum_out=stats[:, 2 * s_ + hf:2 * s_ + hf + 1])
                        S.inc(sn("sq"), i)
            nsq = S.val(sn("sq"))

            @P("vector")
            def _(v, n=nxte, nsq=nsq, j=j):
                v.wait_ge(S.sem(sn("xte")), n)
                v.wait_ge(S.sem("c_pre"), S.val("c_pre"))
                xs = xtok.rearrange("p (s f) -> p s f", s=NSUB)
                for s_ in range(NSUB):
                    v.reduce_sum(stats[:, 4 + s_:5 + s_], xs[:, s_, :], AX)
                v.wait_ge(S.sem(sn("sq")), nsq)
                S.vb(v)
                v.tensor_add(stats[:, 0:1], stats[:, 0:1], stats[:, 1:2])
                S.vb(v)
                v.tensor_add(stats[:, 1:2], stats[:, 2:3], stats[:, 3:4])
                S.vb(v)
                v.tensor_scalar_mul(stats[:, 4:6], stats[:, 4:6], 1.0 / DM)
                v.tensor_scalar_mul(stats[:, 6:8], stats[:, 0:2], 1.0 / DM)
                S.vb(v)
                v.tensor_mul(stats[:, 8:10], stats[:, 4:6], stats[:, 4:6])
                S.vb(v)
                v.tensor_sub(stats[:, 6:8], stats[:, 6:8], stats[:, 8:10])
                S.vb(v)
                v.tensor_scalar_add(stats[:, 6:8], stats[:, 6:8], EPS)
                S.vb(v)
                v.reciprocal(stats[:, 8:10], stats[:, 6:8])
                S.vb(v)
                S.inc(sn("st"), v.engine_nop())

            @P("scalar")
            def _(sc, n=S.val(sn("st"))):
                sc.wait_ge(S.sem(sn("st")), n)
                i = sc.activation(stats[:, 10:12], stats[:, 8:10], ACT.Sqrt)
                S.inc(sn("sq"), i)

            @P("vector")
            def _(v, n=S.val(sn("sq")), j=j):
                v.wait_ge(S.sem(sn("sq")), n)
                v.tensor_mul(stats[:, 12:14], stats[:, 10:12], stats[:, 10:12])
                S.vb(v)
                v.tensor_mul(stats[:, 12:14], stats[:, 12:14], stats[:, 6:8])
                S.vb(v)
                v.tensor_scalar(stats[:, 12:14], stats[:, 12:14], -1.0, 3.0,
                                ALU.mult, ALU.add)
                S.vb(v)
                v.tensor_mul(stats[:, 12:14], stats[:, 12:14], stats[:, 10:12])
                S.vb(v)
                v.tensor_scalar_mul(stats[:, 10:12], stats[:, 12:14], 0.5)
                S.vb(v)
                v.tensor_mul(stats[:, 12:14], stats[:, 10:12],
                             k_vtok[:, j * NSUB:(j + 1) * NSUB])
                S.vb(v)
                S.inc(sn("ct"), v.engine_nop())

            @P("tensor")
            def _(t, n=S.val(sn("ct")), nrw=S.val(sn("rw"))):
                t.wait_ge(S.sem(sn("ct")), n)
                if nrw:
                    t.wait_ge(S.sem(sn("rw")), nrw)
                k = 0
                for col in (4, 5, 12, 13):
                    i = t.transpose(ps[7][:1, k * 128:(k + 1) * 128],
                                    stats[:, col:col + 1], k_id128)
                    S.inc(sn("tq"), i)
                    k += 1

            @P("scalar")
            def _(sc, n=S.val(sn("tq")), ni1=S.val(sn("i1"))):
                sc.wait_ge(S.sem(sn("tq")), n)
                if ni1:
                    sc.wait_ge(S.sem(sn("i1")), ni1)
                i = sc.copy(srow, ps[7][:1, 0:512])
                S.inc(sn("rw"), i)

            @P("tensor")
            def _(t, n=S.val(sn("rw")), nxn=S.val(sn("xnr"))):
                t.wait_ge(S.sem(sn("rw")), n)
                if nxn:
                    t.wait_ge(S.sem(sn("xnr")), nxn)
                i = t.matmul(ps[6][:, 0:512], k_ones, srow, start=True, stop=True)
                S.inc(sn("i1"), i)

            @P("vector")
            def _(v, n=S.val(sn("i1")), pp=pp, nld=nld,
                  nxsp=max(0, S.val(sn("xsp")) - 16)):
                v.wait_ge(S.sem(sn("i1")), n)
                v.wait_ge(S.sem(sn("ld")), nld)
                if nxsp:
                    v.wait_ge(S.sem(sn("xsp")), nxsp)
                murep = bass.AP(ps[6].tensor, ps[6].offset,
                                [list(ps[6].ap[0]), [0, KD], [1, FT]])
                rsrep = bass.AP(ps[6].tensor, ps[6].offset + FT,
                                [list(ps[6].ap[0]), [0, KD], [1, FT]])
                xv = xfm[pp].rearrange("p (k t) -> p k t", k=KD)
                nv = xnf[pp].rearrange("p (k t) -> p k t", k=KD)
                v.tensor_tensor(nv, xv, murep, ALU.subtract)
                S.vb(v)
                v.tensor_tensor(nv, nv, rsrep, ALU.mult)
                S.vb(v)
                i = v.engine_nop()
                S.inc(sn("xnr"), i)
            nxnr = S.val(sn("xnr"))

            @P("gpsimd")
            def _(g, n=nxnr, pp=pp, c0=c0, nxsp0=S.val(sn("xsp"))):
                g.wait_ge(S.sem(sn("xnr")), n)
                if nxsp0:
                    g.wait_ge(S.sem(sn("xsp")), nxsp0)
                i = g.dma_start(
                    out=d_xn[:, :, c0:c0 + FT].rearrange("k p t -> p k t"),
                    in_=xnf[pp].rearrange("p (k t) -> p k t", k=KD))
                S.inc(sn("xsp"), i, 16)

        def stage2(j):
            c0 = j * FT
            pp = j % 2
            basem = S.val(sn("mm"))
            basee = S.val(sn("ev"))

            @P("tensor")
            def _(t, j=j, pp=pp, basem=basem, basee=basee,
                  nxn=S.val(sn("xnr")), nwl=nwl):
                t.wait_ge(S.sem(sn("xnr")), nxn)
                if j == 0:
                    t.wait_ge(S.sem(sn("wl")), nwl)
                nv = xnf[pp].rearrange("p (k t) -> p k t", k=KD)
                r = 0
                for nm in ("wq", "wk", "wv"):
                    wv = wb_[nm].rearrange("p (k o) -> p k o", k=KD)
                    for oc in range(KD):
                        if basee + r - 1 > 0:
                            t.wait_ge(S.sem(sn("ev")), basee + r - 1)
                        i = None
                        for fc in range(KD):
                            i = t.matmul(ps[r % 2][:, 0:FT],
                                         wv[:, fc, oc * 128:(oc + 1) * 128],
                                         nv[:, fc, :],
                                         start=(fc == 0), stop=(fc == KD - 1))
                        S.inc(sn("mm"), i)
                        r += 1

            @P("scalar")
            def _(sc, j=j, basem=basem, ncr=S.val(sn("cr")), nhz=S.val(sn("hz"))):
                r = 0
                sc.wait_ge(S.sem(sn("hz")), nhz)
                for nm in "qkv":
                    cv = cbuf[nm].rearrange("p (h c) -> p h c", h=H)
                    for oc in range(KD):
                        sc.wait_ge(S.sem(sn("mm")), basem + r + 1)
                        if j >= 1:
                            sc.wait_ge(S.sem(sn("cr")), ncr - 24 + r + 1)
                        i = sc.copy(cv[:, oc, 3:259], ps[r % 2][:, 0:FT])
                        S.inc(sn("ev"), i)
                        r += 1

            @P("vector")
            def _(v, j=j, basee=basee, nsp=S.val(sn("sp"))):
                r = 0
                for nm in "qkv":
                    cv = cbuf[nm].rearrange("p (h c) -> p h c", h=H)
                    ov = obuf[nm].rearrange("p (h t) -> p h t", h=H)
                    cw = k_aconv[nm][0].rearrange("p (h c) -> p h c", h=H)
                    cb2 = k_aconv[nm][1].rearrange("p (h c) -> p h c", h=H)
                    for oc in range(KD):
                        v.wait_ge(S.sem(sn("ev")), basee + r + 1)
                        if j >= 1 and r == 0:
                            v.wait_ge(S.sem(sn("sp")), nsp)
                        cc = ov[:, oc, :]
                        v.scalar_tensor_tensor(cc, cv[:, oc, 0:FT], cw[:, oc, 0:1],
                                               bfree(cb2[:, oc, 0:1], FT),
                                               ALU.mult, ALU.add)
                        S.vb(v)
                        for jj in range(1, 4):
                            v.scalar_tensor_tensor(cc, cv[:, oc, jj:jj + FT],
                                                   cw[:, oc, jj:jj + 1], cc,
                                                   ALU.mult, ALU.add)
                            S.vb(v)
                        v.tensor_copy(cv[:, oc, 0:3], cv[:, oc, 256:259])
                        S.vb(v)
                        i = v.engine_nop()
                        S.inc(sn("cr"), i)
                        r += 1

            @P("scalar")
            def _(sc, basec=S.val(sn("cr")) - 24):
                r = 0
                for nm in "qkv":
                    ov = obuf[nm].rearrange("p (h t) -> p h t", h=H)
                    for oc in range(KD):
                        sc.wait_ge(S.sem(sn("cr")), basec + r + 1)
                        i = sc.activation(ov[:, oc, :], ov[:, oc, :], ACT.Silu)
                        S.inc(sn("si"), i)
                        r += 1
            nsi = S.val(sn("si"))

            @P("vector")
            def _(v, nsi=nsi):
                v.wait_ge(S.sem(sn("si")), nsi - 8)
                i = v.tensor_scalar_mul(obuf["k"], obuf["k"], 128 ** -0.5)
                S.vb(v)
                S.inc(sn("ks"), v.engine_nop())
            nks = S.val(sn("ks"))

            @P("gpsimd")
            def _(g, j=j, c0=c0, nsi=nsi, nks=nks):
                for nm, dst in (("q", d_q), ("k", d_k), ("v", d_v)):
                    if nm == "k":
                        g.wait_ge(S.sem(sn("ks")), nks)
                    elif nm == "q":
                        g.wait_ge(S.sem(sn("si")), nsi - 16)
                    else:
                        g.wait_ge(S.sem(sn("si")), nsi)
                    i = g.dma_start(
                        out=dst[:, :, c0:c0 + FT].rearrange("h p t -> p h t"),
                        in_=obuf[nm].rearrange("p (h t) -> p h t", h=H))
                    S.inc(sn("sp"), i, 16)

        for j in range(NFT):
            stage1(j)
            if j >= 1:
                stage2(j - 1)
        stage2(NFT - 1)

    def emit_a1b(bscope):
        sm = lambda n: ("aa_" + n) if n not in ('bm',) else (sn(n))
        sn = lambda n: "ab_" + n
        wb_ = {nm: S.sb(sn(nm), [128, KD * DM], scope=bscope)
               for nm in ("wa", "wg")}
        wbb = S.sb(sn("wbb"), [128, KD * H], scope=bscope)
        xnf = [S.sb(sn("xnf%d" % i), [128, KD * FT], scope=bscope) for i in range(2)]
        obuf = {nm: S.sb(sn("ob" + nm), [128, H * FT], scope=bscope)
                for nm in ("a", "g")}
        bbuf = S.sb(sn("bbuf"), [H, FT], scope=bscope)
        svr = S.sb(sn("svr"), [1, FT], scope=bscope)
        nbar = S.barrier("ab")
        for _eng in ("sync", "tensor", "scalar", "gpsimd", "vector"):
            @P(_eng)
            def _(e, nbar=nbar):
                e.wait_ge(S.sem("pb"), nbar)

        @P("sync")
        def _(e):
            for nm, src in (("wa", wpi["wa"]), ("wg", wpi["wg"])):
                i = e.dma_start(
                    out=wb_[nm].rearrange("p (k o) -> p k o", k=KD),
                    in_=src.rearrange("(k p) o -> p k o", p=128))
                S.inc(sm("wl"), i, 16)
            i = e.dma_start(out=wbb.rearrange("p (k o) -> p k o", k=KD),
                            in_=wbi.rearrange("(k p) o -> p k o", p=128))
            S.inc(sm("wl"), i, 16)
        nwl = S.val(sm("wl"))

        def tile(j):
            c0 = j * FT
            pp = j % 2

            @P("sync")
            def _(e, j=j, c0=c0, pp=pp, nmm=S.val(sm("mm")), nld0=S.val(sm("ld"))):
                if j >= 2:
                    e.wait_ge(S.sem(sm("mm")), nmm)
                if j >= 1:
                    e.wait_ge(S.sem(sm("ld")), nld0)
                i = e.dma_start(
                    out=xnf[pp].rearrange("p (k t) -> p k t", k=KD),
                    in_=d_xn[:, :, c0:c0 + FT].rearrange("k p t -> p k t"))
                S.inc(sm("ld"), i, 16)
                if j >= 1:
                    e.wait_ge(S.sem(sm("mm")), S.val(sm("mm")))
                i = e.dma_start(out=svr, in_=vrowi[:, c0:c0 + FT])
                S.inc(sm("ld"), i, 16)
            nld = S.val(sm("ld"))

            basem = S.val(sm("mm"))
            basee = S.val(sm("ev"))

            @P("tensor")
            def _(t, j=j, pp=pp, nld=nld, basem=basem, basee=basee, nwl=nwl,
                  nbm0=S.val(sm("bm"))):
                t.wait_ge(S.sem(sm("ld")), nld)
                if j == 0:
                    t.wait_ge(S.sem(sm("wl")), nwl)
                nv = xnf[pp].rearrange("p (k t) -> p k t", k=KD)
                r = 0
                for nm in ("wa", "wg"):
                    wv = wb_[nm].rearrange("p (k o) -> p k o", k=KD)
                    for oc in range(KD):
                        if basee + r - 1 > 0:
                            t.wait_ge(S.sem(sm("ev")), basee + r - 1)
                        i = None
                        for fc in range(KD):
                            i = t.matmul(ps[r % 2][:, 0:FT],
                                         wv[:, fc, oc * 128:(oc + 1) * 128],
                                         nv[:, fc, :],
                                         start=(fc == 0), stop=(fc == KD - 1))
                        S.inc(sm("mm"), i)
                        r += 1
                if basee:
                    t.wait_ge(S.sem(sm("ev")), basee)
                wv = wbb.rearrange("p (k o) -> p k o", k=KD)
                i = None
                for fc in range(KD):
                    i = t.matmul(ps[2][0:H, 0:FT], wv[:, fc, :], nv[:, fc, :],
                                 start=(fc == 0), stop=(fc == KD - 1))
                S.inc(sm("mm"), i)
                if nbm0:
                    t.wait_ge(S.sem(sm("bm")), nbm0)
                i = t.matmul(ps[3][0:H, 0:FT], k_ones[:, 0:H], svr,
                             start=True, stop=True)
                S.inc(sm("mm"), i)

            @P("scalar")
            def _(sc, j=j, basem=basem, nsp=S.val(sm("sp"))):
                r = 0
                for nm, bias in (("a", k_ba), ("g", None)):
                    ov = obuf[nm].rearrange("p (h t) -> p h t", h=H)
                    for oc in range(KD):
                        sc.wait_ge(S.sem(sm("mm")), basem + r + 1)
                        if j >= 1 and r == 0:
                            sc.wait_ge(S.sem(sm("sp")), nsp)
                        if bias is not None:
                            i = sc.activation(ov[:, oc, :], ps[r % 2][:, 0:FT],
                                              ACT.Sigmoid,
                                              bias=bias[:, oc:oc + 1])
                        else:
                            i = sc.activation(ov[:, oc, :], ps[r % 2][:, 0:FT],
                                              ACT.Sigmoid)
                        S.inc(sm("ev"), i)
                        r += 1
                sc.wait_ge(S.sem(sm("mm")), basem + 18)
                i = sc.activation(bbuf, ps[2][0:H, 0:FT], ACT.Sigmoid,
                                  bias=k_bb[:, 0:1])
                S.inc(sm("ev"), i)

            @P("vector")
            def _(v, basee=basee):
                v.wait_ge(S.sem(sm("ev")), basee + 17)
                i = v.tensor_mul(bbuf, bbuf, ps[3][0:H, 0:FT])
                S.vb(v)
                S.inc(sm("bm"), v.engine_nop())

            @P("gpsimd")
            def _(g, j=j, c0=c0, basee=basee, nbm=S.val(sm("bm"))):
                for nm, dst, nev in (("a", d_a, basee + 8), ("g", d_g, basee + 16)):
                    g.wait_ge(S.sem(sm("ev")), nev)
                    i = g.dma_start(
                        out=dst[:, :, c0:c0 + FT].rearrange("h p t -> p h t"),
                        in_=obuf[nm].rearrange("p (h t) -> p h t", h=H))
                    S.inc(sm("sp"), i, 16)
                g.wait_ge(S.sem(sm("bm")), nbm)
                i = g.dma_start(out=d_beta[:, c0:c0 + FT], in_=bbuf)
                S.inc(sm("sp"), i, 16)

        for j in range(NFT):
            tile(j)

    def emit_scan(cscope):
        sn = lambda n: "sc_" + n
        NCH = nch
        NB = 4

        Sv = S.sb(sn("Sv"), [128, H * 128], scope=cscope)
        Sin = S.sb(sn("Sin"), [128, H * 128], scope=cscope)
        Sscr = S.sb(sn("Sscr"), [128, H * 128], scope=cscope)
        qa = [S.sb(sn("qa%d" % i), [128, 256], scope=cscope) for i in range(2)]
        ka = [S.sb(sn("ka%d" % i), [128, 256], scope=cscope) for i in range(2)]
        va = [S.sb(sn("va%d" % i), [128, 256], scope=cscope) for i in range(2)]
        aa = [S.sb(sn("aa%d" % i), [128, 256], scope=cscope) for i in range(2)]
        br = [S.sb(sn("br%d" % i), [1, 256], scope=cscope) for i in range(2)]
        Ab = S.sb(sn("A"), [128, 256], scope=cscope)
        Aip = S.sb(sn("Aip"), [128, 256], scope=cscope)
        rho = S.sb(sn("rho"), [128, 256], scope=cscope)
        Wb = S.sb(sn("W"), [128, 256], scope=cscope)
        Acc = S.sb(sn("Acc"), [128, 256], scope=cscope)
        tmp = S.sb(sn("tmp"), [128, 256], scope=cscope)
        sd0 = S.sb(sn("sd0"), [128, 256], scope=cscope)
        sd1 = S.sb(sn("sd1"), [128, 256], scope=cscope)
        ovb = S.sb(sn("ovb"), [128, 256], scope=cscope)
        Gs = S.sb(sn("Gs"), [CH, 256], scope=cscope)
        Kqs = S.sb(sn("Kqs"), [CH, 256], scope=cscope)
        KT = S.sb(sn("KT"), [CH, H * 128], scope=cscope)
        WT = S.sb(sn("WT"), [CH, H * 128], scope=cscope)
        mU = S.sb(sn("mU"), [CH, CH], scope=cscope)
        mUI = S.sb(sn("mUI"), [CH, CH], scope=cscope)
        nones = S.sb(sn("nones"), [1, 128], scope=cscope)

        nbar = S.barrier("sc")
        for _eng in ("sync", "tensor", "scalar", "gpsimd", "vector"):
            @P(_eng)
            def _(e, nbar=nbar):
                e.wait_ge(S.sem("pb"), nbar)

        @P("vector")
        def _(v):
            v.memset(Sv, 0.0)
            v.memset(Sin, 0.0)
            v.memset(nones, -1.0)
            v.tensor_scalar(mU, k_tri, -1.0, 1.0, ALU.mult, ALU.add)
            S.vb(v)
            v.tensor_add(mUI, mU, k_id32)
            S.vb(v)
            S.inc(sn("mz"), v.engine_nop())

        def one_pass(passid):
            if passid == "B" and not single:
                @P("vector")
                def _(v, nrd=S.val(sn("rd"))):
                    v.wait_ge(S.sem(sn("rd")), nrd)
                    v.tensor_scalar_mul(Sin, Sin, k_cfg[:, 0:1])
                    S.vb(v)
                    S.inc(sn("mz"), v.engine_nop())
            nmz = S.val(sn("mz"))

            @P("tensor")
            def _(t, nmz=nmz, nse0=S.val(sn("se")), ncc=S.val(sn("cc"))):
                t.wait_ge(S.sem(sn("mz")), nmz)
                if ncc:
                    t.wait_ge(S.sem(sn("cc")), ncc)
                for h in range(H):
                    i = t.transpose(ps[2 + h // 4][:, (h % 4) * 128:(h % 4) * 128 + 128],
                                    Sin[:, h * 128:(h + 1) * 128], k_id128)
                    S.inc(sn("svt"), i)

            @P("scalar")
            def _(sc, n=S.val(sn("svt")), nmz=nmz):
                sc.wait_ge(S.sem(sn("svt")), n)
                sc.wait_ge(S.sem(sn("mz")), nmz)
                for half in range(2):
                    i = sc.copy(Sv[:, half * 512:(half + 1) * 512],
                                ps[2 + half][:, 0:512])
                    S.inc(sn("se"), i)
            nsvi = S.val(sn("se"))

            ld_base = S.val(sn("ld"))

            def loads(c, gate):
                tc0 = 8 + 32 * c
                rc = c % 2

                @P("sync")
                def _(e, c=c, tc0=tc0, rc=rc, gate=gate, nld0=S.val(sn("ld"))):
                    for s_, v_ in gate:
                        e.wait_ge(S.sem(s_), v_)
                    if nld0:
                        e.wait_ge(S.sem(sn("ld")), nld0)
                    for buf, src in ((qa[rc], d_q), (ka[rc], d_k),
                                     (va[rc], d_v), (aa[rc], d_a)):
                        i = e.dma_start(
                            out=buf.rearrange("p (h t) -> p h t", h=H),
                            in_=src[:, :, tc0:tc0 + 32].rearrange("h p t -> p h t"))
                        S.inc(sn("ld"), i, 16)
                    i = e.dma_start(out=br[rc].rearrange("o (h t) -> o h t", h=H),
                                    in_=d_beta[:, tc0:tc0 + 32])
                    S.inc(sn("ld"), i, 16)

            def chunk(c):
                tc0 = 8 + 32 * c
                rc = c % 2
                stat = Sin if c == 0 else Sscr
                nld = ld_base + 80 * (c + 1)

                @P("tensor")
                def _(t, c=c, rc=rc, nld=nld, ngm=S.val(sn("gm")),
                      nse=(nsvi if c == 0 else S.val(sn("se"))),
                      nob=S.val(sn("ob")), nrh=S.val(sn("rh")),
                      ndel=S.val(sn("del"))):
                    t.wait_ge(S.sem(sn("ld")), nld)
                    if c >= 1:
                        t.wait_ge(S.sem(sn("gm")), ngm)
                    for h in range(H):
                        i = t.matmul(ps[1][0:CH, h * 32:(h + 1) * 32],
                                     ka[rc][:, h * 32:(h + 1) * 32],
                                     ka[rc][:, h * 32:(h + 1) * 32],
                                     start=True, stop=True)
                        S.inc(sn("gk"), i)
                        i = t.matmul(ps[1][0:CH, 256 + h * 32:256 + h * 32 + 32],
                                     ka[rc][:, h * 32:(h + 1) * 32],
                                     qa[rc][:, h * 32:(h + 1) * 32],
                                     start=True, stop=True)
                        S.inc(sn("gk"), i)
                    t.wait_ge(S.sem(sn("se")), nse)
                    for h in range(H):
                        i = t.transpose(ps[2 + h // 4][0:CH,
                                                       (h % 4) * 128:(h % 4) * 128 + 128],
                                        ka[rc][:, h * 32:(h + 1) * 32], k_id128)
                        S.inc(sn("kt"), i)
                    if c == 0:
                        t.wait_ge(S.sem(sn("se")), nsvi)
                        if nob:
                            t.wait_ge(S.sem(sn("ob")), nob)
                    elif nob:
                        t.wait_ge(S.sem(sn("ob")), nob)
                    for h in range(H):
                        i = t.matmul(ps[0][:, h * 64:h * 64 + 32],
                                     stat[:, h * 128:(h + 1) * 128],
                                     ka[rc][:, h * 32:(h + 1) * 32],
                                     start=True, stop=True)
                        S.inc(sn("dd"), i)
                    if c >= 1:
                        t.wait_ge(S.sem(sn("rh")), nrh)
                    i = t.matmul(ps[4][:, 0:256], nones, br[rc], start=True, stop=True)
                    S.inc(sn("bb"), i)

                @P("scalar")
                def _(sc, n=S.val(sn("kt")), ndel=S.val(sn("del"))):
                    sc.wait_ge(S.sem(sn("kt")), n)
                    if ndel:
                        sc.wait_ge(S.sem(sn("del")), ndel)
                    for half in range(2):
                        i = sc.copy(KT[:, half * 512:(half + 1) * 512],
                                    ps[2 + half][0:CH, 0:512])
                        S.inc(sn("ke"), i)

                @P("vector")
                def _(v, c=c, rc=rc, nld=nld, ngk=S.val(sn("gk")),
                      nrp=S.val(sn("rp")), nmx=S.val(sn("mx")),
                      noi=S.val(sn("oi")), ndd=S.val(sn("dd")),
                      nbb=S.val(sn("bb")), nmz=nmz):
                    v.wait_ge(S.sem(sn("gk")), ngk)
                    v.wait_ge(S.sem(sn("mz")), nmz)
                    if c >= 1:
                        v.wait_ge(S.sem(sn("rp")), nrp)
                        v.wait_ge(S.sem(sn("mx")), nmx)
                        v.wait_ge(S.sem(sn("oi")), noi)
                    mUr = bass.AP(mU.tensor, mU.offset, [list(mU.ap[0]), [0, H], [1, CH]])
                    mUIr = bass.AP(mUI.tensor, mUI.offset,
                                   [list(mUI.ap[0]), [0, H], [1, CH]])
                    v.tensor_tensor(Gs.rearrange("p (h t) -> p h t", h=H),
                                    ps[1][0:CH, 0:256].rearrange("p (h t) -> p h t", h=H),
                                    mUr, ALU.mult)
                    v.tensor_tensor(Kqs.rearrange("p (h t) -> p h t", h=H),
                                    ps[1][0:CH, 256:512].rearrange("p (h t) -> p h t", h=H),
                                    mUIr, ALU.mult)
                    S.vb(v)
                    S.inc(sn("gm"), v.engine_nop())
                    v.wait_ge(S.sem(sn("ld")), nld)
                    t0 = lambda b: bass.AP(b.tensor, b.offset,
                                           [list(b.ap[0]), [32, H], [1, 1]])
                    v.tensor_copy(sd0, aa[rc])
                    S.vb(v)
                    v.memset(t0(sd0), 0.0)
                    v.memset(sd1, 0.0)
                    S.vb(v)
                    v.tensor_copy(t0(sd1), t0(aa[rc]))
                    S.vb(v)
                    v.tensor_tensor_scan(Ab, sd0, sd1, 1.0, ALU.mult, ALU.add)
                    S.vb(v)
                    sh = lambda b, o, n: bass.AP(b.tensor, b.offset + o,
                                                 [list(b.ap[0]), [32, H], [1, n]])
                    v.tensor_copy(sh(tmp, 1, 31), sh(Ab, 0, 31))
                    v.memset(t0(tmp), 1.0)
                    S.vb(v)
                    v.reciprocal(Aip, tmp)
                    S.vb(v)
                    v.tensor_mul(sd1, tmp, Aip)
                    S.vb(v)
                    v.tensor_scalar(sd1, sd1, -1.0, 2.0, ALU.mult, ALU.add)
                    S.vb(v)
                    v.tensor_mul(Aip, Aip, sd1)
                    S.vb(v)
                    v.tensor_mul(Acc, va[rc], Aip)
                    S.vb(v)
                    v.wait_ge(S.sem(sn("dd")), ndd)
                    d0ap = bass.AP(ps[0].tensor, ps[0].offset,
                                   [list(ps[0].ap[0]), [64, H], [1, 32]])
                    v.tensor_tensor(Acc.rearrange("p (h t) -> p h t", h=H),
                                    Acc.rearrange("p (h t) -> p h t", h=H),
                                    d0ap, ALU.add)
                    S.vb(v)
                    v.reciprocal(sd0, aa[rc])
                    S.vb(v)
                    v.tensor_mul(sd1, aa[rc], sd0)
                    S.vb(v)
                    v.tensor_scalar(sd1, sd1, -1.0, 2.0, ALU.mult, ALU.add)
                    S.vb(v)
                    v.tensor_mul(sd0, sd0, sd1)
                    S.vb(v)
                    v.wait_ge(S.sem(sn("bb")), nbb)
                    v.tensor_mul(rho, sd0, ps[4][:, 0:256])
                    S.vb(v)
                    S.inc(sn("ac"), v.engine_nop())
                    S.inc(sn("rh"), v.engine_nop())

                nac = S.val(sn("ac"))

                @P("tensor")
                def _(t, nfr0=S.val(sn("fr")), ngm=S.val(sn("gm")),
                      nrh=S.val(sn("rh"))):
                    t.wait_ge(S.sem(sn("gm")), ngm)
                    t.wait_ge(S.sem(sn("rh")), nrh)
                    for st in range(CH - 1):
                        w = CH - 1 - st
                        if nfr0 + st > 0:
                            t.wait_ge(S.sem(sn("fr")), nfr0 + st)
                        sel = bass.AP(k_id32.tensor, k_id32.offset + st,
                                      [list(k_id32.ap[0]), [0, 128]])
                        mv = bass.AP(Gs.tensor, Gs.offset + st + 1,
                                     [list(Gs.ap[0]), [32, H], [1, w]])
                        i = t.matmul(ps[5][:, 0:H * w], sel, mv, start=True, stop=True)
                        S.inc(sn("rp"), i)

                @P("vector")
                def _(v, nrp0=S.val(sn("rp")) - (CH - 1), nac=nac):
                    v.wait_ge(S.sem(sn("ac")), nac)
                    for st in range(CH):
                        colW = bass.AP(Wb.tensor, Wb.offset + st,
                                       [list(Wb.ap[0]), [32, H], [1, 1]])
                        colR = bass.AP(rho.tensor, rho.offset + st,
                                       [list(rho.ap[0]), [32, H], [1, 1]])
                        colA = bass.AP(Acc.tensor, Acc.offset + st,
                                       [list(Acc.ap[0]), [32, H], [1, 1]])
                        v.tensor_tensor(colW, colR, colA, ALU.mult)
                        S.vb(v)
                        if st == CH - 1:
                            break
                        w = CH - 1 - st
                        v.wait_ge(S.sem(sn("rp")), nrp0 + st + 1)
                        wrep = bass.AP(Wb.tensor, Wb.offset + st,
                                       [list(Wb.ap[0]), [32, H], [0, w]])
                        v.tensor_tensor(tmp[:, 0:H * w], ps[5][:, 0:H * w], wrep,
                                        ALU.mult)
                        S.vb(v)
                        i = v.engine_nop()
                        S.inc(sn("fr"), i)
                        accw = bass.AP(Acc.tensor, Acc.offset + st + 1,
                                       [list(Acc.ap[0]), [32, H], [1, w]])
                        tmpw = bass.AP(tmp.tensor, tmp.offset,
                                       [list(tmp.ap[0]), [w, H], [1, w]])
                        v.tensor_tensor(accw, accw, tmpw, ALU.add)
                        S.vb(v)
                    i = v.engine_nop()
                    S.inc(sn("fs"), i)

                @P("tensor")
                def _(t, nfs=S.val(sn("fs")), nke=S.val(sn("ke"))):
                    t.wait_ge(S.sem(sn("fs")), nfs)
                    t.wait_ge(S.sem(sn("ke")), nke)
                    for h in range(H):
                        i = t.transpose(ps[2 + h // 4][0:CH,
                                                       (h % 4) * 128:(h % 4) * 128 + 128],
                                        Wb[:, h * 32:(h + 1) * 32], k_id128)
                        S.inc(sn("wt"), i)

                @P("scalar")
                def _(sc, n=S.val(sn("wt")), ndel=S.val(sn("del")),
                      noi=S.val(sn("oi"))):
                    sc.wait_ge(S.sem(sn("wt")), n)
                    if ndel:
                        sc.wait_ge(S.sem(sn("del")), ndel)
                        sc.wait_ge(S.sem(sn("oi")), noi)
                    for half in range(2):
                        i = sc.copy(WT[:, half * 512:(half + 1) * 512],
                                    ps[2 + half][0:CH, 0:512])
                        S.inc(sn("we"), i)

                @P("tensor")
                def _(t, c=c, rc=rc, n=S.val(sn("we")), nke=S.val(sn("ke")),
                      nac=nac):
                    t.wait_ge(S.sem(sn("we")), n)
                    t.wait_ge(S.sem(sn("ac")), nac)
                    stat2 = Sin if c == 0 else Sscr
                    if passid == "B":
                        for h in range(H):
                            t.matmul(ps[0][:, h * 64 + 32:h * 64 + 64],
                                     stat2[:, h * 128:(h + 1) * 128],
                                     qa[rc][:, h * 32:(h + 1) * 32],
                                     start=True, stop=False)
                            i = t.matmul(ps[0][:, h * 64 + 32:h * 64 + 64],
                                         WT[:, h * 128:(h + 1) * 128],
                                         Kqs[0:CH, h * 32:(h + 1) * 32],
                                         start=False, stop=True)
                            S.inc(sn("oi"), i)
                    t.wait_ge(S.sem(sn("ke")), nke)
                    for h in range(H):
                        i = t.matmul(ps[2 + h // 4][:, (h % 4) * 128:(h % 4) * 128 + 128],
                                     WT[:, h * 128:(h + 1) * 128],
                                     KT[0:CH, h * 128:(h + 1) * 128],
                                     start=True, stop=True)
                        S.inc(sn("del"), i)

                @P("vector")
                def _(v, c=c, noi=S.val(sn("oi")), ndel=S.val(sn("del")),
                      nos=S.val(sn("os")), nsvi=nsvi):
                    if passid == "B":
                        v.wait_ge(S.sem(sn("oi")), noi)
                        if nos:
                            v.wait_ge(S.sem(sn("os")), nos)
                        oap = bass.AP(ps[0].tensor, ps[0].offset + 32,
                                      [list(ps[0].ap[0]), [64, H], [1, 32]])
                        v.tensor_tensor(ovb.rearrange("p (h t) -> p h t", h=H),
                                        Ab.rearrange("p (h t) -> p h t", h=H),
                                        oap, ALU.mult)
                        S.vb(v)
                        i = v.engine_nop()
                        S.inc(sn("ob"), i)
                    v.wait_ge(S.sem(sn("del")), ndel)
                    if c == 0:
                        v.wait_ge(S.sem(sn("se")), nsvi)
                    dels = [bass.AP(ps[2 + half].tensor, ps[2 + half].offset,
                                    [list(ps[2 + half].ap[0]), [1, 512]])
                            for half in range(2)]
                    for half in range(2):
                        src0 = (Sin if c == 0 else Sv)[:, half * 512:(half + 1) * 512]
                        v.tensor_tensor(Sv[:, half * 512:(half + 1) * 512],
                                        src0, dels[half], ALU.add)
                        S.vb(v)
                    acrep = bass.AP(Ab.tensor, Ab.offset + 31,
                                   [list(Ab.ap[0]), [32, H], [0, 128]])
                    v.tensor_tensor(Sv.rearrange("p (h j) -> p h j", h=H),
                                    Sv.rearrange("p (h j) -> p h j", h=H),
                                    acrep, ALU.mult)
                    S.vb(v)
                    S.inc(sn("sv"), v.engine_nop())

                if c < NCH - 1:
                    @P("tensor")
                    def _(t, n=S.val(sn("sv"))):
                        t.wait_ge(S.sem(sn("sv")), n)
                        for h in range(H):
                            i = t.transpose(
                                ps[2 + h // 4][:, (h % 4) * 128:(h % 4) * 128 + 128],
                                Sv[:, h * 128:(h + 1) * 128], k_id128)
                            S.inc(sn("svt"), i)

                    @P("scalar")
                    def _(sc, n=S.val(sn("svt"))):
                        sc.wait_ge(S.sem(sn("svt")), n)
                        for half in range(2):
                            i = sc.copy(Sscr[:, half * 512:(half + 1) * 512],
                                        ps[2 + half][:, 0:512])
                            S.inc(sn("se"), i)

                if scan_dbg and c == 0:
                    @P("sync")
                    def _(e, nsv=S.val(sn("sv")), ngm=S.val(sn("gm")),
                          nfs=S.val(sn("fs"))):
                        e.wait_ge(S.sem(sn("sv")), nsv)
                        e.wait_ge(S.sem(sn("gm")), ngm)
                        e.wait_ge(S.sem(sn("fs")), nfs)
                        for sl, buf in enumerate((Ab, Aip, rho, Wb, Acc, ovb,
                                                  ka[0], aa[0], va[0], qa[0])):
                            i = e.dma_start(out=d_dbg[sl][:, 0:256], in_=buf)
                            S.inc(sn("dbg"), i, 16)
                        i = e.dma_start(out=d_dbg[10][0:CH, 0:256], in_=Gs)
                        S.inc(sn("dbg"), i, 16)
                        i = e.dma_start(out=d_dbg[11][0:CH, 0:256], in_=Kqs)
                        S.inc(sn("dbg"), i, 16)
                        i = e.dma_start(out=d_dbg[12], in_=Sv)
                        S.inc(sn("dbg"), i, 16)
                        e.wait_ge(S.sem(sn("dbg")), S.val(sn("dbg")))

                if passid == "B":
                    @P("sync")
                    def _(e, c=c, n=S.val(sn("ob")), nos0=S.val(sn("os"))):
                        e.wait_ge(S.sem(sn("ob")), n)
                        if nos0:
                            e.wait_ge(S.sem(sn("os")), nos0)
                        i = e.dma_start(
                            out=d_o[:, :, 32 * c:32 * c + 32].rearrange(
                                "h p t -> p h t"),
                            in_=ovb.rearrange("p (h t) -> p h t", h=H))
                        S.inc(sn("os"), i, 16)

            gate0 = [(sn("dd"), S.val(sn("dd"))), (sn("kt"), S.val(sn("kt"))),
                     (sn("bb"), S.val(sn("bb"))), (sn("ac"), S.val(sn("ac")))]
            loads(0, gate0)
            loads(1, gate0)
            for c in range(NCH):
                chunk(c)
                if c + 2 < NCH:
                    loads(c + 2, [(sn("dd"), S.val(sn("dd"))),
                                  (sn("kt"), S.val(sn("kt"))),
                                  (sn("bb"), S.val(sn("bb"))),
                                  (sn("oi"), S.val(sn("oi"))),
                                  (sn("ac"), S.val(sn("ac")))])

        if not single:
            one_pass("A")

            @P("gpsimd")
            def _(g, nsv=S.val(sn("sv")), nos=S.val(sn("os"))):
                g.wait_ge(S.sem(sn("sv")), nsv)
                g.wait_ge(S.sem(sn("os")), nos)
                i = g.dma_start(out=d_cc1in, in_=Sv)
                S.inc(sn("cc"), i, 16)
                g.wait_ge(S.sem(sn("cc")), S.val(sn("cc")))
                i = g.collective_compute(
                    "AllGather", ALU.bypass,
                    replica_groups=[[0, 1], [2, 3], [4, 5], [6, 7]],
                    ins=[d_cc1in.opt()],
                    outs=[d_cc1out.opt()])
                S.inc(sn("ccg"), i)
                g.wait_ge(S.sem(sn("ccg")), S.val(sn("ccg")))
                i = g.dma_start(out=Sin, in_=d_cc1out[:, 0:H * 128])
                S.inc(sn("rd"), i, 16)
                g.wait_ge(S.sem(sn("rd")), S.val(sn("rd")))
        one_pass("B")

    def emit_p3(pscope3):
        sm = lambda n: ("aa_" + n) if n not in ('ote', 'ap', 'gt', 'ldr', 'ldt', 'tt', 'zz', 'zs', 'cq', 'xs', 'wl') else (sn(n))
        sn = lambda n: "p3_" + n
        NT3 = TOWN // FT
        wo = S.sb(sn("wo"), [128, KD * DM], scope=pscope3)
        ofm = S.sb(sn("ofm"), [128, H * FT], scope=pscope3)
        gfm = S.sb(sn("gfm"), [128, H * FT], scope=pscope3)
        zfm = S.sb(sn("zfm"), [128, H * FT], scope=pscope3)
        otok = S.sb(sn("otok"), [128, NSUB * H * 128], scope=pscope3)
        stats = S.sb(sn("stats"), [128, 64], scope=pscope3)
        srow = S.sb(sn("srow"), [1, 512], scope=pscope3)
        rbuf = S.sb(sn("rbuf"), [128, 2 * FT], scope=pscope3)
        xbuf = S.sb(sn("xbuf"), [128, 2 * FT], scope=pscope3)
        xtokf = S.sb(sn("xtokf"), [128, NSUB * DM], scope=pscope3)
        zb = S.sb(sn("zb"), [128, 2048], scope=pscope3)
        lngn = S.sb(sn("lngn"), [128, 1], scope=pscope3)
        nbar = S.barrier("p3")
        for _eng in ("sync", "tensor", "scalar", "gpsimd", "vector"):
            @P(_eng)
            def _(e, nbar=nbar):
                e.wait_ge(S.sem("pb"), nbar)

        @P("sync")
        def _(e):
            i = e.dma_start(out=wo.rearrange("p (k o) -> p k o", k=KD),
                            in_=wpi["wo"].rearrange("(k p) o -> p k o", p=128))
            S.inc(sm("wl"), i, 16)
        nwl = S.val(sm("wl"))

        @P("vector")
        def _(v):
            v.memset(zb, 0.0)
            v.tensor_scalar_mul(lngn, k_lng[:, 0:1], -1.0)
            S.vb(v)
            S.inc(sm("zz"), v.engine_nop())
        nzz = S.val(sm("zz"))

        @P("gpsimd")
        def _(g, nzz=nzz):
            g.wait_ge(S.sem(sm("zz")), nzz)
            zf = zb.rearrange("p (k t) -> p k t", k=KD)
            i = g.dma_start(out=d_xattn[:, :, 0:8].rearrange("k p t -> p k t"),
                            in_=zf[:, :, 0:8])
            S.inc(sm("zs"), i, 16)
            for q0 in range(OWN0 + TOWN, SLICE, 124):
                i = g.dma_start(
                    out=d_xattn[:, :, q0:q0 + 124].rearrange("k p t -> p k t"),
                    in_=zf[:, :, 0:124])
                S.inc(sm("zs"), i, 16)
            i = g.dma_start(out=d_xat[0:8, :], in_=zb[0:8, 0:DM])
            S.inc(sm("zs"), i, 16)
            i = g.dma_start(out=d_xat[OWN0 + TOWN:OWN0 + TOWN + 128, :],
                            in_=zb[:, 0:DM])
            S.inc(sm("zs"), i, 16)
            i = g.dma_start(out=d_xat[OWN0 + TOWN + 128:SLICE, :],
                            in_=zb[0:120, 0:DM])
            S.inc(sm("zs"), i, 16)

        def tile(j):
            c0 = j * FT
            cs = OWN0 + c0

            @P("sync")
            def _(e, j=j, c0=c0, cs=cs, nld0=S.val(sm("ld")),
                  ntq=S.val(sm("tq")), ngt=S.val(sm("gt"))):
                if j >= 1:
                    e.wait_ge(S.sem(sm("ld")), nld0)
                    e.wait_ge(S.sem(sm("tq")), ntq)
                    e.wait_ge(S.sem(sm("gt")), ngt)
                i = e.dma_start(out=ofm.rearrange("p (h t) -> p h t", h=H),
                                in_=d_o[:, :, c0:c0 + FT].rearrange("h p t -> p h t"))
                S.inc(sm("ld"), i, 16)
                i = e.dma_start(out=gfm.rearrange("p (h t) -> p h t", h=H),
                                in_=d_g[:, :, cs:cs + FT].rearrange("h p t -> p h t"))
                S.inc(sm("ld"), i, 16)
            nld = S.val(sm("ld"))

            baseT = S.val(sm("tq"))
            baseE = S.val(sm("ote"))

            @P("tensor")
            def _(t, nld=nld, baseE=baseE, nsq=S.val(sm("sq")),
                  nxte=S.val(sm("xte"))):
                t.wait_ge(S.sem(sm("ld")), nld)
                if nsq:
                    t.wait_ge(S.sem(sm("sq")), nsq)
                if nxte:
                    t.wait_ge(S.sem(sm("xte")), nxte)
                ov = ofm.rearrange("p (h t) -> p h t", h=H)
                k = 0
                for s_ in range(NSUB):
                    for h in range(H):
                        if baseE + k - 1 > 0:
                            t.wait_ge(S.sem(sm("ote")), baseE + k - 1)
                        i = t.transpose(ps[4 + (k % 2)][:, 0:128],
                                        ov[:, h, s_ * 128:(s_ + 1) * 128], k_id128)
                        S.inc(sm("tq"), i)
                        k += 1

            @P("scalar")
            def _(sc, baseT=baseT, j=j):
                ot = otok.rearrange("p (s h f) -> p s h f", s=NSUB, h=H)
                k = 0
                for s_ in range(NSUB):
                    for h in range(H):
                        sc.wait_ge(S.sem(sm("tq")), baseT + k + 1)
                        if k == 0 and j >= 1:
                            sc.wait_ge(S.sem(sm("st")), S.val(sm("st")))
                        i = sc.copy(ot[:, s_, h, :], ps[4 + (k % 2)][:, 0:128])
                        S.inc(sm("ote"), i)
                        k += 1
            note = S.val(sm("ote"))

            @P("scalar")
            def _(sc, n=note):
                sc.wait_ge(S.sem(sm("ote")), n)
                ot = otok.rearrange("p (s h f) -> p s h f", s=NSUB, h=H)
                k = 0
                for s_ in range(NSUB):
                    for h in range(H):
                        if k and k % 4 == 0:
                            sc.wait_ge(S.sem(sm("sq")), S.val(sm("sq")))
                        i = sc.activation(ps[6][:, (k % 4) * 128:(k % 4) * 128 + 128],
                                          ot[:, s_, h, :], ACT.Square,
                                          accum_out=stats[:, k:k + 1])
                        S.inc(sm("sq"), i)
                        k += 1
            nsq = S.val(sm("sq"))

            @P("vector")
            def _(v, n=note, nsq=nsq):
                v.wait_ge(S.sem(sm("ote")), n)
                ot = otok.rearrange("p (s h f) -> p s h f", s=NSUB, h=H)
                k = 0
                for s_ in range(NSUB):
                    for h in range(H):
                        v.reduce_sum(stats[:, 16 + k:17 + k], ot[:, s_, h, :], AX)
                        k += 1
                v.wait_ge(S.sem(sm("sq")), nsq)
                S.vb(v)
                v.tensor_scalar_mul(stats[:, 16:32], stats[:, 16:32], 1.0 / 128)
                v.tensor_scalar_mul(stats[:, 0:16], stats[:, 0:16], 1.0 / 128)
                S.vb(v)
                v.tensor_mul(stats[:, 32:48], stats[:, 16:32], stats[:, 16:32])
                S.vb(v)
                v.tensor_sub(stats[:, 0:16], stats[:, 0:16], stats[:, 32:48])
                S.vb(v)
                v.tensor_scalar_add(stats[:, 0:16], stats[:, 0:16], EPS)
                S.vb(v)
                v.reciprocal(stats[:, 32:48], stats[:, 0:16])
                S.vb(v)
                S.inc(sm("st"), v.engine_nop())

            @P("scalar")
            def _(sc, n=S.val(sm("st"))):
                sc.wait_ge(S.sem(sm("st")), n)
                i = sc.activation(stats[:, 48:64], stats[:, 32:48], ACT.Sqrt)
                S.inc(sm("sq"), i)

            @P("vector")
            def _(v, n=S.val(sm("sq"))):
                v.wait_ge(S.sem(sm("sq")), n)
                v.tensor_mul(stats[:, 32:48], stats[:, 48:64], stats[:, 48:64])
                S.vb(v)
                v.tensor_mul(stats[:, 32:48], stats[:, 32:48], stats[:, 0:16])
                S.vb(v)
                v.tensor_scalar(stats[:, 32:48], stats[:, 32:48], -1.0, 3.0,
                                ALU.mult, ALU.add)
                S.vb(v)
                v.tensor_mul(stats[:, 32:48], stats[:, 32:48], stats[:, 48:64])
                S.vb(v)
                v.tensor_scalar_mul(stats[:, 48:64], stats[:, 32:48], 0.5)
                S.vb(v)
                v.tensor_mul(stats[:, 32:48], stats[:, 16:32], stats[:, 48:64])
                S.vb(v)
                S.inc(sm("ct"), v.engine_nop())
            nct = S.val(sm("ct"))

            ntt0 = S.val(sm("tt"))
            for q in range(4):
                for rnd, col, bank in ((0, 48, 2), (1, 32, 3)):
                    @P("tensor")
                    def _(t, q=q, col=col, nct=nct, nrw=S.val(sm("rw"))):
                        t.wait_ge(S.sem(sm("ct")), nct)
                        if nrw:
                            t.wait_ge(S.sem(sm("rw")), nrw)
                        k = 0
                        for h in (2 * q, 2 * q + 1):
                            for s_ in range(NSUB):
                                i = t.transpose(
                                    ps[7][:1, k * 128:(k + 1) * 128],
                                    stats[:, col + s_ * 8 + h:
                                          col + s_ * 8 + h + 1],
                                    k_id128)
                                S.inc(sm("cq"), i)
                                k += 1

                    @P("scalar")
                    def _(sc, n=S.val(sm("cq")), ni1=S.val(sm("i1"))):
                        sc.wait_ge(S.sem(sm("cq")), n)
                        if ni1:
                            sc.wait_ge(S.sem(sm("i1")), ni1)
                        i = sc.copy(srow, ps[7][:1, 0:512])
                        S.inc(sm("rw"), i)

                    @P("tensor")
                    def _(t, bank=bank, n=S.val(sm("rw")), nap=S.val(sm("ap"))):
                        t.wait_ge(S.sem(sm("rw")), n)
                        if nap:
                            t.wait_ge(S.sem(sm("ap")), nap)
                        i = t.matmul(ps[bank][:, 0:512], k_ones, srow,
                                     start=True, stop=True)
                        S.inc(sm("i1"), i)

                @P("vector")
                def _(v, q=q, n=S.val(sm("i1")), nld=nld, nzz=nzz, ntt0=ntt0):
                    v.wait_ge(S.sem(sm("i1")), n)
                    v.wait_ge(S.sem(sm("ld")), nld)
                    v.wait_ge(S.sem(sm("zz")), nzz)
                    if q == 0 and ntt0:
                        v.wait_ge(S.sem(sm("tt")), ntt0)
                    sl = slice(q * 512, (q + 1) * 512)
                    v.tensor_tensor(zfm[:, sl], ofm[:, sl], ps[2][:, 0:512],
                                    ALU.mult)
                    S.vb(v)
                    v.tensor_tensor(zfm[:, sl], zfm[:, sl], ps[3][:, 0:512],
                                    ALU.subtract)
                    S.vb(v)
                    v.scalar_tensor_tensor(zfm[:, sl], zfm[:, sl], lngn,
                                           bfree(k_lnb[:, 0:1], 512),
                                           ALU.mult, ALU.add)
                    S.vb(v)
                    v.tensor_mul(zfm[:, sl], zfm[:, sl], gfm[:, sl])
                    S.vb(v)
                    i = v.engine_nop()
                    S.inc(sm("ap"), i)
                    if q == 3:
                        S.inc(sm("gt"), v.engine_nop())
            nap = S.val(sm("ap"))

            basem = S.val(sm("mm"))

            @P("tensor")
            def _(t, j=j, nap=nap, basem=basem, nwl=nwl, nrs=S.val(sm("res"))):
                t.wait_ge(S.sem(sm("ap")), nap)
                if j == 0:
                    t.wait_ge(S.sem(sm("wl")), nwl)
                wv = wo.rearrange("p (k o) -> p k o", k=KD)
                zv = zfm.rearrange("p (h t) -> p h t", h=H)
                for oc in range(KD):
                    if nrs + oc - 1 > 0:
                        t.wait_ge(S.sem(sm("res")), nrs + oc - 1)
                    i = None
                    for hc in range(KD):
                        i = t.matmul(ps[oc % 2][:, 0:FT],
                                     wv[:, hc, oc * 128:(oc + 1) * 128],
                                     zv[:, hc, :],
                                     start=(hc == 0), stop=(hc == KD - 1))
                    S.inc(sm("mm"), i)

            @P("sync")
            def _(e, j=j, cs=cs, nrs0=S.val(sm("res"))):
                for oc in range(KD):
                    if nrs0 + oc - 1 > 0:
                        e.wait_ge(S.sem(sm("res")), nrs0 + oc - 1)
                    i = e.dma_start(out=rbuf[:, (oc % 2) * FT:(oc % 2 + 1) * FT],
                                    in_=d_xmid[oc, :, cs:cs + FT])
                    S.inc(sm("ldr"), i, 16)

            @P("vector")
            def _(v, j=j, cs=cs, basem=basem, nldr=S.val(sm("ldr")) - 16 * KD,
                  nsp=S.val(sm("sp"))):
                for oc in range(KD):
                    v.wait_ge(S.sem(sm("mm")), basem + oc + 1)
                    v.wait_ge(S.sem(sm("ldr")), nldr + 16 * (oc + 1))
                    if nsp + 16 * (oc - 1) > 0:
                        v.wait_ge(S.sem(sm("sp")), nsp + 16 * (oc - 1))
                    v.tensor_add(xbuf[:, (oc % 2) * FT:(oc % 2 + 1) * FT],
                                 ps[oc % 2][:, 0:FT],
                                 rbuf[:, (oc % 2) * FT:(oc % 2 + 1) * FT])
                    S.vb(v)
                    i = v.engine_nop()
                    S.inc(sm("res"), i)
            nres = S.val(sm("res"))

            @P("gpsimd")
            def _(g, j=j, cs=cs, nres0=nres - KD, nzs=S.val(sm("zs"))):
                if j == 0:
                    g.wait_ge(S.sem(sm("zs")), nzs)
                for oc in range(KD):
                    g.wait_ge(S.sem(sm("res")), nres0 + oc + 1)
                    i = g.dma_start(out=d_xattn[oc, :, cs:cs + FT],
                                    in_=xbuf[:, (oc % 2) * FT:(oc % 2 + 1) * FT])
                    S.inc(sm("sp"), i, 16)

            @P("sync")
            def _(e, j=j, cs=cs, nsp=S.val(sm("sp")), ntt=S.val(sm("tt"))):
                e.wait_ge(S.sem(sm("sp")), nsp)
                if ntt:
                    e.wait_ge(S.sem(sm("tt")), ntt)
                i = e.dma_start(
                    out=zfm.rearrange("p (k t) -> p k t", k=KD),
                    in_=d_xattn[:, :, cs:cs + FT].rearrange("k p t -> p k t"))
                S.inc(sm("ldt"), i, 16)

            @P("tensor")
            def _(t, n=S.val(sm("ldt")), baseE2=S.val(sm("xte"))):
                t.wait_ge(S.sem(sm("ldt")), n)
                xv = zfm.rearrange("p (k t) -> p k t", k=KD)
                k = 0
                for s_ in range(NSUB):
                    for kc in range(KD):
                        if baseE2 + k - 1 > 0:
                            t.wait_ge(S.sem(sm("xte")), baseE2 + k - 1)
                        i = t.transpose(ps[4 + (k % 2)][:, 0:128],
                                        xv[:, kc, s_ * 128:(s_ + 1) * 128],
                                        k_id128)
                        S.inc(sm("tt"), i)
                        k += 1

            @P("scalar")
            def _(sc, baseT2=S.val(sm("tt")) - 16, nxs=S.val(sm("xs"))):
                xs2 = xtokf.rearrange("p (s f) -> p s f", s=NSUB)
                k = 0
                for s_ in range(NSUB):
                    for kc in range(KD):
                        sc.wait_ge(S.sem(sm("tt")), baseT2 + k + 1)
                        if k == 0 and nxs:
                            sc.wait_ge(S.sem(sm("xs")), nxs)
                        i = sc.copy(xs2[:, s_, kc * 128:(kc + 1) * 128],
                                    ps[4 + (k % 2)][:, 0:128])
                        S.inc(sm("xte"), i)
                        k += 1

            @P("gpsimd")
            def _(g, j=j, cs=cs, n=S.val(sm("xte"))):
                g.wait_ge(S.sem(sm("xte")), n)
                i = g.dma_start(
                    out=d_xat[cs:cs + FT, :].rearrange("(s p) f -> p s f", p=128),
                    in_=xtokf.rearrange("p (s f) -> p s f", s=NSUB))
                S.inc(sm("xs"), i, 16)

        for j in range(NT3):
            tile(j)

    def emit_cc2(semBD0, semBD1):
        sn = lambda n: "c2_" + n
        bnd = S.sb(sn("bnd"), [128, 2 * KD])

        @P("gpsimd")
        def _(g, nsp=S.val("aa_sp"), nxs=S.val("p3_xs")):
            g.wait_ge(S.sem("aa_sp"), nsp)
            g.wait_ge(S.sem("p3_xs"), nxs)
            i = g.dma_start(out=bnd[:, 0:KD],
                            in_=d_xattn[:, :, OWN0].rearrange("k p -> p k"),
                            allow_slow_non_contiguous=True)
            S.inc(sn("ld"), i, 16)
            i = g.dma_start(out=bnd[:, KD:2 * KD],
                            in_=d_xattn[:, :, OWN0 + TOWN - 1].rearrange("k p -> p k"),
                            allow_slow_non_contiguous=True)
            S.inc(sn("ld"), i, 16)
            g.wait_ge(S.sem(sn("ld")), S.val(sn("ld")))
            i = g.dma_start(out=d_cc2in, in_=bnd)
            S.inc(sn("cin"), i, 16)
            g.wait_ge(S.sem(sn("cin")), S.val(sn("cin")))
            if not single:
                i = g.collective_compute(
                    "AllGather", ALU.bypass,
                    replica_groups=[[0, 1], [2, 3], [4, 5], [6, 7]],
                    ins=[d_cc2in.opt()],
                    outs=[d_cc2out.opt()])
                S.inc(sn("ccg"), i)
                g.wait_ge(S.sem(sn("ccg")), S.val(sn("ccg")))
            else:
                i = g.dma_start(out=d_cc2out[:, 0:2 * KD], in_=d_cc2in)
                S.inc(sn("cp"), i, 16)
                i = g.dma_start(out=d_cc2out[:, 2 * KD:4 * KD], in_=d_cc2in)
                S.inc(sn("cp"), i, 16)
                g.wait_ge(S.sem(sn("cp")), S.val(sn("cp")))
            i = g.dma_start(
                out=d_xat[7:8, :].rearrange("r (k p) -> (r k) p", k=KD),
                in_=d_cc2out[:, KD:2 * KD].rearrange("p k -> k p"),
                allow_slow_non_contiguous=True)
            S.inc(sn("pa"), i, 16)
            i = g.dma_start(
                out=d_xat[OWN0 + TOWN:OWN0 + TOWN + 1, :].rearrange(
                    "r (k p) -> (r k) p", k=KD),
                in_=d_cc2out[:, 2 * KD:3 * KD].rearrange("p k -> k p"),
                allow_slow_non_contiguous=True)
            S.inc(sn("pb"), i, 16)
            g.wait_ge(S.sem(sn("pa")), S.val(sn("pa")))
            g.wait_ge(S.sem(sn("pb")), S.val(sn("pb")))
            g.engine_nop().then_inc(semBD0, 1)
            g.engine_nop().then_inc(semBD1, 1)

    def emit_debug_copy(src_dram, waitsem, waitval):
        @P("sync")
        def _(e):
            e.wait_ge(S.sem(waitsem), waitval)
            i = e.dma_start(out=out,
                            in_=src_dram[:, :, OWN0:OWN0 + TOWN])
            S.inc("z_out", i, 16)

        @P("gpsimd")
        def _(g):
            g.wait_ge(S.sem("z_out"), S.val("z_out"))

    wscope = ExitStack()
    if "pre" in run_phases:
        w1q, w2q = emit_pre(wscope)
    if "f1" in run_phases:
        with ExitStack() as fscope:
            emit_fphase("f1", x0t, x0f, d_xmid, w1q, w2q, fscope)
    wscope.close()
    if "a1a" in run_phases:
        with ExitStack() as ascope:
            emit_a1a(ascope)
    if "a1b" in run_phases:
        with ExitStack() as bscope:
            emit_a1b(bscope)
    if "scan" in run_phases:
        with ExitStack() as cscope:
            emit_scan(cscope)
    if "p3" in run_phases:
        with ExitStack() as pscope3:
            emit_p3(pscope3)
    if "f2" in run_phases:
        semBD0 = S.ctx.enter_context(nc.semaphore("bd0"))
        semBD1 = S.ctx.enter_context(nc.semaphore("bd1"))
        nbarw = S.barrier("f2w")
        for _eng in ("sync", "tensor", "scalar", "gpsimd", "vector"):
            @P(_eng)
            def _(e, nbarw=nbarw):
                e.wait_ge(S.sem("pb"), nbarw)
        emit_cc2(semBD0, semBD1)
        w1q2 = S.sb("f2w1q", [128, KD * DH], BF16)
        w2q2 = S.sb("f2w2q", [128, HD * DM], BF16)

        @P("sync")
        def _(e, npo=S.val("p_out")):
            e.wait_ge(S.sem("p_out"), npo)
            i = e.dma_start(out=w1q2, in_=d_w1q)
            S.inc("f2wl", i, 16)
            i = e.dma_start(out=w2q2, in_=d_w2q)
            S.inc("f2wl", i, 16)
        with ExitStack() as fscope2:
                emit_fphase("f2", d_xat, d_xattn, d_xfin, w1q2, w2q2, fscope2,
                        wl=("f2wl", 32),
                        patch=dict(bd0=semBD0, bd1=semBD1))
        if debug_out is None:
            emit_debug_copy(d_xfin, "f_sp", S.val("f_sp"))

    if debug_out == "xmid":
        emit_debug_copy(d_xmid, "f_sp", S.val("f_sp"))
    elif debug_out == "xn":
        emit_debug_copy(d_xn, "aa_xsp", S.val("aa_xsp"))
    elif debug_out in ("q", "k", "v"):
        emit_debug_copy({"q": d_q, "k": d_k, "v": d_v}[debug_out],
                        "aa_sp", S.val("aa_sp"))
    elif debug_out in ("a", "g"):
        emit_debug_copy({"a": d_a, "g": d_g}[debug_out],
                        "aa_sp", S.val("aa_sp"))
    elif debug_out == "o":
        @P("sync")
        def _(e, n=S.val("sc_os")):
            e.wait_ge(S.sem("sc_os"), n)
            i = e.dma_start(out=out[:, :, 0:32 * nch],
                            in_=d_o[:, :, 0:32 * nch])
            S.inc("z_out", i, 16)

        @P("gpsimd")
        def _(g):
            g.wait_ge(S.sem("z_out"), S.val("z_out"))
    elif debug_out == "beta":
        @P("sync")
        def _(e, n=S.val("aa_sp")):
            e.wait_ge(S.sem("aa_sp"), n)
            i = e.dma_start(out=out[0][0:H, :],
                            in_=d_beta[:, OWN0:OWN0 + TOWN])
            S.inc("z_out", i, 16)

        @P("gpsimd")
        def _(g):
            g.wait_ge(S.sem("z_out"), S.val("z_out"))

    S.ctx.close()
    return nc


def _prep_inputs(inputs):
    x = np.asarray(inputs["L0_attn_x"])
    maps = []
    ident = np.eye(CH, dtype=np.float32)
    tri = np.tril(np.ones((CH, CH), np.float32))
    id128 = np.eye(128, dtype=np.float32)
    w1 = np.ascontiguousarray(inputs["L0_ffn_w1"], np.float32)
    w2 = np.ascontiguousarray(inputs["L0_ffn_w2"], np.float32)
    cw = np.ascontiguousarray(inputs["L0_ffn_conv_w"], np.float32).reshape(DH, 3)
    cwl = np.ascontiguousarray(cw.reshape(HD, 128, 3).transpose(1, 0, 2))
    cbl = np.ascontiguousarray(np.asarray(inputs["L0_ffn_conv_b"], np.float32)
                               .reshape(HD, 128).T)
    wmats = {n: np.ascontiguousarray(inputs["L0_attn_W" + n[-1]], np.float32)
             for n in ("wq", "wk", "wv", "wa", "wg", "wo")}
    wb = np.ascontiguousarray(inputs["L0_attn_Wb"], np.float32)
    ba = np.ascontiguousarray(np.asarray(inputs["L0_attn_ba"], np.float32)
                              .reshape(KD, 128).T)
    bb = np.ascontiguousarray(inputs["L0_attn_bb"], np.float32).reshape(H, 1)
    conv = {}
    for nm, wkey, bkey in (("cqw", "L0_attn_cqw", "L0_attn_cqb"),
                           ("ckw", "L0_attn_ckw", "L0_attn_ckb"),
                           ("cvw", "L0_attn_cvw", "L0_attn_cvb")):
        conv[nm] = np.ascontiguousarray(np.asarray(inputs[wkey], np.float32)
                                        .reshape(KD, 128, 4).transpose(1, 0, 2))
        conv[nm[:2] + "b"] = np.ascontiguousarray(np.asarray(inputs[bkey], np.float32)
                                                  .reshape(KD, 128).T)
    lng = np.ascontiguousarray(inputs["L0_attn_ln_g"], np.float32).reshape(128, 1)
    lnb = np.ascontiguousarray(inputs["L0_attn_ln_b"], np.float32).reshape(128, 1)

    for c in range(NCORE):
        b = c // 2
        half = c % 2
        g0 = half * TOWN - HALO
        sl = np.zeros((SLICE, DM), np.float32)
        lo = max(0, g0)
        hi = min(T, g0 + SLICE)
        sl[lo - g0:hi - g0] = x[b, lo:hi]
        x0t_ = np.ascontiguousarray(sl)
        x0f_ = np.ascontiguousarray(sl.T.reshape(KD, 128, SLICE))
        valid = np.zeros(SLICE, np.float32)
        valid[lo - g0:hi - g0] = 1.0
        vtok_ = np.ascontiguousarray(valid.reshape(SLICE // 128, 128).T)
        vrow_ = valid.reshape(1, SLICE).copy()
        cfg_ = np.zeros((128, 8), np.float32)
        cfg_[:, 0] = half
        m = dict(x0f=x0f_, x0t=x0t_, vtok=vtok_, vrow=vrow_, cfg=cfg_, ident=ident, tri=tri,
                 id128=id128, w1=w1, w2=w2, cw=cwl, cb=cbl,
                 wb=wb, ba=ba, bb=bb,
                 cqw=conv["cqw"], cqb=conv["cqb"], ckw=conv["ckw"], ckb=conv["ckb"],
                 cvw=conv["cvw"], cvb=conv["cvb"], lng=lng, lnb=lnb)
        for n in ("wq", "wk", "wv", "wa", "wg", "wo"):
            m[n] = wmats[n]
        maps.append(m)
    return maps


_NC_CACHE = {}


def _get_nc(key, **kw):
    if key not in _NC_CACHE:
        _NC_CACHE[key] = build(**kw)
    return _NC_CACHE[key]


def kernel(**inputs):
    maps = _prep_inputs(inputs)
    nc = _get_nc("full")
    res = run_bass_kernel_spmd(nc, maps, core_ids=list(range(NCORE)))
    full = np.zeros((B, T, DM), np.float32)
    for c in range(NCORE):
        b, half = c // 2, c % 2
        o = np.asarray(res.results[c]["out"])
        full[b, half * TOWN:(half + 1) * TOWN] = o.reshape(DM, TOWN).T
    return full
